# version 02
# baseline (speedup 1.0000x reference)
import os
import numpy as np
import concourse.bass as bass
import concourse.mybir as mybir
from concourse.bass_utils import run_bass_kernel_spmd
from contextlib import ExitStack

F32 = mybir.dt.float32
BF16 = mybir.dt.bfloat16
I32 = mybir.dt.int32
AF = mybir.ActivationFunctionType
ALU = mybir.AluOpType

NCORE = 8
T = 1024
HX = 32
D = 2048
H = 16
DK = 128
NCH = 8
MAGIC = float(1.5 * 2 ** 23)
CE = 16384
DP = 1024
DC = 8192
EPS = 1e-5


class Rec:

    ENGS = ("sync", "gpsimd", "tensor", "vector", "scalar")
    SEMS = ("sync", "gpsimd", "tensor", "vector", "scalar", "gdma")

    def __init__(self, nc, es):
        self.nc = nc
        self.ops = {e: [] for e in self.ENGS}
        self.cnt = {s: 0 for s in self.SEMS}
        self.sem = {s: es.enter_context(nc.semaphore(f"sem_{s}")) for s in self.SEMS}

    PARANOID = False

    def rec(self, eng, fn, wants=(), dma=False):
        inc = 16 if dma else 1
        sk = "gdma" if (eng == "gpsimd" and dma) else eng
        if Rec.PARANOID:
            wants = list(wants) + self.marks()
        wl = [(s, v) for (s, v) in wants
              if v > 0 and not (s == eng and s in ("tensor", "vector", "scalar"))]
        self.ops[eng].append((fn, wl, inc, sk))
        self.cnt[sk] += inc
        return (sk, self.cnt[sk])

    def mark(self, eng):
        return (eng, self.cnt[eng])

    def marks(self):
        return [(s, self.cnt[s]) for s in self.SEMS]

    def emit(self, block):
        for eng_name in self.ENGS:
            ops = self.ops[eng_name]
            if not ops:
                continue
            sec = getattr(block, eng_name)

            def body(eng, eng_name=eng_name, ops=ops):
                seen = {s: 0 for s in self.SEMS}
                for fn, wants, inc, sk in ops:
                    for (pe, pv) in wants:
                        if seen[pe] < pv:
                            eng.wait_ge(self.sem[pe], pv)
                            seen[pe] = pv
                    instr = fn(eng)
                    instr.then_inc(self.sem[sk], inc)
                for sk2 in (eng_name, "gdma") if eng_name == "gpsimd" else (eng_name,):
                    if self.cnt[sk2]:
                        eng.wait_ge(self.sem[sk2], self.cnt[sk2])
            sec(body)


class Builder:

    def __init__(self, nc, es):
        self.nc = nc
        self.es = es
        self.R = Rec(nc, es)

    def dma(self, out, in_, eng="sync", wants=(), slow=False):
        return self.R.rec(eng, lambda g: g.dma_start(
            out=out, in_=in_, allow_slow_non_contiguous=slow), wants=wants, dma=True)

    def mm(self, out, lhsT, rhs, start, stop, wants=()):
        return self.R.rec("tensor",
                          lambda t: t.matmul(out, lhsT, rhs, start=start, stop=stop),
                          wants=wants)

    def tr(self, out, in_, ident, wants=()):
        return self.R.rec("tensor", lambda t: t.transpose(out, in_, ident),
                          wants=wants)

    def v(self, fn, wants=()):
        return self.R.rec("vector", fn, wants=wants)

    def s(self, fn, wants=()):
        return self.R.rec("scalar", fn, wants=wants)

    def g(self, fn, wants=(), dma=False):
        return self.R.rec("gpsimd", fn, wants=wants, dma=dma)

    def sb(self, name, shape, dt):
        return self.es.enter_context(self.nc.sbuf_tensor(name, shape, dt))

    def ps(self, name, shape, dt):
        return self.es.enter_context(self.nc.psum_tensor(name, shape, dt))


def build_A():
    Rec.PARANOID = bool(int(os.environ.get('KPAR', '0')))
    nc = bass.Bass("TRN2", target_bir_lowering=False, detect_race_conditions=False)
    es = ExitStack()
    B = Builder(nc, es)
    R = B.R

    def din(name, shape, dt=F32):
        return nc.dram_tensor(name, shape, dt, kind="ExternalInput").ap()

    x_in = din("x_slice", [HX + T, D])
    Wq_d = din("Wq", [D, D]); Wk_d = din("Wk", [D, D]); Wv_d = din("Wv", [D, D])
    Wg_d = din("Wg", [D, D]); Wo_d = din("Wo", [D, D])
    Wb_d = din("Wb", [D, H]); bb_d = din("bb", [1, H])
    qcw_d = din("qconv_w", [D, 4]); qcb_d = din("qconv_b", [1, D])
    kcw_d = din("kconv_w", [D, 4]); kcb_d = din("kconv_b", [1, D])
    qnw_d = din("qn_w", [1, DK]); qnb_d = din("qn_b", [1, DK])
    knw_d = din("kn_w", [1, DK]); knb_d = din("kn_b", [1, DK])
    WupT_d = din("WupT", [D, DC]); bup_d = din("b_up", [1, DC])
    WdnT_d = din("WdnT", [DC, DP]); bdn_d = din("b_dn", [1, DP])
    WqrT_d = din("WqrT", [DP, DP])
    keysT_d = din("keysT", [16, 64, 128])
    de_d = din("de_f32", [CE, DP])
    ue_d = din("ue_f32", [CE, D])

    out_A = nc.dram_tensor("out_A", [T, D], F32, kind="ExternalOutput").ap()
    S_out = nc.dram_tensor("S_out", [DK, H * DK], F32, kind="ExternalOutput").ap()
    D_out = nc.dram_tensor("D_out", [1, H], F32, kind="ExternalOutput").ap()
    beta_out = nc.dram_tensor("beta_out", [T, H], F32, kind="ExternalOutput").ap()
    qfm_out = nc.dram_tensor("qfm_out", [D, T], BF16, kind="ExternalOutput").ap()
    gtm_out = nc.dram_tensor("gtm_out", [T, D], BF16, kind="ExternalOutput").ap()

    WupT_bf = nc.dram_tensor("WupT_bf", [D, DC], BF16, kind="Internal").ap()
    WdnT_bf = nc.dram_tensor("WdnT_bf", [DC, DP], BF16, kind="Internal").ap()
    attn_dr = nc.dram_tensor("attn_dr", [T, D], BF16, kind="Internal").ap()

    TFX = T + HX
    TFQ = 512 + HX
    xn = B.sb("xn", [128, 16, TFX], BF16)
    sq = B.sb("sq", [128, D], F32)
    scr2 = B.sb("scr2", [128, D], BF16)
    cols = B.sb("cols", [128, 16, 8], F32)
    wA = B.sb("wA", [128, 16, 128], BF16)
    wB = B.sb("wB", [128, 1024], BF16)
    beta_b = B.sb("beta_b", [128, NCH, H], F32)
    Ltm = B.sb("Ltm", [128, NCH, H], F32)
    Acol = B.sb("Acol", [128, NCH, H], F32)
    bAc = B.sb("bAc", [128, NCH, H], F32)
    vsc = B.sb("vsc", [128, NCH, H], F32)
    aend = B.sb("aend", [128, NCH, H], F32)
    tri = B.sb("tri", [128, 128], BF16)
    triF = B.sb("triF", [128, 128], F32)
    sel127 = B.sb("sel127", [128, 128], F32)
    ident = B.sb("ident", [128, 128], BF16)
    onesb = B.sb("onesb", [128, 128], BF16)
    csel = B.sb("csel", [128, H, H], BF16)
    rsel = B.sb("rsel", [16, H, 128], BF16)
    ph_t = B.sb("ph_t", [128, 128], BF16)
    Ssb = B.sb("Ssb", [128, H, DK], F32)
    bbc = B.sb("bbc", [128, H], F32)
    qkc = B.sb("qkc", [128, 2, 16, 4], F32)
    qkb = B.sb("qkb", [128, 2, 16], F32)
    lnw = B.sb("lnw", [128, 4], F32)
    small = B.sb("small", [128, 64], F32)
    epsc = B.sb("epsc", [128, 1], F32)

    pT = B.ps("pT", [128, 1024], BF16)

    blk = es.enter_context(nc.Block())
    barrier_floor = {}

    def barrier():
        for e, v in R.marks():
            barrier_floor[e] = v
    orig_rec = R.rec
    def rec_with_floor(eng, fn, wants=(), dma=False):
        w = list(wants) + [(e, v) for e, v in barrier_floor.items()]
        return orig_rec(eng, fn, w, dma)
    R.rec = rec_with_floor
    B.g(lambda g: g.memset(epsc[:, :], EPS))
    m0 = B.g(lambda g: g.memset(tri[:, :], 1.0))
    B.g(lambda g: g.affine_select(out=tri[:, :], in_=tri[:, :], compare_op=ALU.is_ge,
        fill=0.0, base=0, pattern=[[1, 128]], channel_multiplier=-1), wants=[m0])
    m0 = B.g(lambda g: g.memset(triF[:, :], 1.0))
    B.g(lambda g: g.affine_select(out=triF[:, :], in_=triF[:, :], compare_op=ALU.is_ge,
        fill=0.0, base=0, pattern=[[1, 128]], channel_multiplier=-1), wants=[m0])
    m0 = B.g(lambda g: g.memset(sel127[:, :], 1.0))
    B.g(lambda g: g.affine_select(out=sel127[:, :], in_=sel127[:, :], compare_op=ALU.is_equal,
        fill=0.0, base=-127, pattern=[[0, 128]], channel_multiplier=1), wants=[m0])
    m0 = B.g(lambda g: g.memset(ident[:, :], 0.0))
    B.g(lambda g: g.affine_select(out=ident[:, :], in_=ident[:, :], compare_op=ALU.not_equal,
        fill=1.0, base=0, pattern=[[-1, 128]], channel_multiplier=1), wants=[m0])
    B.g(lambda g: g.memset(onesb[:, :], 1.0))
    for h in range(H):
        m0 = B.g(lambda g, h=h: g.memset(csel[:, h, :], 1.0))
        B.g(lambda g, h=h: g.affine_select(out=csel[:, h, :], in_=csel[:, h, :],
            compare_op=ALU.is_equal, fill=0.0, base=-h, pattern=[[1, H]],
            channel_multiplier=0), wants=[m0])
        m0 = B.g(lambda g, h=h: g.memset(rsel[:, h, :], 1.0))
        B.g(lambda g, h=h: g.affine_select(out=rsel[:, h, :], in_=rsel[:, h, :],
            compare_op=ALU.is_equal, fill=0.0, base=-h, pattern=[[0, 128]],
            channel_multiplier=1), wants=[m0])
    m_consts = R.mark("gpsimd")

    for ec in range(16):
        B.g(lambda g, ec=ec: g.dma_start(out=qkc[:, 0, ec, :], in_=qcw_d[ec * 128:(ec + 1) * 128, :]), dma=True)
        B.g(lambda g, ec=ec: g.dma_start(out=qkc[:, 1, ec, :], in_=kcw_d[ec * 128:(ec + 1) * 128, :]), dma=True)
    for _dst, _src in ((qkb[:, 0, :], qcb_d), (qkb[:, 1, :], kcb_d)):
        B.g(lambda g, _dst=_dst, _src=_src: g.dma_start(
            out=_dst, in_=_src.rearrange("o (a p) -> p (o a)", p=128),
            allow_slow_non_contiguous=True), dma=True)
    for _i, _src in ((0, qnw_d), (1, qnb_d), (2, knw_d), (3, knb_d)):
        B.g(lambda g, _i=_i, _src=_src: g.dma_start(
            out=lnw[:, _i:_i + 1], in_=_src.rearrange("o p -> p o"),
            allow_slow_non_contiguous=True), dma=True)
    m_params = R.mark("gpsimd")

    m_xn = []
    prev_mxn = None
    prev_msd = None
    for i in range(9):
        nrow = HX if i == 0 else 128
        r0 = 0 if i == 0 else HX + (i - 1) * 128
        f0 = r0
        mldx = B.dma(sq[0:nrow, :], x_in[r0:r0 + nrow, :],
                     wants=[prev_mxn] if prev_mxn else [])
        msum = B.v(lambda v, n=nrow, i=i: v.tensor_reduce(
            cols[0:n, i, 0:1], sq[0:n, :], mybir.AxisListType.X, ALU.add), wants=[mldx])
        msq_s = B.s(lambda s, n=nrow, i=i: s.activation(
            scr2[0:n, :], sq[0:n, :], AF.Square), wants=[mldx] + m_xn[-1:])
        msq = B.v(lambda v, n=nrow, i=i: v.tensor_reduce(
            cols[0:n, i, 1:2], scr2[0:n, :], mybir.AxisListType.X, ALU.add),
            wants=[msq_s])
        mm_ = B.v(lambda v, n=nrow, i=i: v.tensor_scalar(
            cols[0:n, i, 2:3], cols[0:n, i, 0:1], 1.0 / D, None, ALU.mult), wants=[msum])
        mex2 = B.v(lambda v, n=nrow, i=i: v.tensor_scalar(
            cols[0:n, i, 4:5], cols[0:n, i, 1:2], 1.0 / D, None, ALU.mult),
            wants=[msq] + ([prev_msd] if prev_msd else []))
        mvar = B.v(lambda v, n=nrow, i=i: v.scalar_tensor_tensor(
            cols[0:n, i, 7:8], cols[0:n, i, 2:3], cols[0:n, i, 2:3], cols[0:n, i, 4:5],
            ALU.mult, ALU.subtract), wants=[mex2, mm_])
        msd = B.s(lambda s, n=nrow, i=i: s.activation(
            cols[0:n, i, 5:6], cols[0:n, i, 7:8], AF.Sqrt, bias=epsc[0:n, 0:1],
            scale=-1.0), wants=[mvar, m_consts])
        mr = B.v(lambda v, n=nrow, i=i: v.reciprocal(cols[0:n, i, 6:7], cols[0:n, i, 5:6]), wants=[msd])
        mxn = B.v(lambda v, n=nrow, i=i: v.tensor_scalar(
            scr2[0:n, :], sq[0:n, :], cols[0:n, i, 2:3], cols[0:n, i, 6:7],
            ALU.subtract, ALU.mult), wants=[mr])
        prev_mxn = mxn
        prev_msd = msd
        for ec in range(16):
            mtr = B.tr(pT[:, 0:nrow],
                       scr2[0:nrow, ec * 128:(ec + 1) * 128], ident[0:nrow, 0:nrow],
                       wants=[mxn, m_consts] + m_xn[-1:])
            mcp = B.v(lambda v, ec=ec, f0=f0, n=nrow: v.tensor_copy(
                xn[:, ec, f0:f0 + n], pT[:, 0:n]), wants=[mtr])
            m_xn.append(mcp)
    m_xn_done = R.marks()
    if int(os.environ.get("KPH", "99")) <= 1:
        R.emit(blk); es.close(); return nc

    es_attn = ExitStack()
    def asb(name, shape, dt):
        return es_attn.enter_context(nc.sbuf_tensor(name, shape, dt))
    def aps(name, shape, dt):
        return es_attn.enter_context(nc.psum_tensor(name, shape, dt))

    qf = asb("qf", [128, 16, TFQ], BF16)
    kf = asb("kf", [128, 16, TFQ], BF16)
    ktm = asb("ktm", [128, 4, 2048], BF16)
    vtm = asb("vtm", [128, 4, 2048], BF16)
    gtm_b = asb("gtm_b", [128, 4, 2048], BF16)
    ogf = asb("ogf", [128, 16, 512], BF16)
    wbt = asb("wbt", [128, 16, H], BF16)
    onesF = asb("onesF", [1, 128], F32)
    cscr = asb("cscr", [128, TFQ], F32)
    ysc = asb("ysc", [128, 512], BF16)
    ogst = asb("ogst", [128, 128], BF16)
    woR = asb("woR", [128, 16, 256], BF16)
    ost = asb("ost", [128, 2, 256], BF16)
    Ssbb = asb("Ssbb", [128, H, DK], BF16)
    rowst = asb("rowst", [16, 4, 512], F32)
    rstb = asb("rstb", [16, 2, 512], BF16)

    pB1 = aps("pB1", [128, 512], F32)
    pB2 = aps("pB2", [128, 512], F32)
    pstS = aps("pstS", [16, 512], F32)
    pstQ = aps("pstQ", [16, 512], F32)
    pP = aps("pP", [128, 512], F32)
    pO = aps("pO", [128, 512], F32)
    pS = aps("pS", [128, 512], F32)

    B.g(lambda g: g.dma_start(out=wbt[:, :, :],
        in_=Wb_d.rearrange("(a p) h -> p a h", p=128)), dma=True)
    mbb = B.dma(small[0:1, 0:16], bb_d)
    B.v(lambda v: v.memset(onesF[:, :], 1.0))
    monesF = R.mark("vector")
    mbbmm = B.mm(pP[:, 0:16], onesF[0:1, :], small[0:1, 0:16], True, True,
                 wants=[mbb, monesF])
    mbbc = B.v(lambda v: v.tensor_copy(bbc[:, :], pP[:, 0:16]), wants=[mbbmm])
    m_wbt = R.mark("gpsimd")

    mprev_beta = None
    for tc in range(NCH):
        wl = [m_wbt, mbbc] + m_xn_done + R.marks()
        mmb = None
        for dc in range(16):
            mmb = B.mm(pP[:, 0:16], xn[:, dc, HX + tc * 128:HX + (tc + 1) * 128],
                       wbt[:, dc, :], dc == 0, dc == 15, wants=wl)
        ma1 = B.v(lambda v: v.tensor_add(small[:, 0:16], pP[:, 0:16], bbc[:, :]),
                  wants=[mmb])
        msg = B.s(lambda s, tc=tc: s.activation(beta_b[:, tc, :], small[:, 0:16],
                  AF.Sigmoid), wants=[ma1])
        md = B.v(lambda v, tc=tc: v.tensor_scalar(small[:, 16:32], beta_b[:, tc, :],
                 -1.0, 1.0, ALU.mult, ALU.add), wants=[msg])
        mln = B.s(lambda s: s.activation(small[:, 32:48], small[:, 16:32], AF.Ln),
                  wants=[md])
        mcs = B.mm(pO[:, 0:16], triF[:, :], small[:, 32:48], True, True, wants=[mln])
        mL = B.v(lambda v, tc=tc: v.tensor_copy(Ltm[:, tc, :], pO[:, 0:16]), wants=[mcs])
        mA = B.s(lambda s, tc=tc: s.activation(Acol[:, tc, :], Ltm[:, tc, :], AF.Exp),
                 wants=[mL])
        meN = B.s(lambda s, tc=tc: s.activation(small[:, 48:64], Ltm[:, tc, :],
                  AF.Exp, scale=-1.0), wants=[mL])
        mbA = B.v(lambda v, tc=tc: v.tensor_mul(bAc[:, tc, :], beta_b[:, tc, :],
                  small[:, 48:64]), wants=[meN, msg])
        mLe = B.mm(pS[:, 0:16], sel127[:, :], Ltm[:, tc, :], True, True, wants=[mL])
        mae = B.s(lambda s, tc=tc: s.activation(aend[:, tc, :], pS[:, 0:16], AF.Exp),
                  wants=[mLe])
        mvs = B.v(lambda v, tc=tc: v.tensor_mul(vsc[:, tc, :], aend[:, tc, :],
                  bAc[:, tc, :]), wants=[mae, mbA])
        mprev_beta = mvs
    mDs = B.v(lambda v: v.tensor_copy(small[:, 0:16], aend[:, 0, :]),
              wants=[mprev_beta] + R.marks())
    for tc in range(1, NCH):
        mDs = B.v(lambda v, tc=tc: v.tensor_mul(small[:, 0:16], small[:, 0:16],
                  aend[:, tc, :]), wants=[mDs])
    B.dma(D_out, small[0:1, 0:16], wants=[mDs])
    B.dma(beta_out.rearrange("(a p) h -> p a h", a=NCH), beta_b[:, :, :],
          wants=[mprev_beta])
    m_decay = R.marks()
    if int(os.environ.get("KPH", "99")) <= 2:
        es_attn.close(); R.emit(blk); es.close(); return nc

    mSsb0 = B.v(lambda v: v.memset(Ssb[:, :, :], 0.0))
    mSsb0 = B.v(lambda v: v.memset(Ssbb[:, :, :], 0.0), wants=[mSsb0])
    m_last_scan = [mSsb0]

    for hf in range(2):
        t0 = 512 * hf
        fs = t0
        for (dst, Wd) in ((qf, Wq_d), (kf, Wk_d)):
            for ec in range(16):
                mw = B.g(lambda g, Wd=Wd, ec=ec: g.dma_start(
                    out=wA[:, :, :],
                    in_=Wd[:, ec * 128:(ec + 1) * 128].rearrange("(a p) e -> p a e", p=128)),
                    wants=R.marks() if ec == 0 else [m_ev1, m_ev2], dma=True)
                mm1 = mm2 = None
                for dc in range(16):
                    mm1 = B.mm(pB1[:, :], wA[:, dc, :], xn[:, dc, fs:fs + 512],
                               dc == 0, dc == 15, wants=[mw])
                    mm2 = B.mm(pB2[:, 0:32], wA[:, dc, :], xn[:, dc, fs + 512:fs + 544],
                               dc == 0, dc == 15, wants=[mw])
                m_ev1 = B.v(lambda v, dst=dst, ec=ec: v.tensor_copy(
                    dst[:, ec, 0:512], pB1[:, :]), wants=[mm1])
                m_ev2 = B.v(lambda v, dst=dst, ec=ec: v.tensor_copy(
                    dst[:, ec, 512:544], pB2[:, 0:32]), wants=[mm2])
        m_proj_qk = R.marks()

        m_conv_last = m_proj_qk
        for cs, dst in ((0, qf), (1, kf)):
            for ec in range(16):
                mc = B.v(lambda v, cs=cs, dst=dst, ec=ec: v.tensor_scalar(
                    cscr[:, 3:TFQ], dst[:, ec, 0:TFQ - 3], qkc[:, cs, ec, 0:1], None,
                    ALU.mult), wants=[m_params] + m_conv_last)
                for j in (1, 2, 3):
                    mc = B.v(lambda v, cs=cs, dst=dst, ec=ec, j=j: v.scalar_tensor_tensor(
                        cscr[:, 3:TFQ], dst[:, ec, j:TFQ - 3 + j], qkc[:, cs, ec, j:j + 1],
                        cscr[:, 3:TFQ], ALU.mult, ALU.add), wants=[mc])
                msl = B.s(lambda s, cs=cs, dst=dst, ec=ec: s.activation(
                    dst[:, ec, HX - 3:TFQ], cscr[:, HX - 3:TFQ], AF.Silu,
                    bias=qkb[:, cs, ec:ec + 1], scale=1.0), wants=[mc])
                m_conv_last = [msl]
        m_conv = R.marks()

        for cs, dst in ((0, qf), (1, kf)):
            wcol = lnw[:, 2 * cs:2 * cs + 1]
            bcol = lnw[:, 2 * cs + 1:2 * cs + 2]
            mcs0 = R.marks()
            mq_prev = mcs0
            msum = msq = None
            for h in range(H):
                msq_s = B.s(lambda s, dst=dst, h=h: s.activation(
                    scr2[:, 0:512], dst[:, h, HX:TFQ], AF.Square),
                    wants=mq_prev + ([msq] if msq else []))
                msum = B.mm(pstS[:, :], csel[:, h, :], dst[:, h, HX:TFQ],
                            h == 0, h == 15, wants=mcs0)
                msq = B.mm(pstQ[:, :], csel[:, h, :], scr2[:, 0:512],
                           h == 0, h == 15, wants=[msq_s])
                mq_prev = []
            mm_ = B.v(lambda v: v.tensor_scalar(rowst[:, 0, :], pstS[:, :], 1.0 / DK,
                      None, ALU.mult), wants=[msum])
            me2 = B.v(lambda v: v.tensor_scalar(rowst[:, 1, :], pstQ[:, :], 1.0 / DK,
                      None, ALU.mult), wants=[msq])
            mm2_ = B.v(lambda v: v.tensor_mul(rowst[:, 2, :], rowst[:, 0, :],
                       rowst[:, 0, :]), wants=[mm_])
            mvr = B.v(lambda v: v.tensor_sub(rowst[:, 1, :], rowst[:, 1, :],
                      rowst[:, 2, :]), wants=[me2, mm2_])
            msd = B.s(lambda s: s.activation(rowst[:, 1, :], rowst[:, 1, :], AF.Sqrt,
                      bias=epsc[0:16, 0:1], scale=1.0), wants=[mvr])
            mr = B.v(lambda v: v.reciprocal(rowst[:, 2, :], rowst[:, 1, :]), wants=[msd])
            mrm = B.v(lambda v: v.tensor_mul(rowst[:, 3, :], rowst[:, 2, :],
                      rowst[:, 0, :]), wants=[mr])
            mrb = B.v(lambda v: v.tensor_copy(rstb[:, 0, :], rowst[:, 2, :]), wants=[mr])
            mrmb = B.v(lambda v: v.tensor_copy(rstb[:, 1, :], rowst[:, 3, :]), wants=[mrm])
            my1 = my2 = None
            for h in range(H):
                mB1 = B.mm(pB1[:, :], rsel[:, h, :], rstb[:, 0, :], True, True,
                           wants=[mrb] + ([my1] if my1 else []))
                mB2 = B.mm(pB2[:, :], rsel[:, h, :], rstb[:, 1, :], True, True,
                           wants=[mrmb] + ([my2] if my2 else []))
                my1 = B.v(lambda v, dst=dst, h=h: v.tensor_mul(
                    ysc[:, :], dst[:, h, HX:TFQ], pB1[:, :]), wants=[mB1])
                my2 = B.v(lambda v: v.tensor_sub(ysc[:, :], ysc[:, :], pB2[:, :]),
                          wants=[mB2, my1])
                my3 = B.v(lambda v, dst=dst, h=h, wcol=wcol, bcol=bcol: v.tensor_scalar(
                    dst[:, h, HX:TFQ], ysc[:, :], wcol, bcol, ALU.mult, ALU.add),
                    wants=[my2])
        m_ln = R.marks()

        B.dma(qfm_out[:, t0:t0 + 512].rearrange("(a p) t -> p a t", p=128),
              qf[:, :, HX:TFQ], wants=m_ln)
        mtr_prev = None
        for ec in range(16):
            for tc in range(4):
                mtr = B.tr(pT[:, 0:128],
                           kf[:, ec, HX + tc * 128:HX + (tc + 1) * 128], ident[:, :],
                           wants=m_ln + ([mtr_prev] if mtr_prev else []))
                mcp = B.v(lambda v, ec=ec, tc=tc: v.tensor_copy(
                    ktm[:, tc, ec * 128:(ec + 1) * 128], pT[:, 0:128]), wants=[mtr])
                mtr_prev = mcp
        m_ktm = R.marks()

        for pi, (Wd, tdst, act) in enumerate(((Wv_d, vtm, None), (Wg_d, gtm_b, AF.Silu))):
            for ec in range(16):
                mw = B.g(lambda g, Wd=Wd, ec=ec: g.dma_start(
                    out=wA[:, :, :],
                    in_=Wd[:, ec * 128:(ec + 1) * 128].rearrange("(a p) e -> p a e", p=128)),
                    wants=R.marks() if ec == 0 else [m_ev1], dma=True)
                mm1 = None
                for dc in range(16):
                    mm1 = B.mm(pB1[:, :], wA[:, dc, :], xn[:, dc, fs + HX:fs + HX + 512],
                               dc == 0, dc == 15, wants=[mw])
                if act is None:
                    m_ev1 = B.v(lambda v, ec=ec: v.tensor_copy(
                        ogf[:, ec, :], pB1[:, :]), wants=[mm1])
                else:
                    m_ev1 = B.s(lambda s, ec=ec: s.activation(
                        ogf[:, ec, :], pB1[:, :], AF.Silu), wants=[mm1])
            m_vg = R.marks()
            mtr_prev = None
            for ec in range(16):
                for tc in range(4):
                    mtr = B.tr(pT[:, 0:128],
                               ogf[:, ec, tc * 128:(tc + 1) * 128], ident[:, :],
                               wants=m_vg + ([mtr_prev] if mtr_prev else []))
                    mcp = B.v(lambda v, tdst=tdst, ec=ec, tc=tc: v.tensor_copy(
                        tdst[:, tc, ec * 128:(ec + 1) * 128], pT[:, 0:128]),
                        wants=[mtr])
                    mtr_prev = mcp
        m_vgt = R.marks()
        B.dma(gtm_out[t0:t0 + 512, :].rearrange("(a p) d -> p a d", a=4),
              gtm_b[:, :, :], wants=m_vgt)

        Sdep = m_last_scan if len(m_last_scan) == H else [m_last_scan[0]] * H
        for cc in range(4):
            c = 4 * hf + cc
            mc0 = R.marks()
            mvst = B.v(lambda v, cc=cc, c=c: v.tensor_tensor(
                scr2[:, :].rearrange("p (h e) -> p h e", h=H),
                vtm[:, cc, :].rearrange("p (h e) -> p h e", h=H),
                bass.AP(vsc[:].tensor, vsc[:, c, :].offset, [vsc[:].ap[0], [1, H], [0, 128]]),
                ALU.mult), wants=mc0)
            mPh_prev = None
            mog_prev = None
            mtrp = [None, None]
            mSu_prev = None
            for h in range(H):
                fq = HX + cc * 128
                mP = B.mm(pP[:, 0:128], kf[:, h, fq:fq + 128], qf[:, h, fq:fq + 128],
                          True, True, wants=mc0 + ([mPh_prev] if mPh_prev else []))
                mPh = B.v(lambda v, c=c, h=h: v.scalar_tensor_tensor(
                    ph_t[:, :], pP[:, 0:128], bAc[:, c, h:h + 1], tri[:, :],
                    ALU.mult, ALU.mult), wants=[mP])
                mo1 = B.mm(pO[:, 0:128], qf[:, h, fq:fq + 128], Ssbb[:, h, :],
                           True, False, wants=[Sdep[h]] + ([mog_prev] if mog_prev else []))
                mo2 = B.mm(pO[:, 0:128], ph_t[:, :], vtm[:, cc, h * 128:(h + 1) * 128],
                           False, True, wants=[mPh])
                mog = B.v(lambda v, c=c, cc=cc, h=h: v.scalar_tensor_tensor(
                    ogst[:, :], pO[:, 0:128], Acol[:, c, h:h + 1],
                    gtm_b[:, cc, h * 128:(h + 1) * 128], ALU.mult, ALU.mult),
                    wants=[mo2])
                mtr = B.tr(pT[:, 0:128], ogst[:, :], ident[:, :],
                           wants=[mog] + ([mtrp[0]] if mtrp[0] else []))
                mcpog = B.v(lambda v, h=h, cc=cc: v.tensor_copy(
                    ogf[:, h, cc * 128:(cc + 1) * 128], pT[:, 0:128]), wants=[mtr])
                mtrp[0] = mcpog
                mdS = B.mm(pS[:, 0:128], ktm[:, cc, h * 128:(h + 1) * 128],
                           scr2[:, h * 128:(h + 1) * 128], True, True,
                           wants=[mvst] + ([mSu_prev] if mSu_prev else []))
                mSu = B.v(lambda v, c=c, h=h: v.scalar_tensor_tensor(
                    Ssb[:, h, :], Ssb[:, h, :], aend[:, c, h:h + 1], pS[:, 0:128],
                    ALU.mult, ALU.add), wants=[mdS, mo1])
                mSu = B.v(lambda v, h=h: v.tensor_copy(Ssbb[:, h, :], Ssb[:, h, :]),
                          wants=[mSu])
                Sdep[h] = mSu
                mSu_prev = mSu
                mog_prev = mog
                mPh_prev = mo2
        m_last_scan = Sdep

        m_scanned = R.marks()
        mwo_prev = None
        for dgq in range(8):
            mwo = B.g(lambda g, dgq=dgq: g.dma_start(
                out=woR[:, :, :],
                in_=Wo_d[:, dgq * 256:(dgq + 1) * 256].rearrange("(a p) e -> p a e", p=128)),
                wants=m_scanned + ([mwo_prev] if mwo_prev else []), dma=True)
            mev_prev = None
            mst_prev = [None, None]
            for tt2 in range(4):
                mo = None
                for vc in range(16):
                    mo = B.mm(pB1[:, 0:256], ogf[:, vc, tt2 * 128:(tt2 + 1) * 128],
                              woR[:, vc, :], vc == 0, vc == 15,
                              wants=[mwo] + ([mev_prev] if mev_prev else []))
                slot = tt2 % 2
                mev = B.v(lambda v, slot=slot: v.tensor_copy(
                    ost[:, slot, :], pB1[:, 0:256]),
                    wants=[mo] + ([mst_prev[slot]] if mst_prev[slot] else []))
                mst = B.dma(attn_dr[t0 + tt2 * 128:t0 + (tt2 + 1) * 128,
                                    dgq * 256:(dgq + 1) * 256],
                            ost[:, slot, :], wants=[mev])
                mst_prev[slot] = mst
                mev_prev = mev
            mwo_prev = mo
    B.dma(S_out, Ssb[:, :, :].rearrange("p h e -> p (h e)"), wants=R.marks())
    m_attn_done = R.marks()
    if int(os.environ.get("KPH", "99")) <= 3:
        es_attn.close(); R.emit(blk); es.close(); return nc

    es_attn.close()
    barrier()

    es_ffn = ExitStack()
    def fsb(name, shape, dt):
        return es_ffn.enter_context(nc.sbuf_tensor(name, shape, dt))
    def fps(name, shape, dt):
        return es_ffn.enter_context(nc.psum_tensor(name, shape, dt))

    h2f = fsb("h2f", [128, 8, 1024], BF16)
    h2t = fsb("h2t", [128, 8, 1024], BF16)
    qrf = fsb("qrf", [128, 8, 1024], BF16)
    keyb = fsb("keyb", [128, 16, 128], BF16)
    encb = fsb("encb", [128, 16, 128], F32)
    a64 = fsb("a64", [128, 512], F32)
    mx8 = fsb("mx8", [128, 16, 8], F32)
    r8 = fsb("r8", [128, 128], F32)
    kid = fsb("kid", [128, 16, 8], F32)
    sc8 = fsb("sc8", [128, 8, 8], F32)
    scR = fsb("scR", [128, 64], F32)
    dec2 = fsb("dec2", [128, 4, 64], F32)
    ixsel = fsb("ixsel", [128, 64], F32)
    iysel = fsb("iysel", [128, 64], F32)
    eidf = fsb("eidf", [128, 64], F32)
    eidi = fsb("eidi", [128, 64], I32)
    wsm = fsb("wsm", [128, 64], F32)
    smax = fsb("smax", [128, 8], F32)
    ssum = fsb("ssum", [128, 8], F32)
    zbuf = fsb("zbuf", [128, 128], F32)
    zt = fsb("zt", [128, 64], F32)
    deg = fsb("deg", [128, 2, 1024], BF16)
    ueg = fsb("ueg", [128, 2, 2048], BF16)
    acc4 = fsb("acc4", [128, 2, 2048], BF16)
    accF = fsb("accF", [128, 8, 2048], BF16)
    fout = fsb("fout", [128, 2048], F32)
    KM1 = fsb("KM1", [128, 128], F32)
    KM2 = fsb("KM2", [128, 512], F32)
    IOTA8 = fsb("IOTA8", [128, 512], BF16)

    pH1 = fps("pH1", [128, 512], F32)
    pH2 = fps("pH2", [128, 2048], F32)
    pQ = fps("pQ", [128, 512], F32)
    pSim = fps("pSim", [128, 512], F32)

    mi1 = B.g(lambda g: g.iota(KM1[:, :], pattern=[[1, 128]], base=0, channel_multiplier=0, allow_small_or_imprecise_dtypes=True))
    B.v(lambda v: v.tensor_scalar(KM1[:, :], KM1[:, :], -0.0078125, 63.5 * 0.0078125,
        ALU.mult, ALU.add), wants=[mi1])
    mi2 = B.g(lambda g: g.iota(KM2[:, :].rearrange("p (a m) -> p a m", a=8),
                               pattern=[[0, 8], [1, 64]], base=0, channel_multiplier=0, allow_small_or_imprecise_dtypes=True))
    B.v(lambda v: v.tensor_scalar(KM2[:, :], KM2[:, :], -0.015625, 31.5 * 0.015625,
        ALU.mult, ALU.add), wants=[mi2])
    B.g(lambda g: g.iota(IOTA8[:, :].rearrange("p (a j) -> p a j", a=64),
                         pattern=[[0, 64], [1, 8]], base=0, channel_multiplier=0, allow_small_or_imprecise_dtypes=True))
    mkey = B.g(lambda g: g.dma_start(out=keyb[0:64, :, :],
               in_=keysT_d.rearrange("a n k -> n a k")), dma=True)
    mkey2 = B.g(lambda g: g.dma_start(out=keyb[64:128, :, :],
                in_=keysT_d.rearrange("a n k -> n a k")), dma=True)
    mkey = mkey2
    m_ffn_const = R.marks()

    conv_marks = []
    def convert(src_d, dst_d, nrows, width):
        nonlocal_marks = []
        prev = [None, None]
        for i in range(nrows // 128):
            for w0 in range(0, width, 2048):
                wlen = min(2048, width - w0)
                slot = (i * ((width + 2047) // 2048) + w0 // 2048) % 2
                buf = ueg[:, slot, :].rearrange("p x -> p x")
                mg_ = B.g(lambda g, i=i, w0=w0, wlen=wlen, slot=slot, src_d=src_d:
                          g.dma_start(out=ueg[:, slot, 0:wlen],
                                      in_=src_d[i * 128:(i + 1) * 128, w0:w0 + wlen]),
                          wants=[prev[slot]] if prev[slot] else [], dma=True)
                ms_ = B.dma(dst_d[i * 128:(i + 1) * 128, w0:w0 + wlen],
                            ueg[:, slot, 0:wlen], wants=[mg_])
                prev[slot] = ms_
        return prev
    cm = convert(WupT_d, WupT_bf, D, DC)
    conv_marks += [m for m in cm if m]
    cm = convert(WdnT_d, WdnT_bf, DC, DP)
    conv_marks += [m for m in cm if m]
    m_conv_done = R.marks()

    bupc = small
    bdnc = cols[:, 15, :]
    mbu = B.dma(bupc[:, 0:64], bup_d.rearrange("o (a p) -> p (o a)", p=128),
                wants=m_ffn_const, slow=True)
    mbd = B.dma(bdnc[0:128, 0:8], bdn_d.rearrange("o (a p) -> p (o a)", p=128),
                wants=m_ffn_const, slow=True)
    m_bias = R.marks()
    h1g = fsb("h1g", [128, 8, 256], BF16)
    mh2ev_all = []
    for ts4 in range(4):
        f0 = HX + ts4 * 256
        mwup_prev = None
        mh1ev = None
        macc = [None] * 8
        mac_prev = None
        for cg in range(8):
            mh1evs = []
            mh2rd = None
            for ci in range(8):
                c = cg * 8 + ci
                mwup = B.dma(wA[:, :, :],
                             WupT_bf[:, c * 128:(c + 1) * 128].rearrange("(a p) e -> p a e", p=128),
                             wants=m_conv_done + ([mwup_prev] if mwup_prev else []))
                mm1 = None
                for dc in range(16):
                    mm1 = B.mm(pH1[:, 0:256], wA[:, dc, :], xn[:, dc, f0:f0 + 256],
                               dc == 0, dc == 15,
                               wants=[mwup] + ([mh1ev] if mh1ev else []))
                mh1ev = B.s(lambda s, c=c, ci=ci: s.activation(
                    h1g[:, ci, :], pH1[:, 0:256], AF.Silu, bias=bupc[:, c:c + 1], scale=1.0),
                    wants=[mm1] + m_bias + R.marks()[0:0])
                mh1evs.append(mh1ev)
                mwup_prev = mm1
            mh2rd_last = None
            for pc in range(8):
                mh2rd = None
                for ci in range(8):
                    c = cg * 8 + ci
                    mwd = B.dma(wB[:, ci * 128:(ci + 1) * 128],
                                WdnT_bf[c * 128:(c + 1) * 128, pc * 128:(pc + 1) * 128],
                                wants=m_conv_done + ([mh2rd_last] if mh2rd_last else []))
                    mh2rd = B.mm(pH2[:, 0:256], wB[:, ci * 128:(ci + 1) * 128],
                                 h1g[:, ci, :], ci == 0, ci == 7,
                                 wants=[mh1evs[ci], mwd] + ([mac_prev] if mac_prev and ci == 0 else []))
                mh2rd_last = mh2rd
                if cg == 0:
                    mac = B.v(lambda v, pc=pc: v.tensor_copy(
                        fout[:, pc * 256:(pc + 1) * 256], pH2[:, 0:256]), wants=[mh2rd])
                else:
                    mac = B.v(lambda v, pc=pc: v.tensor_add(
                        fout[:, pc * 256:(pc + 1) * 256], fout[:, pc * 256:(pc + 1) * 256],
                        pH2[:, 0:256]), wants=[mh2rd, macc[pc]])
                macc[pc] = mac
                mac_prev = mac
        mtrp = [None, None]
        for pc in range(8):
            mev = B.v(lambda v, pc=pc, ts4=ts4: v.tensor_scalar(
                h2f[:, pc, ts4 * 256:(ts4 + 1) * 256], fout[:, pc * 256:(pc + 1) * 256],
                bdnc[0:128, pc:pc + 1], None, ALU.add), wants=[macc[pc]] + m_bias)
            for tp in range(2):
                tt8 = ts4 * 2 + tp
                mtr = B.tr(pT[:, 0:128],
                           h2f[:, pc, ts4 * 256 + tp * 128:ts4 * 256 + (tp + 1) * 128],
                           ident[:, :],
                           wants=[mev] + ([mtrp[0]] if mtrp[0] else []))
                mcp = B.v(lambda v, pc=pc, tt8=tt8: v.tensor_copy(
                    h2t[:, tt8, pc * 128:(pc + 1) * 128], pT[:, 0:128]), wants=[mtr])
                mtrp[0] = mcp
            mh2ev_all.append(mcp)
    m_front = R.marks()
    if int(os.environ.get("KPH", "99")) <= 4:
        for dbg in range(8):
            B.dma(out_A[dbg * 128:(dbg + 1) * 128, 0:1024], h2f[:, dbg, :],
                  wants=m_front, eng="gpsimd")
            B.dma(out_A[dbg * 128:(dbg + 1) * 128, 1024:2048], h2t[:, dbg, :],
                  wants=m_front, eng="gpsimd")
        es_ffn.close(); R.emit(blk); es.close(); return nc

    mqr_prev = None
    for rc in range(8):
        mwq = B.g(lambda g, rc=rc: g.dma_start(
            out=wA[:, 0:8, :],
            in_=WqrT_d[:, rc * 128:(rc + 1) * 128].rearrange("(a p) e -> p a e", p=128)),
            wants=m_front + ([mqr_prev] if mqr_prev else []), dma=True)
        for th in range(2):
            mmq = None
            for pc in range(8):
                mmq = B.mm(pQ[:, :], wA[:, pc, :], h2f[:, pc, th * 512:(th + 1) * 512],
                           pc == 0, pc == 7, wants=[mwq] + ([mqr_prev] if mqr_prev else []))
            mqr_prev = B.v(lambda v, rc=rc, th=th: v.tensor_copy(
                qrf[:, rc, th * 512:(th + 1) * 512], pQ[:, :]), wants=[mmq])
    m_qr = R.marks()

    IND = bass.IndirectOffsetOnAxis
    m_tail_prev = m_qr
    for tt8 in range(8):
        me2_all = []
        mPs_prev = None
        for xh in range(16):
            x_, h_ = xh // 8, xh % 8
            rc = x_ * 4 + h_ // 2
            off = (h_ % 2) * 64
            mPs = B.mm(pSim[:, 0:128], qrf[off:off + 64, rc, tt8 * 128:(tt8 + 1) * 128],
                       keyb[off:off + 64, xh, :], True, True,
                       wants=m_tail_prev + [mkey] + ([mPs_prev] if mPs_prev else []))
            me1 = B.v(lambda v, xh=xh: v.tensor_scalar(
                encb[:, xh, :], pSim[:, 0:128], 1024.0, MAGIC, ALU.mult, ALU.add),
                wants=[mPs])
            me2 = B.v(lambda v, xh=xh: v.scalar_tensor_tensor(
                encb[:, xh, :], encb[:, xh, :], -MAGIC, KM1[:, :], ALU.add, ALU.add),
                wants=[me1])
            mPs_prev = me1
            me2_all.append(me2)
        mmx = None
        for xh in range(16):
            mmx = B.v(lambda v, xh=xh: v.max(mx8[:, xh, :], encb[:, xh, :]),
                      wants=[me2_all[xh]])
        mR1 = B.v(lambda v: v.tensor_scalar(
            r8[:, :], mx8[:, :, :].rearrange("p a b -> p (a b)"), MAGIC, MAGIC,
            ALU.add, ALU.subtract), wants=[mmx])
        md8 = B.v(lambda v: v.tensor_sub(
            kid[:, :, :].rearrange("p a b -> p (a b)"),
            mx8[:, :, :].rearrange("p a b -> p (a b)"), r8[:, :]), wants=[mR1])
        mk8 = B.v(lambda v: v.tensor_scalar(
            kid[:, :, :].rearrange("p a b -> p (a b)"),
            kid[:, :, :].rearrange("p a b -> p (a b)"), -128.0, 63.5,
            ALU.mult, ALU.add), wants=[md8])
        ma64 = B.v(lambda v: v.tensor_tensor(
            a64[:, :],
            bass.AP(r8[:].tensor, r8[:].offset, [r8[:].ap[0], [8, 8], [1, 8], [0, 8]]),
            bass.AP(r8[:].tensor, r8[:, 64:65].offset, [r8[:].ap[0], [8, 8], [0, 8], [1, 8]]),
            ALU.add), wants=[mR1])
        me2b = B.v(lambda v: v.scalar_tensor_tensor(
            a64[:, :], a64[:, :], 0.0, KM2[:, :], ALU.add, ALU.add), wants=[ma64])
        msc = None
        for h_ in range(8):
            msc = B.v(lambda v, h_=h_: v.max(sc8[:, h_, :], a64[:, h_ * 64:(h_ + 1) * 64]),
                      wants=[me2b] + ([msc] if msc else []))
        mR2 = B.v(lambda v: v.tensor_scalar(
            scR[:, :], sc8[:, :, :].rearrange("p a b -> p (a b)"), MAGIC, MAGIC,
            ALU.add, ALU.subtract), wants=[msc])
        md2 = B.v(lambda v: v.tensor_sub(
            dec2[:, 0, :], sc8[:, :, :].rearrange("p a b -> p (a b)"), scR[:, :]),
            wants=[mR2])
        mm2_ = B.v(lambda v: v.tensor_scalar(
            dec2[:, 1, :], dec2[:, 0, :], -64.0, 31.5, ALU.mult, ALU.add), wants=[md2])
        mi2a = B.v(lambda v: v.tensor_scalar(
            dec2[:, 2, :], dec2[:, 1, :], 0.125, -0.4375, ALU.mult, ALU.add),
            wants=[mm2_])
        mi2 = B.v(lambda v: v.tensor_scalar(
            dec2[:, 2, :], dec2[:, 2, :], MAGIC, MAGIC, ALU.add, ALU.subtract),
            wants=[mi2a])
        mj2 = B.v(lambda v: v.scalar_tensor_tensor(
            dec2[:, 3, :], dec2[:, 2, :], -8.0, dec2[:, 1, :], ALU.mult, ALU.add),
            wants=[mi2])
        sel_marks = []
        for (sel_out, idx_m, base_off) in ((ixsel, 2, 0), (iysel, 3, 64)):
            mc_ = B.v(lambda v, idx_m=idx_m: v.tensor_tensor(
                scr2[:, 1536:2048],
                bass.AP(dec2[:].tensor, dec2[:, idx_m, :].offset,
                        [dec2[:].ap[0], [1, 64], [0, 8]]),
                IOTA8[:, :], ALU.is_equal), wants=[mi2, mj2])
            mu_ = B.v(lambda v, base_off=base_off: v.tensor_tensor(
                scr2[:, 1024:1536], scr2[:, 1536:2048],
                bass.AP(kid[:].tensor, kid[:].offset + base_off,
                        [kid[:].ap[0], [8, 8], [0, 8], [1, 8]]),
                ALU.mult), wants=[mc_, mk8])
            ms_ = B.v(lambda v, sel_out=sel_out: v.tensor_reduce(
                sel_out[:, :], scr2[:, 1024:1536].rearrange("p (a b) -> p a b", a=64),
                mybir.AxisListType.X, ALU.add), wants=[mu_])
            sel_marks.append(ms_)
        mef = B.v(lambda v: v.scalar_tensor_tensor(
            eidf[:, :], ixsel[:, :], 128.0, iysel[:, :], ALU.mult, ALU.add),
            wants=sel_marks)
        mei = B.s(lambda s: s.copy(eidi[:, :], eidf[:, :]), wants=[mef])
        mmh = B.v(lambda v: v.tensor_reduce(
            smax[:, :], scR[:, :].rearrange("p (a b) -> p a b", a=8),
            mybir.AxisListType.X, ALU.max), wants=[mR2])
        msb = B.v(lambda v: v.tensor_tensor(
            wsm[:, :], scR[:, :],
            bass.AP(smax[:].tensor, smax[:].offset, [smax[:].ap[0], [1, 8], [0, 8]]),
            ALU.subtract), wants=[mmh])
        msc_ = B.v(lambda v: v.tensor_scalar(
            wsm[:, :], wsm[:, :], 0.0009765625, None, ALU.mult), wants=[msb])
        mex = B.s(lambda s: s.activation(wsm[:, :], wsm[:, :], AF.Exp),
                  wants=[msc_])
        mse = B.v(lambda v: v.tensor_reduce(
            ssum[:, :], wsm[:, :].rearrange("p (a b) -> p a b", a=8),
            mybir.AxisListType.X, ALU.add), wants=[mex])
        mrs_l = []
        for ri in range(8):
            mrs_l.append(B.v(lambda v, ri=ri: v.reciprocal(
                smax[:, ri:ri + 1], ssum[:, ri:ri + 1]), wants=[mse]))
        mw_ = B.v(lambda v: v.tensor_tensor(
            wsm[:, :], wsm[:, :],
            bass.AP(smax[:].tensor, smax[:].offset, [smax[:].ap[0], [1, 8], [0, 8]]),
            ALU.mult), wants=mrs_l)
        mz_prev = [None, None]
        mzl = None
        for m in range(64):
            slot = m % 2
            mg_ = B.g(lambda g, m=m, slot=slot: g.indirect_dma_start(
                out=deg[:, slot, :], out_offset=None, in_=de_d,
                in_offset=IND(ap=eidi[:, m:m + 1], axis=0)),
                wants=[mei] + ([mz_prev[slot]] if mz_prev[slot] else []),
                dma=True)
            mzl = B.v(lambda v, m=m, slot=slot, tt8=tt8: v.scalar_tensor_tensor(
                fout[:, 0:1024], deg[:, slot, :], 1.0, h2t[:, tt8, :], ALU.mult,
                ALU.mult, accum_out=zbuf[:, 2 * m:2 * m + 1]), wants=[mg_])
            mz_prev[slot] = mzl
        mzw = B.v(lambda v: v.tensor_tensor(
            zt[:, :],
            bass.AP(zbuf[:].tensor, zbuf[:].offset, [zbuf[:].ap[0], [2, 64]]),
            wsm[:, :], ALU.mult), wants=[mzl, mw_])
        mzs = B.s(lambda s: s.activation(zt[:, :], zt[:, :], AF.Silu), wants=[mzw])
        mu_prev = [None, None]
        macc = [None, None]
        for m in range(64):
            slot = m % 2
            a_ = m % 2
            mgu = B.g(lambda g, m=m, slot=slot: g.indirect_dma_start(
                out=ueg[:, slot, :], out_offset=None, in_=ue_d,
                in_offset=IND(ap=eidi[:, m:m + 1], axis=0)),
                wants=[mei] + ([mu_prev[slot]] if mu_prev[slot] else []), dma=True)
            if m < 2:
                mac = B.v(lambda v, m=m, slot=slot, a_=a_: v.tensor_scalar(
                    acc4[:, a_, :], ueg[:, slot, :], zt[:, m:m + 1], None, ALU.mult),
                    wants=[mgu, mzs])
            else:
                mac = B.v(lambda v, m=m, slot=slot, a_=a_: v.scalar_tensor_tensor(
                    acc4[:, a_, :], ueg[:, slot, :], zt[:, m:m + 1], acc4[:, a_, :],
                    ALU.mult, ALU.add), wants=[mgu, macc[a_]])
            macc[a_] = mac
            mu_prev[slot] = mac
        mtf = B.v(lambda v, tt8=tt8: v.tensor_add(accF[:, tt8, :], acc4[:, 0, :],
                  acc4[:, 1, :]), wants=[macc[0], macc[1]])
        m_tail_prev = [mtf]
    m_tail = R.marks()
    if int(os.environ.get("KPH", "99")) <= 6:
        B.dma(out_A[0:128, 0:64], eidf[:, :], wants=m_tail, eng="gpsimd")
        B.dma(out_A[0:128, 64:192],
              bass.AP(zbuf[:].tensor, zbuf[:].offset, [zbuf[:].ap[0], [1, 128]]),
              wants=m_tail, eng="gpsimd")
        B.dma(out_A[0:128, 192:256], wsm[:, :], wants=m_tail, eng="gpsimd")
        B.dma(out_A[0:128, 256:320], scR[:, :], wants=m_tail, eng="gpsimd")
        B.dma(out_A[0:128, 320:384], zt[:, :], wants=m_tail, eng="gpsimd")
        B.dma(out_A[128:256, 0:2048], accF[:, 7, :], wants=m_tail, eng="gpsimd")
        B.dma(out_A[256:384, 0:128], r8[:, :], wants=m_tail, eng="gpsimd")
        B.dma(out_A[256:384, 128:256],
              kid[:, :, :].rearrange("p a b -> p (a b)"), wants=m_tail, eng="gpsimd")
        B.dma(out_A[256:384, 256:384],
              mx8[:, :, :].rearrange("p a b -> p (a b)"), wants=m_tail, eng="gpsimd")
        B.dma(out_A[256:384, 384:896], a64[:, :], wants=m_tail, eng="gpsimd")
        B.dma(out_A[384:512, 0:2048],
              encb[:, :, :].rearrange("p a b -> p (a b)"), wants=m_tail, eng="gpsimd")
        es_ffn.close(); R.emit(blk); es.close(); return nc

    for tt8 in range(8):
        mx_ = B.dma(sq[:, :], x_in[HX + tt8 * 128:HX + (tt8 + 1) * 128, :],
                    wants=m_tail if tt8 == 0 else [mo2_])
        mat = B.dma(scr2[:, :], attn_dr[tt8 * 128:(tt8 + 1) * 128, :],
                    wants=m_tail if tt8 == 0 else [mo2_])
        mf1 = B.v(lambda v, tt8=tt8: v.tensor_add(fout[:, :], scr2[:, :],
                  accF[:, tt8, :]), wants=[mat])
        mf2 = B.v(lambda v: v.tensor_add(fout[:, :], fout[:, :], sq[:, :]),
                  wants=[mf1, mx_])
        mo2_ = B.dma(out_A[tt8 * 128:(tt8 + 1) * 128, :], fout[:, :], wants=[mf2])

    es_ffn.close()
    R.emit(blk)
    es.close()
    return nc


def build_B_marker():
    pass


def build_B():
    Rec.PARANOID = bool(int(os.environ.get('KPAR', '0')))
    """Launch B: S_start = Horner-combine of predecessors' states (host passes
    them per-core in descending order, zero/one padded), then additive
    correction out = out_A + (gate * A_t * (S q_t)) @ Wo."""
    nc = bass.Bass("TRN2", target_bir_lowering=False, detect_race_conditions=False)
    es = ExitStack()
    B = Builder(nc, es)
    R = B.R

    S_in = nc.dram_tensor("S_in", [7, DK, H * DK], F32, kind="ExternalInput").ap()
    D_in = nc.dram_tensor("D_in", [1, 7 * H], F32, kind="ExternalInput").ap()
    beta_in = nc.dram_tensor("beta_in", [T, H], F32, kind="ExternalInput").ap()
    qfm_in = nc.dram_tensor("qfm_in", [D, T], BF16, kind="ExternalInput").ap()
    gtm_in = nc.dram_tensor("gtm_in", [T, D], BF16, kind="ExternalInput").ap()
    outA_in = nc.dram_tensor("outA_in", [T, D], F32, kind="ExternalInput").ap()
    Wo_d = nc.dram_tensor("Wo", [D, D], F32, kind="ExternalInput").ap()
    out_B = nc.dram_tensor("out_B", [T, D], F32, kind="ExternalOutput").ap()

    qft = B.sb("qft", [128, 16, T], BF16)
    gt = B.sb("gt", [128, NCH, D], BF16)
    ogfa = B.sb("ogfa", [128, 16, NCH * 128], BF16)
    fcor = B.sb("fcor", [128, NCH, D], BF16)
    Sacc = B.sb("Sacc", [128, H, DK], F32)
    SB2 = B.sb("SB2", [128, H, DK], F32)
    Saccb = B.sb("Saccb", [128, H, DK], BF16)
    Dbc = B.sb("Dbc", [128, 7, H], F32)
    beta_b = B.sb("beta_b", [128, NCH, H], F32)
    Ltm = B.sb("Ltm", [128, NCH, H], F32)
    Acol = B.sb("Acol", [128, NCH, H], F32)
    aend = B.sb("aend", [128, NCH, H], F32)
    triF = B.sb("triF", [128, 128], F32)
    sel127 = B.sb("sel127", [128, 128], F32)
    ident = B.sb("ident", [128, 128], BF16)
    onesF = B.sb("onesF", [1, 128], F32)
    small = B.sb("small", [128, 128], F32)
    woR = B.sb("woR", [128, 16, 256], BF16)
    ogst = B.sb("ogst", [128, 128], BF16)
    fout = B.sb("fout", [128, D], F32)

    pO = B.ps("pO", [128, 128], F32)
    pS = B.ps("pS", [128, 128], F32)
    pB = B.ps("pB", [128, 256], F32)
    pT = B.ps("pT", [128, 1024], BF16)

    blk = es.enter_context(nc.Block())

    m0 = B.g(lambda g: g.memset(triF[:, :], 1.0))
    B.g(lambda g: g.affine_select(out=triF[:, :], in_=triF[:, :], compare_op=ALU.is_ge,
        fill=0.0, base=0, pattern=[[1, 128]], channel_multiplier=-1), wants=[m0])
    m0 = B.g(lambda g: g.memset(sel127[:, :], 1.0))
    B.g(lambda g: g.affine_select(out=sel127[:, :], in_=sel127[:, :],
        compare_op=ALU.is_equal, fill=0.0, base=-127, pattern=[[0, 128]],
        channel_multiplier=1), wants=[m0])
    m0 = B.g(lambda g: g.memset(ident[:, :], 0.0))
    B.g(lambda g: g.affine_select(out=ident[:, :], in_=ident[:, :],
        compare_op=ALU.not_equal, fill=1.0, base=0, pattern=[[-1, 128]],
        channel_multiplier=1), wants=[m0])
    B.v(lambda v: v.memset(onesF[:, :], 1.0))
    m_consts = R.marks()

    mq = B.dma(qft[:, :, :], qfm_in.rearrange("(a p) t -> p a t", p=128))
    mg_ = B.dma(gt[:, :, :], gtm_in.rearrange("(a p) d -> p a d", a=NCH))
    mb = B.dma(beta_b[:, :, :], beta_in.rearrange("(a p) h -> p a h", a=NCH))
    mD = B.dma(small[0:1, 0:7 * H], D_in, wants=m_consts)
    mDb = B.mm(pB[:, 0:112], onesF[0:1, :], small[0:1, 0:112], True, True,
               wants=[mD] + m_consts)
    mDbc = B.v(lambda v: v.tensor_copy(Dbc[:, :, :].rearrange("p a h -> p (a h)"),
               pB[:, 0:112]), wants=[mDb])

    mprev = m_consts
    for tc in range(NCH):
        md = B.v(lambda v, tc=tc: v.tensor_scalar(small[:, 0:16], beta_b[:, tc, :],
                 -1.0, 1.0, ALU.mult, ALU.add), wants=[mb] + mprev)
        mln = B.s(lambda s: s.activation(small[:, 16:32], small[:, 0:16], AF.Ln),
                  wants=[md])
        mcs = B.mm(pO[:, 0:16], triF[:, :], small[:, 16:32], True, True, wants=[mln])
        mL = B.v(lambda v, tc=tc: v.tensor_copy(Ltm[:, tc, :], pO[:, 0:16]), wants=[mcs])
        mA = B.s(lambda s, tc=tc: s.activation(Acol[:, tc, :], Ltm[:, tc, :], AF.Exp),
                 wants=[mL])
        mLe = B.mm(pS[:, 0:16], sel127[:, :], Ltm[:, tc, :], True, True, wants=[mL])
        mae = B.s(lambda s, tc=tc: s.activation(aend[:, tc, :], pS[:, 0:16], AF.Exp),
                  wants=[mLe])
        mprev = [mA, mae, mcs, mLe]

    mS = B.dma(Sacc[:, :, :].rearrange("p h e -> p (h e)"), S_in[6])
    mldS = None
    for j in range(5, -1, -1):
        mldS = B.dma(SB2[:, :, :].rearrange("p h e -> p (h e)"), S_in[j],
                     wants=[mldS] if mldS else [])
        for h in range(H):
            mS = B.v(lambda v, j=j, h=h: v.scalar_tensor_tensor(
                Sacc[:, h, :], Sacc[:, h, :], Dbc[:, j, h:h + 1], SB2[:, h, :],
                ALU.mult, ALU.add), wants=[mldS, mS, mDbc])
    mSb0 = B.v(lambda v: v.tensor_copy(
        Saccb[:, :, :].rearrange("p h e -> p (h e)"),
        Sacc[:, :, :].rearrange("p h e -> p (h e)")), wants=[mS])
    m_state = R.marks()

    mog_prev = None
    mSu_prev = [None] * H
    mtrp = [None, None]
    for tc in range(NCH):
        for h in range(H):
            mo = B.mm(pO[:, 0:128], qft[:, h, tc * 128:(tc + 1) * 128], Saccb[:, h, :],
                      True, True,
                      wants=m_state + [mq] + ([mog_prev] if mog_prev else [])
                      + ([mSu_prev[h]] if mSu_prev[h] else []))
            mog = B.v(lambda v, tc=tc, h=h: v.scalar_tensor_tensor(
                ogst[:, :], pO[:, 0:128], Acol[:, tc, h:h + 1], gt[:, tc, h * 128:(h + 1) * 128],
                ALU.mult, ALU.mult), wants=[mo, mg_])
            mtr = B.tr(pT[:, 0:128], ogst[:, :], ident[:, :],
                       wants=[mog] + ([mtrp[0]] if mtrp[0] else []))
            mcp = B.v(lambda v, tc=tc, h=h: v.tensor_copy(
                ogfa[:, h, tc * 128:(tc + 1) * 128], pT[:, 0:128]), wants=[mtr])
            mtrp[0] = mcp
            mog_prev = mog
            mSu = B.v(lambda v, tc=tc, h=h: v.tensor_scalar(
                Sacc[:, h, :], Sacc[:, h, :], aend[:, tc, h:h + 1], None, ALU.mult),
                wants=[mo])
            mSu = B.v(lambda v, h=h: v.tensor_copy(Saccb[:, h, :], Sacc[:, h, :]),
                      wants=[mSu])
            mSu_prev[h] = mSu
    m_corr = R.marks()

    mwo_prev = None
    for dgq in range(8):
        mwo = B.g(lambda g, dgq=dgq: g.dma_start(
            out=woR[:, :, :],
            in_=Wo_d[:, dgq * 256:(dgq + 1) * 256].rearrange("(a p) e -> p a e", p=128)),
            wants=m_corr + ([mwo_prev] if mwo_prev else []), dma=True)
        mev_prev = None
        for tc in range(NCH):
            mo = None
            for vc in range(16):
                mo = B.mm(pB[:, 0:256], ogfa[:, vc, tc * 128:(tc + 1) * 128],
                          woR[:, vc, :], vc == 0, vc == 15,
                          wants=[mwo] + ([mev_prev] if mev_prev else []))
            mev_prev = B.v(lambda v, tc=tc, dgq=dgq: v.tensor_copy(
                fcor[:, tc, dgq * 256:(dgq + 1) * 256], pB[:, 0:256]), wants=[mo])
        mwo_prev = mo
    m_wo = R.marks()

    mo2_ = None
    for tc in range(NCH):
        mx_ = B.dma(fout[:, :], outA_in[tc * 128:(tc + 1) * 128, :],
                    wants=m_wo if tc == 0 else [mo2_])
        mf1 = B.v(lambda v, tc=tc: v.tensor_add(fout[:, :], fout[:, :], fcor[:, tc, :]),
                  wants=[mx_] + m_wo)
        mo2_ = B.dma(out_B[tc * 128:(tc + 1) * 128, :], fout[:, :], wants=[mf1])

    R.emit(blk)
    es.close()
    return nc


_NC_CACHE = {}


def _get_progs():
    if "A" not in _NC_CACHE:
        _NC_CACHE["A"] = build_A()
        _NC_CACHE["B"] = build_B()
    return _NC_CACHE["A"], _NC_CACHE["B"]


def kernel(**inp):
    f32 = np.float32
    x = np.asarray(inp["L0_attn_hidden_states"], f32).reshape(8192, D)
    Wq = np.ascontiguousarray(np.asarray(inp["L0_attn_Wq"], f32))
    Wk = np.ascontiguousarray(np.asarray(inp["L0_attn_Wk"], f32))
    Wv = np.ascontiguousarray(np.asarray(inp["L0_attn_Wv"], f32))
    Wb = np.ascontiguousarray(np.asarray(inp["L0_attn_Wb"], f32))
    bb = np.asarray(inp["L0_attn_bb"], f32).reshape(1, H)
    Wg = np.ascontiguousarray(np.asarray(inp["L0_attn_Wg"], f32))
    Wo = np.ascontiguousarray(np.asarray(inp["L0_attn_Wo"], f32))
    qcw = np.ascontiguousarray(np.asarray(inp["L0_attn_qconv_w"], f32))
    qcb = np.asarray(inp["L0_attn_qconv_b"], f32).reshape(1, D)
    kcw = np.ascontiguousarray(np.asarray(inp["L0_attn_kconv_w"], f32))
    kcb = np.asarray(inp["L0_attn_kconv_b"], f32).reshape(1, D)
    qnw = np.asarray(inp["L0_attn_qn_w"], f32).reshape(1, DK)
    qnb = np.asarray(inp["L0_attn_qn_b"], f32).reshape(1, DK)
    knw = np.asarray(inp["L0_attn_kn_w"], f32).reshape(1, DK)
    knb = np.asarray(inp["L0_attn_kn_b"], f32).reshape(1, DK)
    WupT = np.ascontiguousarray(np.asarray(inp["L0_ffn_W_up"], f32).T)
    bup = np.asarray(inp["L0_ffn_b_up"], f32).reshape(1, DC)
    WdnT = np.ascontiguousarray(np.asarray(inp["L0_ffn_W_down"], f32).T)
    bdn = np.asarray(inp["L0_ffn_b_down"], f32).reshape(1, DP)
    WqrT = np.ascontiguousarray(np.asarray(inp["L0_ffn_W_q"], f32).T)
    keys = np.asarray(inp["L0_ffn_keys"], f32)
    keysT = np.ascontiguousarray(np.transpose(keys, (2, 0, 3, 1)).reshape(16, 64, 128))
    de = np.ascontiguousarray(np.asarray(inp["L0_ffn_down_embed"], f32))
    ue = np.ascontiguousarray(np.asarray(inp["L0_ffn_up_embed"], f32))

    xp = np.zeros((HX + 8192, D), f32)
    xp[HX:] = x

    ncA, ncB = _get_progs()
    shared = dict(Wq=Wq, Wk=Wk, Wv=Wv, Wb=Wb, bb=bb, Wg=Wg, Wo=Wo,
                  qconv_w=qcw, qconv_b=qcb, kconv_w=kcw, kconv_b=kcb,
                  qn_w=qnw, qn_b=qnb, kn_w=knw, kn_b=knb,
                  WupT=WupT, b_up=bup, WdnT=WdnT, b_dn=bdn, WqrT=WqrT,
                  keysT=keysT, de_f32=de, ue_f32=ue)
    in_maps = []
    for i in range(NCORE):
        m = dict(shared)
        m["x_slice"] = np.ascontiguousarray(xp[T * i:T * i + HX + T])
        in_maps.append(m)
    resA = run_bass_kernel_spmd(ncA, in_maps, core_ids=list(range(NCORE)))
    rA = resA.results

    in_maps_b = []
    for i in range(NCORE):
        Ss = np.zeros((7, DK, H * DK), f32)
        Dd = np.ones((7, H), f32)
        for jj, j in enumerate(range(i - 1, -1, -1)):
            Ss[jj] = rA[j]["S_out"]
            Dd[jj] = rA[j]["D_out"].reshape(H)
        in_maps_b.append(dict(
            S_in=Ss, D_in=np.ascontiguousarray(Dd.reshape(1, 7 * H)),
            beta_in=rA[i]["beta_out"], qfm_in=rA[i]["qfm_out"],
            gtm_in=rA[i]["gtm_out"], outA_in=rA[i]["out_A"], Wo=Wo))
    resB = run_bass_kernel_spmd(ncB, in_maps_b, core_ids=list(range(NCORE)))
    out = np.concatenate([np.asarray(resB.results[i]["out_B"], f32)
                          for i in range(NCORE)], axis=0)
    return out.reshape(1, 8192, D).astype(np.float32)


# version 03
# speedup vs baseline: 1.0439x; 1.0439x over previous
import os
import numpy as np
import concourse.bass as bass
import concourse.mybir as mybir
from concourse.bass_utils import run_bass_kernel_spmd
from contextlib import ExitStack

F32 = mybir.dt.float32
BF16 = mybir.dt.bfloat16
I32 = mybir.dt.int32
AF = mybir.ActivationFunctionType
ALU = mybir.AluOpType

NCORE = 8
T = 1024
HX = 32
D = 2048
H = 16
DK = 128
NCH = 8
MAGIC = float(1.5 * 2 ** 23)
CE = 16384
DP = 1024
DC = 8192
EPS = 1e-5


class Rec:

    ENGS = ("sync", "gpsimd", "tensor", "vector", "scalar")
    SEMS = ("sync", "gpsimd", "tensor", "vector", "scalar", "gdma")

    def __init__(self, nc, es):
        self.nc = nc
        self.ops = {e: [] for e in self.ENGS}
        self.cnt = {s: 0 for s in self.SEMS}
        self.sem = {s: es.enter_context(nc.semaphore(f"sem_{s}")) for s in self.SEMS}

    PARANOID = False

    def rec(self, eng, fn, wants=(), dma=False):
        inc = 16 if dma else 1
        sk = "gdma" if (eng == "gpsimd" and dma) else eng
        if Rec.PARANOID:
            wants = list(wants) + self.marks()
        wl = [(s, v) for (s, v) in wants
              if v > 0 and not (s == eng and s in ("tensor", "vector", "scalar"))]
        self.ops[eng].append((fn, wl, inc, sk))
        self.cnt[sk] += inc
        return (sk, self.cnt[sk])

    def mark(self, eng):
        return (eng, self.cnt[eng])

    def marks(self):
        return [(s, self.cnt[s]) for s in self.SEMS]

    def emit(self, block):
        for eng_name in self.ENGS:
            ops = self.ops[eng_name]
            if not ops:
                continue
            sec = getattr(block, eng_name)

            def body(eng, eng_name=eng_name, ops=ops):
                seen = {s: 0 for s in self.SEMS}
                for fn, wants, inc, sk in ops:
                    for (pe, pv) in wants:
                        if seen[pe] < pv:
                            eng.wait_ge(self.sem[pe], pv)
                            seen[pe] = pv
                    instr = fn(eng)
                    instr.then_inc(self.sem[sk], inc)
                for sk2 in (eng_name, "gdma") if eng_name == "gpsimd" else (eng_name,):
                    if self.cnt[sk2]:
                        eng.wait_ge(self.sem[sk2], self.cnt[sk2])
            sec(body)


class Builder:

    def __init__(self, nc, es):
        self.nc = nc
        self.es = es
        self.R = Rec(nc, es)

    def dma(self, out, in_, eng="sync", wants=(), slow=False):
        return self.R.rec(eng, lambda g: g.dma_start(
            out=out, in_=in_, allow_slow_non_contiguous=slow), wants=wants, dma=True)

    def mm(self, out, lhsT, rhs, start, stop, wants=()):
        return self.R.rec("tensor",
                          lambda t: t.matmul(out, lhsT, rhs, start=start, stop=stop),
                          wants=wants)

    def tr(self, out, in_, ident, wants=()):
        return self.R.rec("tensor", lambda t: t.transpose(out, in_, ident),
                          wants=wants)

    def v(self, fn, wants=()):
        return self.R.rec("vector", fn, wants=wants)

    def s(self, fn, wants=()):
        return self.R.rec("scalar", fn, wants=wants)

    def g(self, fn, wants=(), dma=False):
        return self.R.rec("gpsimd", fn, wants=wants, dma=dma)

    def sb(self, name, shape, dt):
        return self.es.enter_context(self.nc.sbuf_tensor(name, shape, dt))

    def ps(self, name, shape, dt):
        return self.es.enter_context(self.nc.psum_tensor(name, shape, dt))


def build_A():
    Rec.PARANOID = bool(int(os.environ.get('KPAR', '0')))
    nc = bass.Bass("TRN2", target_bir_lowering=False, detect_race_conditions=False)
    es = ExitStack()
    B = Builder(nc, es)
    R = B.R

    def din(name, shape, dt=F32):
        return nc.dram_tensor(name, shape, dt, kind="ExternalInput").ap()

    x_in = din("x_slice", [HX + T, D])
    Wq_d = din("Wq", [D, D]); Wk_d = din("Wk", [D, D]); Wv_d = din("Wv", [D, D])
    Wg_d = din("Wg", [D, D]); Wo_d = din("Wo", [D, D])
    Wb_d = din("Wb", [D, H]); bb_d = din("bb", [1, H])
    qcw_d = din("qconv_w", [D, 4]); qcb_d = din("qconv_b", [1, D])
    kcw_d = din("kconv_w", [D, 4]); kcb_d = din("kconv_b", [1, D])
    qnw_d = din("qn_w", [1, DK]); qnb_d = din("qn_b", [1, DK])
    knw_d = din("kn_w", [1, DK]); knb_d = din("kn_b", [1, DK])
    WupT_d = din("WupT", [D, DC]); bup_d = din("b_up", [1, DC])
    WdnT_d = din("WdnT", [DC, DP]); bdn_d = din("b_dn", [1, DP])
    WqrT_d = din("WqrT", [DP, DP])
    keysT_d = din("keysT", [16, 64, 128])
    de_d = din("de_f32", [CE, DP])
    ue_d = din("ue_f32", [CE, D])

    out_A = nc.dram_tensor("out_A", [T, D], F32, kind="ExternalOutput").ap()
    S_out = nc.dram_tensor("S_out", [DK, H * DK], F32, kind="ExternalOutput").ap()
    D_out = nc.dram_tensor("D_out", [1, H], F32, kind="ExternalOutput").ap()
    beta_out = nc.dram_tensor("beta_out", [T, H], F32, kind="ExternalOutput").ap()
    qfm_out = nc.dram_tensor("qfm_out", [D, T], BF16, kind="ExternalOutput").ap()
    gtm_out = nc.dram_tensor("gtm_out", [T, D], BF16, kind="ExternalOutput").ap()

    WupT_bf = nc.dram_tensor("WupT_bf", [D, DC], BF16, kind="Internal").ap()
    WdnT_bf = nc.dram_tensor("WdnT_bf", [DC, DP], BF16, kind="Internal").ap()
    attn_dr = nc.dram_tensor("attn_dr", [T, D], BF16, kind="Internal").ap()

    TFX = T + HX
    TFQ = 512 + HX
    xn = B.sb("xn", [128, 16, TFX], BF16)
    sq = B.sb("sq", [128, D], F32)
    scr2 = B.sb("scr2", [128, D], BF16)
    cols = B.sb("cols", [128, 16, 8], F32)
    wA = B.sb("wA", [128, 16, 128], BF16)
    wB = B.sb("wB", [128, 1024], BF16)
    beta_b = B.sb("beta_b", [128, NCH, H], F32)
    Ltm = B.sb("Ltm", [128, NCH, H], F32)
    Acol = B.sb("Acol", [128, NCH, H], F32)
    bAc = B.sb("bAc", [128, NCH, H], F32)
    vsc = B.sb("vsc", [128, NCH, H], F32)
    aend = B.sb("aend", [128, NCH, H], F32)
    tri = B.sb("tri", [128, 128], BF16)
    triF = B.sb("triF", [128, 128], F32)
    sel127 = B.sb("sel127", [128, 128], F32)
    ident = B.sb("ident", [128, 128], BF16)
    onesb = B.sb("onesb", [128, 128], BF16)
    csel = B.sb("csel", [128, H, H], BF16)
    rsel = B.sb("rsel", [16, H, 128], BF16)
    ph_t = B.sb("ph_t", [128, 128], BF16)
    Ssb = B.sb("Ssb", [128, H, DK], F32)
    bbc = B.sb("bbc", [128, H], F32)
    qkc = B.sb("qkc", [128, 2, 16, 4], F32)
    qkb = B.sb("qkb", [128, 2, 16], F32)
    lnw = B.sb("lnw", [128, 4], F32)
    small = B.sb("small", [128, 64], F32)
    epsc = B.sb("epsc", [128, 1], F32)

    pT = B.ps("pT", [128, 1024], BF16)

    blk = es.enter_context(nc.Block())
    barrier_floor = {}

    def barrier():
        for e, v in R.marks():
            barrier_floor[e] = v
    orig_rec = R.rec
    def rec_with_floor(eng, fn, wants=(), dma=False):
        w = list(wants) + [(e, v) for e, v in barrier_floor.items()]
        return orig_rec(eng, fn, w, dma)
    R.rec = rec_with_floor
    B.g(lambda g: g.memset(epsc[:, :], EPS))
    m0 = B.g(lambda g: g.memset(tri[:, :], 1.0))
    B.g(lambda g: g.affine_select(out=tri[:, :], in_=tri[:, :], compare_op=ALU.is_ge,
        fill=0.0, base=0, pattern=[[1, 128]], channel_multiplier=-1), wants=[m0])
    m0 = B.g(lambda g: g.memset(triF[:, :], 1.0))
    B.g(lambda g: g.affine_select(out=triF[:, :], in_=triF[:, :], compare_op=ALU.is_ge,
        fill=0.0, base=0, pattern=[[1, 128]], channel_multiplier=-1), wants=[m0])
    m0 = B.g(lambda g: g.memset(sel127[:, :], 1.0))
    B.g(lambda g: g.affine_select(out=sel127[:, :], in_=sel127[:, :], compare_op=ALU.is_equal,
        fill=0.0, base=-127, pattern=[[0, 128]], channel_multiplier=1), wants=[m0])
    m0 = B.g(lambda g: g.memset(ident[:, :], 0.0))
    B.g(lambda g: g.affine_select(out=ident[:, :], in_=ident[:, :], compare_op=ALU.not_equal,
        fill=1.0, base=0, pattern=[[-1, 128]], channel_multiplier=1), wants=[m0])
    B.g(lambda g: g.memset(onesb[:, :], 1.0))
    for h in range(H):
        m0 = B.g(lambda g, h=h: g.memset(csel[:, h, :], 1.0))
        B.g(lambda g, h=h: g.affine_select(out=csel[:, h, :], in_=csel[:, h, :],
            compare_op=ALU.is_equal, fill=0.0, base=-h, pattern=[[1, H]],
            channel_multiplier=0), wants=[m0])
        m0 = B.g(lambda g, h=h: g.memset(rsel[:, h, :], 1.0))
        B.g(lambda g, h=h: g.affine_select(out=rsel[:, h, :], in_=rsel[:, h, :],
            compare_op=ALU.is_equal, fill=0.0, base=-h, pattern=[[0, 128]],
            channel_multiplier=1), wants=[m0])
    m_consts = R.mark("gpsimd")

    for ec in range(16):
        B.g(lambda g, ec=ec: g.dma_start(out=qkc[:, 0, ec, :], in_=qcw_d[ec * 128:(ec + 1) * 128, :]), dma=True)
        B.g(lambda g, ec=ec: g.dma_start(out=qkc[:, 1, ec, :], in_=kcw_d[ec * 128:(ec + 1) * 128, :]), dma=True)
    for _dst, _src in ((qkb[:, 0, :], qcb_d), (qkb[:, 1, :], kcb_d)):
        B.g(lambda g, _dst=_dst, _src=_src: g.dma_start(
            out=_dst, in_=_src.rearrange("o (a p) -> p (o a)", p=128),
            allow_slow_non_contiguous=True), dma=True)
    for _i, _src in ((0, qnw_d), (1, qnb_d), (2, knw_d), (3, knb_d)):
        B.g(lambda g, _i=_i, _src=_src: g.dma_start(
            out=lnw[:, _i:_i + 1], in_=_src.rearrange("o p -> p o"),
            allow_slow_non_contiguous=True), dma=True)
    m_params = R.mark("gpsimd")

    m_xn = []
    prev_mxn = None
    prev_msd = None
    for i in range(9):
        nrow = HX if i == 0 else 128
        r0 = 0 if i == 0 else HX + (i - 1) * 128
        f0 = r0
        mldx = B.dma(sq[0:nrow, :], x_in[r0:r0 + nrow, :],
                     wants=[prev_mxn] if prev_mxn else [])
        msum = B.v(lambda v, n=nrow, i=i: v.tensor_reduce(
            cols[0:n, i, 0:1], sq[0:n, :], mybir.AxisListType.X, ALU.add), wants=[mldx])
        msq_s = B.s(lambda s, n=nrow, i=i: s.activation(
            scr2[0:n, :], sq[0:n, :], AF.Square), wants=[mldx] + m_xn[-1:])
        msq = B.v(lambda v, n=nrow, i=i: v.tensor_reduce(
            cols[0:n, i, 1:2], scr2[0:n, :], mybir.AxisListType.X, ALU.add),
            wants=[msq_s])
        mm_ = B.v(lambda v, n=nrow, i=i: v.tensor_scalar(
            cols[0:n, i, 2:3], cols[0:n, i, 0:1], 1.0 / D, None, ALU.mult), wants=[msum])
        mex2 = B.v(lambda v, n=nrow, i=i: v.tensor_scalar(
            cols[0:n, i, 4:5], cols[0:n, i, 1:2], 1.0 / D, None, ALU.mult),
            wants=[msq] + ([prev_msd] if prev_msd else []))
        mvar = B.v(lambda v, n=nrow, i=i: v.scalar_tensor_tensor(
            cols[0:n, i, 7:8], cols[0:n, i, 2:3], cols[0:n, i, 2:3], cols[0:n, i, 4:5],
            ALU.mult, ALU.subtract), wants=[mex2, mm_])
        msd = B.s(lambda s, n=nrow, i=i: s.activation(
            cols[0:n, i, 5:6], cols[0:n, i, 7:8], AF.Sqrt, bias=epsc[0:n, 0:1],
            scale=-1.0), wants=[mvar, m_consts])
        mr = B.v(lambda v, n=nrow, i=i: v.reciprocal(cols[0:n, i, 6:7], cols[0:n, i, 5:6]), wants=[msd])
        mxn = B.v(lambda v, n=nrow, i=i: v.tensor_scalar(
            scr2[0:n, :], sq[0:n, :], cols[0:n, i, 2:3], cols[0:n, i, 6:7],
            ALU.subtract, ALU.mult), wants=[mr])
        prev_mxn = mxn
        prev_msd = msd
        for ec in range(16):
            mtr = B.tr(pT[:, 0:nrow],
                       scr2[0:nrow, ec * 128:(ec + 1) * 128], ident[0:nrow, 0:nrow],
                       wants=[mxn, m_consts] + m_xn[-1:])
            mcp = B.v(lambda v, ec=ec, f0=f0, n=nrow: v.tensor_copy(
                xn[:, ec, f0:f0 + n], pT[:, 0:n]), wants=[mtr])
            m_xn.append(mcp)
    m_xn_done = R.marks()
    if int(os.environ.get("KPH", "99")) <= 1:
        R.emit(blk); es.close(); return nc

    es_attn = ExitStack()
    def asb(name, shape, dt):
        return es_attn.enter_context(nc.sbuf_tensor(name, shape, dt))
    def aps(name, shape, dt):
        return es_attn.enter_context(nc.psum_tensor(name, shape, dt))

    qf = asb("qf", [128, 16, TFQ], BF16)
    kf = asb("kf", [128, 16, TFQ], BF16)
    ktm = asb("ktm", [128, 4, 2048], BF16)
    vtm = asb("vtm", [128, 4, 2048], BF16)
    gtm_b = asb("gtm_b", [128, 4, 2048], BF16)
    ogf = asb("ogf", [128, 16, 512], BF16)
    wbt = asb("wbt", [128, 16, H], BF16)
    onesF = asb("onesF", [1, 128], F32)
    cscr = asb("cscr", [128, TFQ], F32)
    ysc = asb("ysc", [128, 512], BF16)
    ogst = asb("ogst", [128, 128], BF16)
    woR = asb("woR", [128, 16, 256], BF16)
    ost = asb("ost", [128, 2, 256], BF16)
    Ssbb = asb("Ssbb", [128, H, DK], BF16)
    rowst = asb("rowst", [16, 4, 512], F32)
    rstb = asb("rstb", [16, 2, 512], BF16)

    pB1 = aps("pB1", [128, 512], F32)
    pB2 = aps("pB2", [128, 512], F32)
    pstS = aps("pstS", [16, 512], F32)
    pstQ = aps("pstQ", [16, 512], F32)
    pP = aps("pP", [128, 512], F32)
    pO = aps("pO", [128, 512], F32)
    pS = aps("pS", [128, 512], F32)

    B.g(lambda g: g.dma_start(out=wbt[:, :, :],
        in_=Wb_d.rearrange("(a p) h -> p a h", p=128)), dma=True)
    mbb = B.dma(small[0:1, 0:16], bb_d)
    B.v(lambda v: v.memset(onesF[:, :], 1.0))
    monesF = R.mark("vector")
    mbbmm = B.mm(pP[:, 0:16], onesF[0:1, :], small[0:1, 0:16], True, True,
                 wants=[mbb, monesF])
    mbbc = B.v(lambda v: v.tensor_copy(bbc[:, :], pP[:, 0:16]), wants=[mbbmm])
    m_wbt = R.mark("gpsimd")

    mprev_beta = None
    for tc in range(NCH):
        wl = [m_wbt, mbbc] + m_xn_done + R.marks()
        mmb = None
        for dc in range(16):
            mmb = B.mm(pP[:, 0:16], xn[:, dc, HX + tc * 128:HX + (tc + 1) * 128],
                       wbt[:, dc, :], dc == 0, dc == 15, wants=wl)
        ma1 = B.v(lambda v: v.tensor_add(small[:, 0:16], pP[:, 0:16], bbc[:, :]),
                  wants=[mmb])
        msg = B.s(lambda s, tc=tc: s.activation(beta_b[:, tc, :], small[:, 0:16],
                  AF.Sigmoid), wants=[ma1])
        md = B.v(lambda v, tc=tc: v.tensor_scalar(small[:, 16:32], beta_b[:, tc, :],
                 -1.0, 1.0, ALU.mult, ALU.add), wants=[msg])
        mln = B.s(lambda s: s.activation(small[:, 32:48], small[:, 16:32], AF.Ln),
                  wants=[md])
        mcs = B.mm(pO[:, 0:16], triF[:, :], small[:, 32:48], True, True, wants=[mln])
        mL = B.v(lambda v, tc=tc: v.tensor_copy(Ltm[:, tc, :], pO[:, 0:16]), wants=[mcs])
        mA = B.s(lambda s, tc=tc: s.activation(Acol[:, tc, :], Ltm[:, tc, :], AF.Exp),
                 wants=[mL])
        meN = B.s(lambda s, tc=tc: s.activation(small[:, 48:64], Ltm[:, tc, :],
                  AF.Exp, scale=-1.0), wants=[mL])
        mbA = B.v(lambda v, tc=tc: v.tensor_mul(bAc[:, tc, :], beta_b[:, tc, :],
                  small[:, 48:64]), wants=[meN, msg])
        mLe = B.mm(pS[:, 0:16], sel127[:, :], Ltm[:, tc, :], True, True, wants=[mL])
        mae = B.s(lambda s, tc=tc: s.activation(aend[:, tc, :], pS[:, 0:16], AF.Exp),
                  wants=[mLe])
        mvs = B.v(lambda v, tc=tc: v.tensor_mul(vsc[:, tc, :], aend[:, tc, :],
                  bAc[:, tc, :]), wants=[mae, mbA])
        mprev_beta = mvs
    mDs = B.v(lambda v: v.tensor_copy(small[:, 0:16], aend[:, 0, :]),
              wants=[mprev_beta] + R.marks())
    for tc in range(1, NCH):
        mDs = B.v(lambda v, tc=tc: v.tensor_mul(small[:, 0:16], small[:, 0:16],
                  aend[:, tc, :]), wants=[mDs])
    B.dma(D_out, small[0:1, 0:16], wants=[mDs])
    B.dma(beta_out.rearrange("(a p) h -> p a h", a=NCH), beta_b[:, :, :],
          wants=[mprev_beta])
    m_decay = R.marks()
    if int(os.environ.get("KPH", "99")) <= 2:
        es_attn.close(); R.emit(blk); es.close(); return nc

    mSsb0 = B.v(lambda v: v.memset(Ssb[:, :, :], 0.0))
    mSsb0 = B.v(lambda v: v.memset(Ssbb[:, :, :], 0.0), wants=[mSsb0])
    m_last_scan = [mSsb0]

    for hf in range(2):
        t0 = 512 * hf
        fs = t0
        for (dst, Wd) in ((qf, Wq_d), (kf, Wk_d)):
            for ec in range(16):
                mw = B.g(lambda g, Wd=Wd, ec=ec: g.dma_start(
                    out=wA[:, :, :],
                    in_=Wd[:, ec * 128:(ec + 1) * 128].rearrange("(a p) e -> p a e", p=128)),
                    wants=R.marks() if ec == 0 else [m_ev1, m_ev2], dma=True)
                mm1 = mm2 = None
                for dc in range(16):
                    mm1 = B.mm(pB1[:, :], wA[:, dc, :], xn[:, dc, fs:fs + 512],
                               dc == 0, dc == 15, wants=[mw])
                    mm2 = B.mm(pB2[:, 0:32], wA[:, dc, :], xn[:, dc, fs + 512:fs + 544],
                               dc == 0, dc == 15, wants=[mw])
                m_ev1 = B.v(lambda v, dst=dst, ec=ec: v.tensor_copy(
                    dst[:, ec, 0:512], pB1[:, :]), wants=[mm1])
                m_ev2 = B.v(lambda v, dst=dst, ec=ec: v.tensor_copy(
                    dst[:, ec, 512:544], pB2[:, 0:32]), wants=[mm2])
        m_proj_qk = R.marks()

        m_conv_last = m_proj_qk
        for cs, dst in ((0, qf), (1, kf)):
            for ec in range(16):
                mc = B.v(lambda v, cs=cs, dst=dst, ec=ec: v.tensor_scalar(
                    cscr[:, 3:TFQ], dst[:, ec, 0:TFQ - 3], qkc[:, cs, ec, 0:1], None,
                    ALU.mult), wants=[m_params] + m_conv_last)
                for j in (1, 2, 3):
                    mc = B.v(lambda v, cs=cs, dst=dst, ec=ec, j=j: v.scalar_tensor_tensor(
                        cscr[:, 3:TFQ], dst[:, ec, j:TFQ - 3 + j], qkc[:, cs, ec, j:j + 1],
                        cscr[:, 3:TFQ], ALU.mult, ALU.add), wants=[mc])
                msl = B.s(lambda s, cs=cs, dst=dst, ec=ec: s.activation(
                    dst[:, ec, HX - 3:TFQ], cscr[:, HX - 3:TFQ], AF.Silu,
                    bias=qkb[:, cs, ec:ec + 1], scale=1.0), wants=[mc])
                m_conv_last = [msl]
        m_conv = R.marks()

        for cs, dst in ((0, qf), (1, kf)):
            wcol = lnw[:, 2 * cs:2 * cs + 1]
            bcol = lnw[:, 2 * cs + 1:2 * cs + 2]
            mcs0 = R.marks()
            mq_prev = mcs0
            msum = msq = None
            for h in range(H):
                msq_s = B.s(lambda s, dst=dst, h=h: s.activation(
                    scr2[:, 0:512], dst[:, h, HX:TFQ], AF.Square),
                    wants=mq_prev + ([msq] if msq else []))
                msum = B.mm(pstS[:, :], csel[:, h, :], dst[:, h, HX:TFQ],
                            h == 0, h == 15, wants=mcs0)
                msq = B.mm(pstQ[:, :], csel[:, h, :], scr2[:, 0:512],
                           h == 0, h == 15, wants=[msq_s])
                mq_prev = []
            mm_ = B.v(lambda v: v.tensor_scalar(rowst[:, 0, :], pstS[:, :], 1.0 / DK,
                      None, ALU.mult), wants=[msum])
            me2 = B.v(lambda v: v.tensor_scalar(rowst[:, 1, :], pstQ[:, :], 1.0 / DK,
                      None, ALU.mult), wants=[msq])
            mm2_ = B.v(lambda v: v.tensor_mul(rowst[:, 2, :], rowst[:, 0, :],
                       rowst[:, 0, :]), wants=[mm_])
            mvr = B.v(lambda v: v.tensor_sub(rowst[:, 1, :], rowst[:, 1, :],
                      rowst[:, 2, :]), wants=[me2, mm2_])
            msd = B.s(lambda s: s.activation(rowst[:, 1, :], rowst[:, 1, :], AF.Sqrt,
                      bias=epsc[0:16, 0:1], scale=1.0), wants=[mvr])
            mr = B.v(lambda v: v.reciprocal(rowst[:, 2, :], rowst[:, 1, :]), wants=[msd])
            mrm = B.v(lambda v: v.tensor_mul(rowst[:, 3, :], rowst[:, 2, :],
                      rowst[:, 0, :]), wants=[mr])
            mrb = B.v(lambda v: v.tensor_copy(rstb[:, 0, :], rowst[:, 2, :]), wants=[mr])
            mrmb = B.v(lambda v: v.tensor_copy(rstb[:, 1, :], rowst[:, 3, :]), wants=[mrm])
            my1 = my2 = None
            for h in range(H):
                mB1 = B.mm(pB1[:, :], rsel[:, h, :], rstb[:, 0, :], True, True,
                           wants=[mrb] + ([my1] if my1 else []))
                mB2 = B.mm(pB2[:, :], rsel[:, h, :], rstb[:, 1, :], True, True,
                           wants=[mrmb] + ([my2] if my2 else []))
                my1 = B.v(lambda v, dst=dst, h=h: v.tensor_mul(
                    ysc[:, :], dst[:, h, HX:TFQ], pB1[:, :]), wants=[mB1])
                my2 = B.v(lambda v: v.tensor_sub(ysc[:, :], ysc[:, :], pB2[:, :]),
                          wants=[mB2, my1])
                my3 = B.v(lambda v, dst=dst, h=h, wcol=wcol, bcol=bcol: v.tensor_scalar(
                    dst[:, h, HX:TFQ], ysc[:, :], wcol, bcol, ALU.mult, ALU.add),
                    wants=[my2])
        m_ln = R.marks()

        B.dma(qfm_out[:, t0:t0 + 512].rearrange("(a p) t -> p a t", p=128),
              qf[:, :, HX:TFQ], wants=m_ln)
        mtr_prev = None
        for ec in range(16):
            for tc in range(4):
                mtr = B.tr(pT[:, 0:128],
                           kf[:, ec, HX + tc * 128:HX + (tc + 1) * 128], ident[:, :],
                           wants=m_ln + ([mtr_prev] if mtr_prev else []))
                mcp = B.v(lambda v, ec=ec, tc=tc: v.tensor_copy(
                    ktm[:, tc, ec * 128:(ec + 1) * 128], pT[:, 0:128]), wants=[mtr])
                mtr_prev = mcp
        m_ktm = R.marks()

        for pi, (Wd, tdst, act) in enumerate(((Wv_d, vtm, None), (Wg_d, gtm_b, AF.Silu))):
            for ec in range(16):
                mw = B.g(lambda g, Wd=Wd, ec=ec: g.dma_start(
                    out=wA[:, :, :],
                    in_=Wd[:, ec * 128:(ec + 1) * 128].rearrange("(a p) e -> p a e", p=128)),
                    wants=R.marks() if ec == 0 else [m_ev1], dma=True)
                mm1 = None
                for dc in range(16):
                    mm1 = B.mm(pB1[:, :], wA[:, dc, :], xn[:, dc, fs + HX:fs + HX + 512],
                               dc == 0, dc == 15, wants=[mw])
                if act is None:
                    m_ev1 = B.v(lambda v, ec=ec: v.tensor_copy(
                        ogf[:, ec, :], pB1[:, :]), wants=[mm1])
                else:
                    m_ev1 = B.s(lambda s, ec=ec: s.activation(
                        ogf[:, ec, :], pB1[:, :], AF.Silu), wants=[mm1])
            m_vg = R.marks()
            mtr_prev = None
            for ec in range(16):
                for tc in range(4):
                    mtr = B.tr(pT[:, 0:128],
                               ogf[:, ec, tc * 128:(tc + 1) * 128], ident[:, :],
                               wants=m_vg + ([mtr_prev] if mtr_prev else []))
                    mcp = B.v(lambda v, tdst=tdst, ec=ec, tc=tc: v.tensor_copy(
                        tdst[:, tc, ec * 128:(ec + 1) * 128], pT[:, 0:128]),
                        wants=[mtr])
                    mtr_prev = mcp
        m_vgt = R.marks()
        B.dma(gtm_out[t0:t0 + 512, :].rearrange("(a p) d -> p a d", a=4),
              gtm_b[:, :, :], wants=m_vgt)

        Sdep = m_last_scan if len(m_last_scan) == H else [m_last_scan[0]] * H
        for cc in range(4):
            c = 4 * hf + cc
            mc0 = R.marks()
            mvst = B.v(lambda v, cc=cc, c=c: v.tensor_tensor(
                scr2[:, :].rearrange("p (h e) -> p h e", h=H),
                vtm[:, cc, :].rearrange("p (h e) -> p h e", h=H),
                bass.AP(vsc[:].tensor, vsc[:, c, :].offset, [vsc[:].ap[0], [1, H], [0, 128]]),
                ALU.mult), wants=mc0)
            mPh_prev = None
            mog_prev = None
            mtrp = [None, None]
            mSu_prev = None
            for h in range(H):
                fq = HX + cc * 128
                mP = B.mm(pP[:, 0:128], kf[:, h, fq:fq + 128], qf[:, h, fq:fq + 128],
                          True, True, wants=mc0 + ([mPh_prev] if mPh_prev else []))
                mPh = B.v(lambda v, c=c, h=h: v.scalar_tensor_tensor(
                    ph_t[:, :], pP[:, 0:128], bAc[:, c, h:h + 1], tri[:, :],
                    ALU.mult, ALU.mult), wants=[mP])
                mo1 = B.mm(pO[:, 0:128], qf[:, h, fq:fq + 128], Ssbb[:, h, :],
                           True, False, wants=[Sdep[h]] + ([mog_prev] if mog_prev else []))
                mo2 = B.mm(pO[:, 0:128], ph_t[:, :], vtm[:, cc, h * 128:(h + 1) * 128],
                           False, True, wants=[mPh])
                mog = B.v(lambda v, c=c, cc=cc, h=h: v.scalar_tensor_tensor(
                    ogst[:, :], pO[:, 0:128], Acol[:, c, h:h + 1],
                    gtm_b[:, cc, h * 128:(h + 1) * 128], ALU.mult, ALU.mult),
                    wants=[mo2])
                mtr = B.tr(pT[:, 0:128], ogst[:, :], ident[:, :],
                           wants=[mog] + ([mtrp[0]] if mtrp[0] else []))
                mcpog = B.v(lambda v, h=h, cc=cc: v.tensor_copy(
                    ogf[:, h, cc * 128:(cc + 1) * 128], pT[:, 0:128]), wants=[mtr])
                mtrp[0] = mcpog
                mdS = B.mm(pS[:, 0:128], ktm[:, cc, h * 128:(h + 1) * 128],
                           scr2[:, h * 128:(h + 1) * 128], True, True,
                           wants=[mvst] + ([mSu_prev] if mSu_prev else []))
                mSu = B.v(lambda v, c=c, h=h: v.scalar_tensor_tensor(
                    Ssb[:, h, :], Ssb[:, h, :], aend[:, c, h:h + 1], pS[:, 0:128],
                    ALU.mult, ALU.add), wants=[mdS, mo1])
                mSu = B.v(lambda v, h=h: v.tensor_copy(Ssbb[:, h, :], Ssb[:, h, :]),
                          wants=[mSu])
                Sdep[h] = mSu
                mSu_prev = mSu
                mog_prev = mog
                mPh_prev = mo2
        m_last_scan = Sdep

        m_scanned = R.marks()
        mwo_prev = None
        for dgq in range(8):
            mwo = B.g(lambda g, dgq=dgq: g.dma_start(
                out=woR[:, :, :],
                in_=Wo_d[:, dgq * 256:(dgq + 1) * 256].rearrange("(a p) e -> p a e", p=128)),
                wants=m_scanned + ([mwo_prev] if mwo_prev else []), dma=True)
            mev_prev = None
            mst_prev = [None, None]
            for tt2 in range(4):
                mo = None
                for vc in range(16):
                    mo = B.mm(pB1[:, 0:256], ogf[:, vc, tt2 * 128:(tt2 + 1) * 128],
                              woR[:, vc, :], vc == 0, vc == 15,
                              wants=[mwo] + ([mev_prev] if mev_prev else []))
                slot = tt2 % 2
                mev = B.v(lambda v, slot=slot: v.tensor_copy(
                    ost[:, slot, :], pB1[:, 0:256]),
                    wants=[mo] + ([mst_prev[slot]] if mst_prev[slot] else []))
                mst = B.dma(attn_dr[t0 + tt2 * 128:t0 + (tt2 + 1) * 128,
                                    dgq * 256:(dgq + 1) * 256],
                            ost[:, slot, :], wants=[mev])
                mst_prev[slot] = mst
                mev_prev = mev
            mwo_prev = mo
    B.dma(S_out, Ssb[:, :, :].rearrange("p h e -> p (h e)"), wants=R.marks())
    m_attn_done = R.marks()
    if int(os.environ.get("KPH", "99")) <= 3:
        es_attn.close(); R.emit(blk); es.close(); return nc

    es_attn.close()
    barrier()

    es_ffn = ExitStack()
    def fsb(name, shape, dt):
        return es_ffn.enter_context(nc.sbuf_tensor(name, shape, dt))
    def fps(name, shape, dt):
        return es_ffn.enter_context(nc.psum_tensor(name, shape, dt))

    h2f = fsb("h2f", [128, 8, 1024], BF16)
    h2t = fsb("h2t", [128, 8, 1024], BF16)
    qrf = fsb("qrf", [128, 8, 1024], BF16)
    keyb = fsb("keyb", [128, 16, 128], BF16)
    encb = fsb("encb", [128, 16, 128], F32)
    a64 = fsb("a64", [128, 512], F32)
    mx8 = fsb("mx8", [128, 16, 8], F32)
    r8 = fsb("r8", [128, 128], F32)
    kid = fsb("kid", [128, 16, 8], F32)
    sc8 = fsb("sc8", [128, 8, 8], F32)
    scR = fsb("scR", [128, 64], F32)
    dec2 = fsb("dec2", [128, 4, 64], F32)
    ixsel = fsb("ixsel", [128, 64], F32)
    iysel = fsb("iysel", [128, 64], F32)
    eidf = fsb("eidf", [128, 64], F32)
    eidi = fsb("eidi", [128, 64], I32)
    wsm = fsb("wsm", [128, 64], F32)
    smax = fsb("smax", [128, 8], F32)
    ssum = fsb("ssum", [128, 8], F32)
    zbuf = fsb("zbuf", [128, 128], F32)
    zt = fsb("zt", [128, 64], F32)
    deg = fsb("deg", [128, 2, 1024], BF16)
    ueg = fsb("ueg", [128, 2, 2048], BF16)
    acc4 = fsb("acc4", [128, 2, 2048], BF16)
    accF = fsb("accF", [128, 8, 2048], BF16)
    fout = fsb("fout", [128, 2048], F32)
    KM1 = fsb("KM1", [128, 128], F32)
    KM2 = fsb("KM2", [128, 512], F32)
    IOTA8 = fsb("IOTA8", [128, 512], BF16)

    pH1 = fps("pH1", [128, 512], F32)
    pH2 = fps("pH2", [128, 2048], F32)
    pQ = fps("pQ", [128, 512], F32)
    pSim = fps("pSim", [128, 512], F32)

    mi1 = B.g(lambda g: g.iota(KM1[:, :], pattern=[[1, 128]], base=0, channel_multiplier=0, allow_small_or_imprecise_dtypes=True))
    B.v(lambda v: v.tensor_scalar(KM1[:, :], KM1[:, :], -0.0078125, 63.5 * 0.0078125,
        ALU.mult, ALU.add), wants=[mi1])
    mi2 = B.g(lambda g: g.iota(KM2[:, :].rearrange("p (a m) -> p a m", a=8),
                               pattern=[[0, 8], [1, 64]], base=0, channel_multiplier=0, allow_small_or_imprecise_dtypes=True))
    B.v(lambda v: v.tensor_scalar(KM2[:, :], KM2[:, :], -0.015625, 31.5 * 0.015625,
        ALU.mult, ALU.add), wants=[mi2])
    B.g(lambda g: g.iota(IOTA8[:, :].rearrange("p (a j) -> p a j", a=64),
                         pattern=[[0, 64], [1, 8]], base=0, channel_multiplier=0, allow_small_or_imprecise_dtypes=True))
    mkey = B.g(lambda g: g.dma_start(out=keyb[0:64, :, :],
               in_=keysT_d.rearrange("a n k -> n a k")), dma=True)
    mkey2 = B.g(lambda g: g.dma_start(out=keyb[64:128, :, :],
                in_=keysT_d.rearrange("a n k -> n a k")), dma=True)
    mkey = mkey2
    m_ffn_const = R.marks()

    conv_marks = []
    def convert(src_d, dst_d, nrows, width):
        nonlocal_marks = []
        prev = [None, None]
        for i in range(nrows // 128):
            for w0 in range(0, width, 2048):
                wlen = min(2048, width - w0)
                slot = (i * ((width + 2047) // 2048) + w0 // 2048) % 2
                buf = ueg[:, slot, :].rearrange("p x -> p x")
                mg_ = B.g(lambda g, i=i, w0=w0, wlen=wlen, slot=slot, src_d=src_d:
                          g.dma_start(out=ueg[:, slot, 0:wlen],
                                      in_=src_d[i * 128:(i + 1) * 128, w0:w0 + wlen]),
                          wants=[prev[slot]] if prev[slot] else [], dma=True)
                ms_ = B.dma(dst_d[i * 128:(i + 1) * 128, w0:w0 + wlen],
                            ueg[:, slot, 0:wlen], wants=[mg_])
                prev[slot] = ms_
        return prev
    cm = convert(WupT_d, WupT_bf, D, DC)
    conv_marks += [m for m in cm if m]
    cm = convert(WdnT_d, WdnT_bf, DC, DP)
    conv_marks += [m for m in cm if m]
    m_conv_done = R.marks()

    bupc = small
    bdnc = cols[:, 15, :]
    mbu = B.dma(bupc[:, 0:64], bup_d.rearrange("o (a p) -> p (o a)", p=128),
                wants=m_ffn_const, slow=True)
    mbd = B.dma(bdnc[0:128, 0:8], bdn_d.rearrange("o (a p) -> p (o a)", p=128),
                wants=m_ffn_const, slow=True)
    m_bias = R.marks()
    h1g = fsb("h1g", [128, 8, 256], BF16)
    wB2 = fsb("wB2", [128, 1024], BF16)
    mh2ev_all = []
    mh2rd_bybuf = {0: None, 1: None}
    for ts4 in range(4):
        f0 = HX + ts4 * 256
        mwup_prev = None
        mh1ev = None
        macc = [None] * 8
        mac_prev = None
        for cg in range(8):
            mh1evs = []
            mh2rd = None
            for ci in range(8):
                c = cg * 8 + ci
                mwup = B.dma(wA[:, :, :],
                             WupT_bf[:, c * 128:(c + 1) * 128].rearrange("(a p) e -> p a e", p=128),
                             wants=m_conv_done + ([mwup_prev] if mwup_prev else []))
                mm1 = None
                for dc in range(16):
                    mm1 = B.mm(pH1[:, 0:256], wA[:, dc, :], xn[:, dc, f0:f0 + 256],
                               dc == 0, dc == 15,
                               wants=[mwup] + ([mh1ev] if mh1ev else []))
                mh1ev = B.s(lambda s, c=c, ci=ci: s.activation(
                    h1g[:, ci, :], pH1[:, 0:256], AF.Silu, bias=bupc[:, c:c + 1], scale=1.0),
                    wants=[mm1] + m_bias + R.marks()[0:0])
                mh1evs.append(mh1ev)
                mwup_prev = mm1
            for pc in range(8):
                wBuf = wB if pc % 2 == 0 else wB2
                pb = pc % 2
                mh2rd = None
                for ci in range(8):
                    c = cg * 8 + ci
                    mwd = B.dma(wBuf[:, ci * 128:(ci + 1) * 128],
                                WdnT_bf[c * 128:(c + 1) * 128, pc * 128:(pc + 1) * 128],
                                wants=m_conv_done + ([mh2rd_bybuf[pb]] if mh2rd_bybuf[pb] else []))
                    mh2rd = B.mm(pH2[:, 0:256], wBuf[:, ci * 128:(ci + 1) * 128],
                                 h1g[:, ci, :], ci == 0, ci == 7,
                                 wants=[mh1evs[ci], mwd] + ([mac_prev] if mac_prev and ci == 0 else []))
                mh2rd_bybuf[pb] = mh2rd
                if cg == 0:
                    mac = B.v(lambda v, pc=pc: v.tensor_copy(
                        fout[:, pc * 256:(pc + 1) * 256], pH2[:, 0:256]), wants=[mh2rd])
                else:
                    mac = B.v(lambda v, pc=pc: v.tensor_add(
                        fout[:, pc * 256:(pc + 1) * 256], fout[:, pc * 256:(pc + 1) * 256],
                        pH2[:, 0:256]), wants=[mh2rd, macc[pc]])
                macc[pc] = mac
                mac_prev = mac
        mtrp = [None, None]
        for pc in range(8):
            mev = B.v(lambda v, pc=pc, ts4=ts4: v.tensor_scalar(
                h2f[:, pc, ts4 * 256:(ts4 + 1) * 256], fout[:, pc * 256:(pc + 1) * 256],
                bdnc[0:128, pc:pc + 1], None, ALU.add), wants=[macc[pc]] + m_bias)
            for tp in range(2):
                tt8 = ts4 * 2 + tp
                mtr = B.tr(pT[:, 0:128],
                           h2f[:, pc, ts4 * 256 + tp * 128:ts4 * 256 + (tp + 1) * 128],
                           ident[:, :],
                           wants=[mev] + ([mtrp[0]] if mtrp[0] else []))
                mcp = B.v(lambda v, pc=pc, tt8=tt8: v.tensor_copy(
                    h2t[:, tt8, pc * 128:(pc + 1) * 128], pT[:, 0:128]), wants=[mtr])
                mtrp[0] = mcp
            mh2ev_all.append(mcp)
    m_front = R.marks()
    if int(os.environ.get("KPH", "99")) <= 4:
        for dbg in range(8):
            B.dma(out_A[dbg * 128:(dbg + 1) * 128, 0:1024], h2f[:, dbg, :],
                  wants=m_front, eng="gpsimd")
            B.dma(out_A[dbg * 128:(dbg + 1) * 128, 1024:2048], h2t[:, dbg, :],
                  wants=m_front, eng="gpsimd")
        es_ffn.close(); R.emit(blk); es.close(); return nc

    mqr_prev = None
    for rc in range(8):
        mwq = B.g(lambda g, rc=rc: g.dma_start(
            out=wA[:, 0:8, :],
            in_=WqrT_d[:, rc * 128:(rc + 1) * 128].rearrange("(a p) e -> p a e", p=128)),
            wants=m_front + ([mqr_prev] if mqr_prev else []), dma=True)
        for th in range(2):
            mmq = None
            for pc in range(8):
                mmq = B.mm(pQ[:, :], wA[:, pc, :], h2f[:, pc, th * 512:(th + 1) * 512],
                           pc == 0, pc == 7, wants=[mwq] + ([mqr_prev] if mqr_prev else []))
            mqr_prev = B.v(lambda v, rc=rc, th=th: v.tensor_copy(
                qrf[:, rc, th * 512:(th + 1) * 512], pQ[:, :]), wants=[mmq])
    m_qr = R.marks()

    IND = bass.IndirectOffsetOnAxis
    m_tail_prev = m_qr
    for tt8 in range(8):
        me2_all = []
        mPs_prev = None
        for xh in range(16):
            x_, h_ = xh // 8, xh % 8
            rc = x_ * 4 + h_ // 2
            off = (h_ % 2) * 64
            mPs = B.mm(pSim[:, 0:128], qrf[off:off + 64, rc, tt8 * 128:(tt8 + 1) * 128],
                       keyb[off:off + 64, xh, :], True, True,
                       wants=m_tail_prev + [mkey] + ([mPs_prev] if mPs_prev else []))
            me1 = B.v(lambda v, xh=xh: v.tensor_scalar(
                encb[:, xh, :], pSim[:, 0:128], 1024.0, MAGIC, ALU.mult, ALU.add),
                wants=[mPs])
            me2 = B.v(lambda v, xh=xh: v.scalar_tensor_tensor(
                encb[:, xh, :], encb[:, xh, :], -MAGIC, KM1[:, :], ALU.add, ALU.add),
                wants=[me1])
            mPs_prev = me1
            me2_all.append(me2)
        mmx = None
        for xh in range(16):
            mmx = B.v(lambda v, xh=xh: v.max(mx8[:, xh, :], encb[:, xh, :]),
                      wants=[me2_all[xh]])
        mR1 = B.v(lambda v: v.tensor_scalar(
            r8[:, :], mx8[:, :, :].rearrange("p a b -> p (a b)"), MAGIC, MAGIC,
            ALU.add, ALU.subtract), wants=[mmx])
        md8 = B.v(lambda v: v.tensor_sub(
            kid[:, :, :].rearrange("p a b -> p (a b)"),
            mx8[:, :, :].rearrange("p a b -> p (a b)"), r8[:, :]), wants=[mR1])
        mk8 = B.v(lambda v: v.tensor_scalar(
            kid[:, :, :].rearrange("p a b -> p (a b)"),
            kid[:, :, :].rearrange("p a b -> p (a b)"), -128.0, 63.5,
            ALU.mult, ALU.add), wants=[md8])
        ma64 = B.v(lambda v: v.tensor_tensor(
            a64[:, :],
            bass.AP(r8[:].tensor, r8[:].offset, [r8[:].ap[0], [8, 8], [1, 8], [0, 8]]),
            bass.AP(r8[:].tensor, r8[:, 64:65].offset, [r8[:].ap[0], [8, 8], [0, 8], [1, 8]]),
            ALU.add), wants=[mR1])
        me2b = B.v(lambda v: v.scalar_tensor_tensor(
            a64[:, :], a64[:, :], 0.0, KM2[:, :], ALU.add, ALU.add), wants=[ma64])
        msc = None
        for h_ in range(8):
            msc = B.v(lambda v, h_=h_: v.max(sc8[:, h_, :], a64[:, h_ * 64:(h_ + 1) * 64]),
                      wants=[me2b] + ([msc] if msc else []))
        mR2 = B.v(lambda v: v.tensor_scalar(
            scR[:, :], sc8[:, :, :].rearrange("p a b -> p (a b)"), MAGIC, MAGIC,
            ALU.add, ALU.subtract), wants=[msc])
        md2 = B.v(lambda v: v.tensor_sub(
            dec2[:, 0, :], sc8[:, :, :].rearrange("p a b -> p (a b)"), scR[:, :]),
            wants=[mR2])
        mm2_ = B.v(lambda v: v.tensor_scalar(
            dec2[:, 1, :], dec2[:, 0, :], -64.0, 31.5, ALU.mult, ALU.add), wants=[md2])
        mi2a = B.v(lambda v: v.tensor_scalar(
            dec2[:, 2, :], dec2[:, 1, :], 0.125, -0.4375, ALU.mult, ALU.add),
            wants=[mm2_])
        mi2 = B.v(lambda v: v.tensor_scalar(
            dec2[:, 2, :], dec2[:, 2, :], MAGIC, MAGIC, ALU.add, ALU.subtract),
            wants=[mi2a])
        mj2 = B.v(lambda v: v.scalar_tensor_tensor(
            dec2[:, 3, :], dec2[:, 2, :], -8.0, dec2[:, 1, :], ALU.mult, ALU.add),
            wants=[mi2])
        sel_marks = []
        for (sel_out, idx_m, base_off) in ((ixsel, 2, 0), (iysel, 3, 64)):
            mc_ = B.v(lambda v, idx_m=idx_m: v.tensor_tensor(
                scr2[:, 1536:2048],
                bass.AP(dec2[:].tensor, dec2[:, idx_m, :].offset,
                        [dec2[:].ap[0], [1, 64], [0, 8]]),
                IOTA8[:, :], ALU.is_equal), wants=[mi2, mj2])
            mu_ = B.v(lambda v, base_off=base_off: v.tensor_tensor(
                scr2[:, 1024:1536], scr2[:, 1536:2048],
                bass.AP(kid[:].tensor, kid[:].offset + base_off,
                        [kid[:].ap[0], [8, 8], [0, 8], [1, 8]]),
                ALU.mult), wants=[mc_, mk8])
            ms_ = B.v(lambda v, sel_out=sel_out: v.tensor_reduce(
                sel_out[:, :], scr2[:, 1024:1536].rearrange("p (a b) -> p a b", a=64),
                mybir.AxisListType.X, ALU.add), wants=[mu_])
            sel_marks.append(ms_)
        mef = B.v(lambda v: v.scalar_tensor_tensor(
            eidf[:, :], ixsel[:, :], 128.0, iysel[:, :], ALU.mult, ALU.add),
            wants=sel_marks)
        mei = B.s(lambda s: s.copy(eidi[:, :], eidf[:, :]), wants=[mef])
        mmh = B.v(lambda v: v.tensor_reduce(
            smax[:, :], scR[:, :].rearrange("p (a b) -> p a b", a=8),
            mybir.AxisListType.X, ALU.max), wants=[mR2])
        msb = B.v(lambda v: v.tensor_tensor(
            wsm[:, :], scR[:, :],
            bass.AP(smax[:].tensor, smax[:].offset, [smax[:].ap[0], [1, 8], [0, 8]]),
            ALU.subtract), wants=[mmh])
        msc_ = B.v(lambda v: v.tensor_scalar(
            wsm[:, :], wsm[:, :], 0.0009765625, None, ALU.mult), wants=[msb])
        mex = B.s(lambda s: s.activation(wsm[:, :], wsm[:, :], AF.Exp),
                  wants=[msc_])
        mse = B.v(lambda v: v.tensor_reduce(
            ssum[:, :], wsm[:, :].rearrange("p (a b) -> p a b", a=8),
            mybir.AxisListType.X, ALU.add), wants=[mex])
        mrs_l = []
        for ri in range(8):
            mrs_l.append(B.v(lambda v, ri=ri: v.reciprocal(
                smax[:, ri:ri + 1], ssum[:, ri:ri + 1]), wants=[mse]))
        mw_ = B.v(lambda v: v.tensor_tensor(
            wsm[:, :], wsm[:, :],
            bass.AP(smax[:].tensor, smax[:].offset, [smax[:].ap[0], [1, 8], [0, 8]]),
            ALU.mult), wants=mrs_l)
        mz_prev = [None, None]
        mzl = None
        for m in range(64):
            slot = m % 2
            mg_ = B.g(lambda g, m=m, slot=slot: g.indirect_dma_start(
                out=deg[:, slot, :], out_offset=None, in_=de_d,
                in_offset=IND(ap=eidi[:, m:m + 1], axis=0)),
                wants=[mei] + ([mz_prev[slot]] if mz_prev[slot] else []),
                dma=True)
            mzl = B.v(lambda v, m=m, slot=slot, tt8=tt8: v.scalar_tensor_tensor(
                fout[:, 0:1024], deg[:, slot, :], 1.0, h2t[:, tt8, :], ALU.mult,
                ALU.mult, accum_out=zbuf[:, 2 * m:2 * m + 1]), wants=[mg_])
            mz_prev[slot] = mzl
        mzw = B.v(lambda v: v.tensor_tensor(
            zt[:, :],
            bass.AP(zbuf[:].tensor, zbuf[:].offset, [zbuf[:].ap[0], [2, 64]]),
            wsm[:, :], ALU.mult), wants=[mzl, mw_])
        mzs = B.s(lambda s: s.activation(zt[:, :], zt[:, :], AF.Silu), wants=[mzw])
        mu_prev = [None, None]
        macc = [None, None]
        for m in range(64):
            slot = m % 2
            a_ = m % 2
            mgu = B.g(lambda g, m=m, slot=slot: g.indirect_dma_start(
                out=ueg[:, slot, :], out_offset=None, in_=ue_d,
                in_offset=IND(ap=eidi[:, m:m + 1], axis=0)),
                wants=[mei] + ([mu_prev[slot]] if mu_prev[slot] else []), dma=True)
            if m < 2:
                mac = B.v(lambda v, m=m, slot=slot, a_=a_: v.tensor_scalar(
                    acc4[:, a_, :], ueg[:, slot, :], zt[:, m:m + 1], None, ALU.mult),
                    wants=[mgu, mzs])
            else:
                mac = B.v(lambda v, m=m, slot=slot, a_=a_: v.scalar_tensor_tensor(
                    acc4[:, a_, :], ueg[:, slot, :], zt[:, m:m + 1], acc4[:, a_, :],
                    ALU.mult, ALU.add), wants=[mgu, macc[a_]])
            macc[a_] = mac
            mu_prev[slot] = mac
        mtf = B.v(lambda v, tt8=tt8: v.tensor_add(accF[:, tt8, :], acc4[:, 0, :],
                  acc4[:, 1, :]), wants=[macc[0], macc[1]])
        m_tail_prev = [mtf]
    m_tail = R.marks()
    if int(os.environ.get("KPH", "99")) <= 6:
        B.dma(out_A[0:128, 0:64], eidf[:, :], wants=m_tail, eng="gpsimd")
        B.dma(out_A[0:128, 64:192],
              bass.AP(zbuf[:].tensor, zbuf[:].offset, [zbuf[:].ap[0], [1, 128]]),
              wants=m_tail, eng="gpsimd")
        B.dma(out_A[0:128, 192:256], wsm[:, :], wants=m_tail, eng="gpsimd")
        B.dma(out_A[0:128, 256:320], scR[:, :], wants=m_tail, eng="gpsimd")
        B.dma(out_A[0:128, 320:384], zt[:, :], wants=m_tail, eng="gpsimd")
        B.dma(out_A[128:256, 0:2048], accF[:, 7, :], wants=m_tail, eng="gpsimd")
        B.dma(out_A[256:384, 0:128], r8[:, :], wants=m_tail, eng="gpsimd")
        B.dma(out_A[256:384, 128:256],
              kid[:, :, :].rearrange("p a b -> p (a b)"), wants=m_tail, eng="gpsimd")
        B.dma(out_A[256:384, 256:384],
              mx8[:, :, :].rearrange("p a b -> p (a b)"), wants=m_tail, eng="gpsimd")
        B.dma(out_A[256:384, 384:896], a64[:, :], wants=m_tail, eng="gpsimd")
        B.dma(out_A[384:512, 0:2048],
              encb[:, :, :].rearrange("p a b -> p (a b)"), wants=m_tail, eng="gpsimd")
        es_ffn.close(); R.emit(blk); es.close(); return nc

    for tt8 in range(8):
        mx_ = B.dma(sq[:, :], x_in[HX + tt8 * 128:HX + (tt8 + 1) * 128, :],
                    wants=m_tail if tt8 == 0 else [mo2_])
        mat = B.dma(scr2[:, :], attn_dr[tt8 * 128:(tt8 + 1) * 128, :],
                    wants=m_tail if tt8 == 0 else [mo2_])
        mf1 = B.v(lambda v, tt8=tt8: v.tensor_add(fout[:, :], scr2[:, :],
                  accF[:, tt8, :]), wants=[mat])
        mf2 = B.v(lambda v: v.tensor_add(fout[:, :], fout[:, :], sq[:, :]),
                  wants=[mf1, mx_])
        mo2_ = B.dma(out_A[tt8 * 128:(tt8 + 1) * 128, :], fout[:, :], wants=[mf2])

    es_ffn.close()
    R.emit(blk)
    es.close()
    return nc


def build_B_marker():
    pass


def build_B():
    Rec.PARANOID = bool(int(os.environ.get('KPAR', '0')))
    """Launch B: S_start = Horner-combine of predecessors' states (host passes
    them per-core in descending order, zero/one padded), then additive
    correction out = out_A + (gate * A_t * (S q_t)) @ Wo."""
    nc = bass.Bass("TRN2", target_bir_lowering=False, detect_race_conditions=False)
    es = ExitStack()
    B = Builder(nc, es)
    R = B.R

    S_in = nc.dram_tensor("S_in", [7, DK, H * DK], F32, kind="ExternalInput").ap()
    D_in = nc.dram_tensor("D_in", [1, 7 * H], F32, kind="ExternalInput").ap()
    beta_in = nc.dram_tensor("beta_in", [T, H], F32, kind="ExternalInput").ap()
    qfm_in = nc.dram_tensor("qfm_in", [D, T], BF16, kind="ExternalInput").ap()
    gtm_in = nc.dram_tensor("gtm_in", [T, D], BF16, kind="ExternalInput").ap()
    outA_in = nc.dram_tensor("outA_in", [T, D], F32, kind="ExternalInput").ap()
    Wo_d = nc.dram_tensor("Wo", [D, D], F32, kind="ExternalInput").ap()
    out_B = nc.dram_tensor("out_B", [T, D], F32, kind="ExternalOutput").ap()

    qft = B.sb("qft", [128, 16, T], BF16)
    gt = B.sb("gt", [128, NCH, D], BF16)
    ogfa = B.sb("ogfa", [128, 16, NCH * 128], BF16)
    fcor = B.sb("fcor", [128, NCH, D], BF16)
    Sacc = B.sb("Sacc", [128, H, DK], F32)
    SB2 = B.sb("SB2", [128, H, DK], F32)
    Saccb = B.sb("Saccb", [128, H, DK], BF16)
    Dbc = B.sb("Dbc", [128, 7, H], F32)
    beta_b = B.sb("beta_b", [128, NCH, H], F32)
    Ltm = B.sb("Ltm", [128, NCH, H], F32)
    Acol = B.sb("Acol", [128, NCH, H], F32)
    aend = B.sb("aend", [128, NCH, H], F32)
    triF = B.sb("triF", [128, 128], F32)
    sel127 = B.sb("sel127", [128, 128], F32)
    ident = B.sb("ident", [128, 128], BF16)
    onesF = B.sb("onesF", [1, 128], F32)
    small = B.sb("small", [128, 128], F32)
    woR = B.sb("woR", [128, 16, 256], BF16)
    ogst = B.sb("ogst", [128, 128], BF16)
    fout = B.sb("fout", [128, D], F32)

    pO = B.ps("pO", [128, 128], F32)
    pS = B.ps("pS", [128, 128], F32)
    pB = B.ps("pB", [128, 256], F32)
    pT = B.ps("pT", [128, 1024], BF16)

    blk = es.enter_context(nc.Block())

    m0 = B.g(lambda g: g.memset(triF[:, :], 1.0))
    B.g(lambda g: g.affine_select(out=triF[:, :], in_=triF[:, :], compare_op=ALU.is_ge,
        fill=0.0, base=0, pattern=[[1, 128]], channel_multiplier=-1), wants=[m0])
    m0 = B.g(lambda g: g.memset(sel127[:, :], 1.0))
    B.g(lambda g: g.affine_select(out=sel127[:, :], in_=sel127[:, :],
        compare_op=ALU.is_equal, fill=0.0, base=-127, pattern=[[0, 128]],
        channel_multiplier=1), wants=[m0])
    m0 = B.g(lambda g: g.memset(ident[:, :], 0.0))
    B.g(lambda g: g.affine_select(out=ident[:, :], in_=ident[:, :],
        compare_op=ALU.not_equal, fill=1.0, base=0, pattern=[[-1, 128]],
        channel_multiplier=1), wants=[m0])
    B.v(lambda v: v.memset(onesF[:, :], 1.0))
    m_consts = R.marks()

    mq = B.dma(qft[:, :, :], qfm_in.rearrange("(a p) t -> p a t", p=128))
    mg_ = B.dma(gt[:, :, :], gtm_in.rearrange("(a p) d -> p a d", a=NCH))
    mb = B.dma(beta_b[:, :, :], beta_in.rearrange("(a p) h -> p a h", a=NCH))
    mD = B.dma(small[0:1, 0:7 * H], D_in, wants=m_consts)
    mDb = B.mm(pB[:, 0:112], onesF[0:1, :], small[0:1, 0:112], True, True,
               wants=[mD] + m_consts)
    mDbc = B.v(lambda v: v.tensor_copy(Dbc[:, :, :].rearrange("p a h -> p (a h)"),
               pB[:, 0:112]), wants=[mDb])

    mprev = m_consts
    for tc in range(NCH):
        md = B.v(lambda v, tc=tc: v.tensor_scalar(small[:, 0:16], beta_b[:, tc, :],
                 -1.0, 1.0, ALU.mult, ALU.add), wants=[mb] + mprev)
        mln = B.s(lambda s: s.activation(small[:, 16:32], small[:, 0:16], AF.Ln),
                  wants=[md])
        mcs = B.mm(pO[:, 0:16], triF[:, :], small[:, 16:32], True, True, wants=[mln])
        mL = B.v(lambda v, tc=tc: v.tensor_copy(Ltm[:, tc, :], pO[:, 0:16]), wants=[mcs])
        mA = B.s(lambda s, tc=tc: s.activation(Acol[:, tc, :], Ltm[:, tc, :], AF.Exp),
                 wants=[mL])
        mLe = B.mm(pS[:, 0:16], sel127[:, :], Ltm[:, tc, :], True, True, wants=[mL])
        mae = B.s(lambda s, tc=tc: s.activation(aend[:, tc, :], pS[:, 0:16], AF.Exp),
                  wants=[mLe])
        mprev = [mA, mae, mcs, mLe]

    mS = B.dma(Sacc[:, :, :].rearrange("p h e -> p (h e)"), S_in[6])
    mldS = None
    for j in range(5, -1, -1):
        mldS = B.dma(SB2[:, :, :].rearrange("p h e -> p (h e)"), S_in[j],
                     wants=[mldS] if mldS else [])
        for h in range(H):
            mS = B.v(lambda v, j=j, h=h: v.scalar_tensor_tensor(
                Sacc[:, h, :], Sacc[:, h, :], Dbc[:, j, h:h + 1], SB2[:, h, :],
                ALU.mult, ALU.add), wants=[mldS, mS, mDbc])
    mSb0 = B.v(lambda v: v.tensor_copy(
        Saccb[:, :, :].rearrange("p h e -> p (h e)"),
        Sacc[:, :, :].rearrange("p h e -> p (h e)")), wants=[mS])
    m_state = R.marks()

    mog_prev = None
    mSu_prev = [None] * H
    mtrp = [None, None]
    for tc in range(NCH):
        for h in range(H):
            mo = B.mm(pO[:, 0:128], qft[:, h, tc * 128:(tc + 1) * 128], Saccb[:, h, :],
                      True, True,
                      wants=m_state + [mq] + ([mog_prev] if mog_prev else [])
                      + ([mSu_prev[h]] if mSu_prev[h] else []))
            mog = B.v(lambda v, tc=tc, h=h: v.scalar_tensor_tensor(
                ogst[:, :], pO[:, 0:128], Acol[:, tc, h:h + 1], gt[:, tc, h * 128:(h + 1) * 128],
                ALU.mult, ALU.mult), wants=[mo, mg_])
            mtr = B.tr(pT[:, 0:128], ogst[:, :], ident[:, :],
                       wants=[mog] + ([mtrp[0]] if mtrp[0] else []))
            mcp = B.v(lambda v, tc=tc, h=h: v.tensor_copy(
                ogfa[:, h, tc * 128:(tc + 1) * 128], pT[:, 0:128]), wants=[mtr])
            mtrp[0] = mcp
            mog_prev = mog
            mSu = B.v(lambda v, tc=tc, h=h: v.tensor_scalar(
                Sacc[:, h, :], Sacc[:, h, :], aend[:, tc, h:h + 1], None, ALU.mult),
                wants=[mo])
            mSu = B.v(lambda v, h=h: v.tensor_copy(Saccb[:, h, :], Sacc[:, h, :]),
                      wants=[mSu])
            mSu_prev[h] = mSu
    m_corr = R.marks()

    mwo_prev = None
    for dgq in range(8):
        mwo = B.g(lambda g, dgq=dgq: g.dma_start(
            out=woR[:, :, :],
            in_=Wo_d[:, dgq * 256:(dgq + 1) * 256].rearrange("(a p) e -> p a e", p=128)),
            wants=m_corr + ([mwo_prev] if mwo_prev else []), dma=True)
        mev_prev = None
        for tc in range(NCH):
            mo = None
            for vc in range(16):
                mo = B.mm(pB[:, 0:256], ogfa[:, vc, tc * 128:(tc + 1) * 128],
                          woR[:, vc, :], vc == 0, vc == 15,
                          wants=[mwo] + ([mev_prev] if mev_prev else []))
            mev_prev = B.v(lambda v, tc=tc, dgq=dgq: v.tensor_copy(
                fcor[:, tc, dgq * 256:(dgq + 1) * 256], pB[:, 0:256]), wants=[mo])
        mwo_prev = mo
    m_wo = R.marks()

    mo2_ = None
    for tc in range(NCH):
        mx_ = B.dma(fout[:, :], outA_in[tc * 128:(tc + 1) * 128, :],
                    wants=m_wo if tc == 0 else [mo2_])
        mf1 = B.v(lambda v, tc=tc: v.tensor_add(fout[:, :], fout[:, :], fcor[:, tc, :]),
                  wants=[mx_] + m_wo)
        mo2_ = B.dma(out_B[tc * 128:(tc + 1) * 128, :], fout[:, :], wants=[mf1])

    R.emit(blk)
    es.close()
    return nc


_NC_CACHE = {}


def _get_progs():
    if "A" not in _NC_CACHE:
        _NC_CACHE["A"] = build_A()
        _NC_CACHE["B"] = build_B()
    return _NC_CACHE["A"], _NC_CACHE["B"]


def kernel(**inp):
    f32 = np.float32
    x = np.asarray(inp["L0_attn_hidden_states"], f32).reshape(8192, D)
    Wq = np.ascontiguousarray(np.asarray(inp["L0_attn_Wq"], f32))
    Wk = np.ascontiguousarray(np.asarray(inp["L0_attn_Wk"], f32))
    Wv = np.ascontiguousarray(np.asarray(inp["L0_attn_Wv"], f32))
    Wb = np.ascontiguousarray(np.asarray(inp["L0_attn_Wb"], f32))
    bb = np.asarray(inp["L0_attn_bb"], f32).reshape(1, H)
    Wg = np.ascontiguousarray(np.asarray(inp["L0_attn_Wg"], f32))
    Wo = np.ascontiguousarray(np.asarray(inp["L0_attn_Wo"], f32))
    qcw = np.ascontiguousarray(np.asarray(inp["L0_attn_qconv_w"], f32))
    qcb = np.asarray(inp["L0_attn_qconv_b"], f32).reshape(1, D)
    kcw = np.ascontiguousarray(np.asarray(inp["L0_attn_kconv_w"], f32))
    kcb = np.asarray(inp["L0_attn_kconv_b"], f32).reshape(1, D)
    qnw = np.asarray(inp["L0_attn_qn_w"], f32).reshape(1, DK)
    qnb = np.asarray(inp["L0_attn_qn_b"], f32).reshape(1, DK)
    knw = np.asarray(inp["L0_attn_kn_w"], f32).reshape(1, DK)
    knb = np.asarray(inp["L0_attn_kn_b"], f32).reshape(1, DK)
    WupT = np.ascontiguousarray(np.asarray(inp["L0_ffn_W_up"], f32).T)
    bup = np.asarray(inp["L0_ffn_b_up"], f32).reshape(1, DC)
    WdnT = np.ascontiguousarray(np.asarray(inp["L0_ffn_W_down"], f32).T)
    bdn = np.asarray(inp["L0_ffn_b_down"], f32).reshape(1, DP)
    WqrT = np.ascontiguousarray(np.asarray(inp["L0_ffn_W_q"], f32).T)
    keys = np.asarray(inp["L0_ffn_keys"], f32)
    keysT = np.ascontiguousarray(np.transpose(keys, (2, 0, 3, 1)).reshape(16, 64, 128))
    de = np.ascontiguousarray(np.asarray(inp["L0_ffn_down_embed"], f32))
    ue = np.ascontiguousarray(np.asarray(inp["L0_ffn_up_embed"], f32))

    xp = np.zeros((HX + 8192, D), f32)
    xp[HX:] = x

    ncA, ncB = _get_progs()
    shared = dict(Wq=Wq, Wk=Wk, Wv=Wv, Wb=Wb, bb=bb, Wg=Wg, Wo=Wo,
                  qconv_w=qcw, qconv_b=qcb, kconv_w=kcw, kconv_b=kcb,
                  qn_w=qnw, qn_b=qnb, kn_w=knw, kn_b=knb,
                  WupT=WupT, b_up=bup, WdnT=WdnT, b_dn=bdn, WqrT=WqrT,
                  keysT=keysT, de_f32=de, ue_f32=ue)
    in_maps = []
    for i in range(NCORE):
        m = dict(shared)
        m["x_slice"] = np.ascontiguousarray(xp[T * i:T * i + HX + T])
        in_maps.append(m)
    resA = run_bass_kernel_spmd(ncA, in_maps, core_ids=list(range(NCORE)))
    rA = resA.results

    in_maps_b = []
    for i in range(NCORE):
        Ss = np.zeros((7, DK, H * DK), f32)
        Dd = np.ones((7, H), f32)
        for jj, j in enumerate(range(i - 1, -1, -1)):
            Ss[jj] = rA[j]["S_out"]
            Dd[jj] = rA[j]["D_out"].reshape(H)
        in_maps_b.append(dict(
            S_in=Ss, D_in=np.ascontiguousarray(Dd.reshape(1, 7 * H)),
            beta_in=rA[i]["beta_out"], qfm_in=rA[i]["qfm_out"],
            gtm_in=rA[i]["gtm_out"], outA_in=rA[i]["out_A"], Wo=Wo))
    resB = run_bass_kernel_spmd(ncB, in_maps_b, core_ids=list(range(NCORE)))
    out = np.concatenate([np.asarray(resB.results[i]["out_B"], f32)
                          for i in range(NCORE)], axis=0)
    return out.reshape(1, 8192, D).astype(np.float32)


# version 04 (final)
# speedup vs baseline: 1.1253x; 1.0780x over previous
import os
import numpy as np
import concourse.bass as bass
import concourse.mybir as mybir
from concourse.bass_utils import run_bass_kernel_spmd
from contextlib import ExitStack

F32 = mybir.dt.float32
BF16 = mybir.dt.bfloat16
I32 = mybir.dt.int32
AF = mybir.ActivationFunctionType
ALU = mybir.AluOpType

NCORE = 8
T = 1024
HX = 32
D = 2048
H = 16
DK = 128
NCH = 8
MAGIC = float(1.5 * 2 ** 23)
CE = 16384
DP = 1024
DC = 8192
EPS = 1e-5


class Rec:

    ENGS = ("sync", "gpsimd", "tensor", "vector", "scalar")
    SEMS = ("sync", "gpsimd", "tensor", "vector", "scalar", "gdma")

    def __init__(self, nc, es):
        self.nc = nc
        self.ops = {e: [] for e in self.ENGS}
        self.cnt = {s: 0 for s in self.SEMS}
        self.sem = {s: es.enter_context(nc.semaphore(f"sem_{s}")) for s in self.SEMS}

    PARANOID = False

    def rec(self, eng, fn, wants=(), dma=False):
        inc = 16 if dma else 1
        sk = "gdma" if (eng == "gpsimd" and dma) else eng
        if Rec.PARANOID:
            wants = list(wants) + self.marks()
        wl = [(s, v) for (s, v) in wants
              if v > 0 and not (s == eng and s in ("tensor", "vector", "scalar"))]
        self.ops[eng].append((fn, wl, inc, sk))
        self.cnt[sk] += inc
        return (sk, self.cnt[sk])

    def mark(self, eng):
        return (eng, self.cnt[eng])

    def marks(self):
        return [(s, self.cnt[s]) for s in self.SEMS]

    def emit(self, block):
        for eng_name in self.ENGS:
            ops = self.ops[eng_name]
            if not ops:
                continue
            sec = getattr(block, eng_name)

            def body(eng, eng_name=eng_name, ops=ops):
                seen = {s: 0 for s in self.SEMS}
                for fn, wants, inc, sk in ops:
                    for (pe, pv) in wants:
                        if seen[pe] < pv:
                            eng.wait_ge(self.sem[pe], pv)
                            seen[pe] = pv
                    instr = fn(eng)
                    instr.then_inc(self.sem[sk], inc)
                for sk2 in (eng_name, "gdma") if eng_name == "gpsimd" else (eng_name,):
                    if self.cnt[sk2]:
                        eng.wait_ge(self.sem[sk2], self.cnt[sk2])
            sec(body)


class Builder:

    def __init__(self, nc, es):
        self.nc = nc
        self.es = es
        self.R = Rec(nc, es)

    def dma(self, out, in_, eng="sync", wants=(), slow=False):
        return self.R.rec(eng, lambda g: g.dma_start(
            out=out, in_=in_, allow_slow_non_contiguous=slow), wants=wants, dma=True)

    def mm(self, out, lhsT, rhs, start, stop, wants=()):
        return self.R.rec("tensor",
                          lambda t: t.matmul(out, lhsT, rhs, start=start, stop=stop),
                          wants=wants)

    def tr(self, out, in_, ident, wants=()):
        return self.R.rec("tensor", lambda t: t.transpose(out, in_, ident),
                          wants=wants)

    def v(self, fn, wants=()):
        return self.R.rec("vector", fn, wants=wants)

    def s(self, fn, wants=()):
        return self.R.rec("scalar", fn, wants=wants)

    def g(self, fn, wants=(), dma=False):
        return self.R.rec("gpsimd", fn, wants=wants, dma=dma)

    def sb(self, name, shape, dt):
        return self.es.enter_context(self.nc.sbuf_tensor(name, shape, dt))

    def ps(self, name, shape, dt):
        return self.es.enter_context(self.nc.psum_tensor(name, shape, dt))


def build_A():
    Rec.PARANOID = bool(int(os.environ.get('KPAR', '0')))
    nc = bass.Bass("TRN2", target_bir_lowering=False, detect_race_conditions=False)
    es = ExitStack()
    B = Builder(nc, es)
    R = B.R

    def din(name, shape, dt=F32):
        return nc.dram_tensor(name, shape, dt, kind="ExternalInput").ap()

    x_in = din("x_slice", [HX + T, D])
    Wq_d = din("Wq", [D, D]); Wk_d = din("Wk", [D, D]); Wv_d = din("Wv", [D, D])
    Wg_d = din("Wg", [D, D]); Wo_d = din("Wo", [D, D])
    Wb_d = din("Wb", [D, H]); bb_d = din("bb", [1, H])
    qcw_d = din("qconv_w", [D, 4]); qcb_d = din("qconv_b", [1, D])
    kcw_d = din("kconv_w", [D, 4]); kcb_d = din("kconv_b", [1, D])
    qnw_d = din("qn_w", [1, DK]); qnb_d = din("qn_b", [1, DK])
    knw_d = din("kn_w", [1, DK]); knb_d = din("kn_b", [1, DK])
    WupT_d = din("WupT", [D, DC]); bup_d = din("b_up", [1, DC])
    WdnT_d = din("WdnT", [DC, DP]); bdn_d = din("b_dn", [1, DP])
    WqrT_d = din("WqrT", [DP, DP])
    keysT_d = din("keysT", [16, 64, 128])
    de_d = din("de_f32", [CE, DP])
    ue_d = din("ue_f32", [CE, D])

    out_A = nc.dram_tensor("out_A", [T, D], F32, kind="ExternalOutput").ap()
    S_out = nc.dram_tensor("S_out", [DK, H * DK], F32, kind="ExternalOutput").ap()
    D_out = nc.dram_tensor("D_out", [1, H], F32, kind="ExternalOutput").ap()
    beta_out = nc.dram_tensor("beta_out", [T, H], F32, kind="ExternalOutput").ap()
    qfm_out = nc.dram_tensor("qfm_out", [D, T], BF16, kind="ExternalOutput").ap()
    gtm_out = nc.dram_tensor("gtm_out", [T, D], BF16, kind="ExternalOutput").ap()

    WupT_bf = nc.dram_tensor("WupT_bf", [D, DC], BF16, kind="Internal").ap()
    WdnT_bf = nc.dram_tensor("WdnT_bf", [DC, DP], BF16, kind="Internal").ap()
    attn_dr = nc.dram_tensor("attn_dr", [T, D], BF16, kind="Internal").ap()

    TFX = T + HX
    TFQ = 512 + HX
    xn = B.sb("xn", [128, 16, TFX], BF16)
    sq = B.sb("sq", [128, D], F32)
    scr2 = B.sb("scr2", [128, D], BF16)
    cols = B.sb("cols", [128, 16, 8], F32)
    wA = B.sb("wA", [128, 16, 128], BF16)
    wB = B.sb("wB", [128, 1024], BF16)
    beta_b = B.sb("beta_b", [128, NCH, H], F32)
    Ltm = B.sb("Ltm", [128, NCH, H], F32)
    Acol = B.sb("Acol", [128, NCH, H], F32)
    bAc = B.sb("bAc", [128, NCH, H], F32)
    vsc = B.sb("vsc", [128, NCH, H], F32)
    aend = B.sb("aend", [128, NCH, H], F32)
    tri = B.sb("tri", [128, 128], BF16)
    triF = B.sb("triF", [128, 128], F32)
    sel127 = B.sb("sel127", [128, 128], F32)
    ident = B.sb("ident", [128, 128], BF16)
    onesb = B.sb("onesb", [128, 128], BF16)
    csel = B.sb("csel", [128, H, H], BF16)
    rsel = B.sb("rsel", [16, H, 128], BF16)
    ph_t = B.sb("ph_t", [128, 128], BF16)
    Ssb = B.sb("Ssb", [128, H, DK], F32)
    bbc = B.sb("bbc", [128, H], F32)
    qkc = B.sb("qkc", [128, 2, 16, 4], F32)
    qkb = B.sb("qkb", [128, 2, 16], F32)
    lnw = B.sb("lnw", [128, 4], F32)
    small = B.sb("small", [128, 64], F32)
    epsc = B.sb("epsc", [128, 1], F32)

    pT = B.ps("pT", [128, 1024], BF16)

    blk = es.enter_context(nc.Block())
    barrier_floor = {}

    def barrier():
        for e, v in R.marks():
            barrier_floor[e] = v
    orig_rec = R.rec
    def rec_with_floor(eng, fn, wants=(), dma=False):
        w = list(wants) + [(e, v) for e, v in barrier_floor.items()]
        return orig_rec(eng, fn, w, dma)
    R.rec = rec_with_floor
    B.g(lambda g: g.memset(epsc[:, :], EPS))
    m0 = B.g(lambda g: g.memset(tri[:, :], 1.0))
    B.g(lambda g: g.affine_select(out=tri[:, :], in_=tri[:, :], compare_op=ALU.is_ge,
        fill=0.0, base=0, pattern=[[1, 128]], channel_multiplier=-1), wants=[m0])
    m0 = B.g(lambda g: g.memset(triF[:, :], 1.0))
    B.g(lambda g: g.affine_select(out=triF[:, :], in_=triF[:, :], compare_op=ALU.is_ge,
        fill=0.0, base=0, pattern=[[1, 128]], channel_multiplier=-1), wants=[m0])
    m0 = B.g(lambda g: g.memset(sel127[:, :], 1.0))
    B.g(lambda g: g.affine_select(out=sel127[:, :], in_=sel127[:, :], compare_op=ALU.is_equal,
        fill=0.0, base=-127, pattern=[[0, 128]], channel_multiplier=1), wants=[m0])
    m0 = B.g(lambda g: g.memset(ident[:, :], 0.0))
    B.g(lambda g: g.affine_select(out=ident[:, :], in_=ident[:, :], compare_op=ALU.not_equal,
        fill=1.0, base=0, pattern=[[-1, 128]], channel_multiplier=1), wants=[m0])
    B.g(lambda g: g.memset(onesb[:, :], 1.0))
    for h in range(H):
        m0 = B.g(lambda g, h=h: g.memset(csel[:, h, :], 1.0))
        B.g(lambda g, h=h: g.affine_select(out=csel[:, h, :], in_=csel[:, h, :],
            compare_op=ALU.is_equal, fill=0.0, base=-h, pattern=[[1, H]],
            channel_multiplier=0), wants=[m0])
        m0 = B.g(lambda g, h=h: g.memset(rsel[:, h, :], 1.0))
        B.g(lambda g, h=h: g.affine_select(out=rsel[:, h, :], in_=rsel[:, h, :],
            compare_op=ALU.is_equal, fill=0.0, base=-h, pattern=[[0, 128]],
            channel_multiplier=1), wants=[m0])
    m_consts = R.mark("gpsimd")

    for ec in range(16):
        B.g(lambda g, ec=ec: g.dma_start(out=qkc[:, 0, ec, :], in_=qcw_d[ec * 128:(ec + 1) * 128, :]), dma=True)
        B.g(lambda g, ec=ec: g.dma_start(out=qkc[:, 1, ec, :], in_=kcw_d[ec * 128:(ec + 1) * 128, :]), dma=True)
    for _dst, _src in ((qkb[:, 0, :], qcb_d), (qkb[:, 1, :], kcb_d)):
        B.g(lambda g, _dst=_dst, _src=_src: g.dma_start(
            out=_dst, in_=_src.rearrange("o (a p) -> p (o a)", p=128),
            allow_slow_non_contiguous=True), dma=True)
    for _i, _src in ((0, qnw_d), (1, qnb_d), (2, knw_d), (3, knb_d)):
        B.g(lambda g, _i=_i, _src=_src: g.dma_start(
            out=lnw[:, _i:_i + 1], in_=_src.rearrange("o p -> p o"),
            allow_slow_non_contiguous=True), dma=True)
    m_params = R.mark("gpsimd")

    m_xn = []
    prev_mxn = None
    prev_msd = None
    for i in range(9):
        nrow = HX if i == 0 else 128
        r0 = 0 if i == 0 else HX + (i - 1) * 128
        f0 = r0
        mldx = B.dma(sq[0:nrow, :], x_in[r0:r0 + nrow, :],
                     wants=[prev_mxn] if prev_mxn else [])
        msum = B.v(lambda v, n=nrow, i=i: v.tensor_reduce(
            cols[0:n, i, 0:1], sq[0:n, :], mybir.AxisListType.X, ALU.add), wants=[mldx])
        msq_s = B.s(lambda s, n=nrow, i=i: s.activation(
            scr2[0:n, :], sq[0:n, :], AF.Square), wants=[mldx] + m_xn[-1:])
        msq = B.v(lambda v, n=nrow, i=i: v.tensor_reduce(
            cols[0:n, i, 1:2], scr2[0:n, :], mybir.AxisListType.X, ALU.add),
            wants=[msq_s])
        mm_ = B.v(lambda v, n=nrow, i=i: v.tensor_scalar(
            cols[0:n, i, 2:3], cols[0:n, i, 0:1], 1.0 / D, None, ALU.mult), wants=[msum])
        mex2 = B.v(lambda v, n=nrow, i=i: v.tensor_scalar(
            cols[0:n, i, 4:5], cols[0:n, i, 1:2], 1.0 / D, None, ALU.mult),
            wants=[msq] + ([prev_msd] if prev_msd else []))
        mvar = B.v(lambda v, n=nrow, i=i: v.scalar_tensor_tensor(
            cols[0:n, i, 7:8], cols[0:n, i, 2:3], cols[0:n, i, 2:3], cols[0:n, i, 4:5],
            ALU.mult, ALU.subtract), wants=[mex2, mm_])
        msd = B.s(lambda s, n=nrow, i=i: s.activation(
            cols[0:n, i, 5:6], cols[0:n, i, 7:8], AF.Sqrt, bias=epsc[0:n, 0:1],
            scale=-1.0), wants=[mvar, m_consts])
        mr = B.v(lambda v, n=nrow, i=i: v.reciprocal(cols[0:n, i, 6:7], cols[0:n, i, 5:6]), wants=[msd])
        mxn = B.v(lambda v, n=nrow, i=i: v.tensor_scalar(
            scr2[0:n, :], sq[0:n, :], cols[0:n, i, 2:3], cols[0:n, i, 6:7],
            ALU.subtract, ALU.mult), wants=[mr])
        prev_mxn = mxn
        prev_msd = msd
        for ec in range(16):
            mtr = B.tr(pT[:, 0:nrow],
                       scr2[0:nrow, ec * 128:(ec + 1) * 128], ident[0:nrow, 0:nrow],
                       wants=[mxn, m_consts] + m_xn[-1:])
            mcp = B.v(lambda v, ec=ec, f0=f0, n=nrow: v.tensor_copy(
                xn[:, ec, f0:f0 + n], pT[:, 0:n]), wants=[mtr])
            m_xn.append(mcp)
    m_xn_done = R.marks()
    if int(os.environ.get("KPH", "99")) <= 1:
        R.emit(blk); es.close(); return nc

    es_attn = ExitStack()
    def asb(name, shape, dt):
        return es_attn.enter_context(nc.sbuf_tensor(name, shape, dt))
    def aps(name, shape, dt):
        return es_attn.enter_context(nc.psum_tensor(name, shape, dt))

    qf = asb("qf", [128, 16, TFQ], BF16)
    kf = asb("kf", [128, 16, TFQ], BF16)
    ktm = asb("ktm", [128, 4, 2048], BF16)
    vtm = asb("vtm", [128, 4, 2048], BF16)
    gtm_b = asb("gtm_b", [128, 4, 2048], BF16)
    ogf = asb("ogf", [128, 16, 512], BF16)
    wbt = asb("wbt", [128, 16, H], BF16)
    onesF = asb("onesF", [1, 128], F32)
    cscr = asb("cscr", [128, TFQ], F32)
    ysc = asb("ysc", [128, 512], BF16)
    ogst = asb("ogst", [128, 128], BF16)
    woR = asb("woR", [128, 16, 256], BF16)
    ost = asb("ost", [128, 2, 256], BF16)
    Ssbb = asb("Ssbb", [128, H, DK], BF16)
    rowst = asb("rowst", [16, 4, 512], F32)
    rstb = asb("rstb", [16, 2, 512], BF16)

    pB1 = aps("pB1", [128, 512], F32)
    pB2 = aps("pB2", [128, 512], F32)
    pstS = aps("pstS", [16, 512], F32)
    pstQ = aps("pstQ", [16, 512], F32)
    pP = aps("pP", [128, 512], F32)
    pO = aps("pO", [128, 512], F32)
    pS = aps("pS", [128, 512], F32)

    B.g(lambda g: g.dma_start(out=wbt[:, :, :],
        in_=Wb_d.rearrange("(a p) h -> p a h", p=128)), dma=True)
    mbb = B.dma(small[0:1, 0:16], bb_d)
    B.v(lambda v: v.memset(onesF[:, :], 1.0))
    monesF = R.mark("vector")
    mbbmm = B.mm(pP[:, 0:16], onesF[0:1, :], small[0:1, 0:16], True, True,
                 wants=[mbb, monesF])
    mbbc = B.v(lambda v: v.tensor_copy(bbc[:, :], pP[:, 0:16]), wants=[mbbmm])
    m_wbt = R.mark("gpsimd")

    mprev_beta = None
    for tc in range(NCH):
        wl = [m_wbt, mbbc] + m_xn_done + R.marks()
        mmb = None
        for dc in range(16):
            mmb = B.mm(pP[:, 0:16], xn[:, dc, HX + tc * 128:HX + (tc + 1) * 128],
                       wbt[:, dc, :], dc == 0, dc == 15, wants=wl)
        ma1 = B.v(lambda v: v.tensor_add(small[:, 0:16], pP[:, 0:16], bbc[:, :]),
                  wants=[mmb])
        msg = B.s(lambda s, tc=tc: s.activation(beta_b[:, tc, :], small[:, 0:16],
                  AF.Sigmoid), wants=[ma1])
        md = B.v(lambda v, tc=tc: v.tensor_scalar(small[:, 16:32], beta_b[:, tc, :],
                 -1.0, 1.0, ALU.mult, ALU.add), wants=[msg])
        mln = B.s(lambda s: s.activation(small[:, 32:48], small[:, 16:32], AF.Ln),
                  wants=[md])
        mcs = B.mm(pO[:, 0:16], triF[:, :], small[:, 32:48], True, True, wants=[mln])
        mL = B.v(lambda v, tc=tc: v.tensor_copy(Ltm[:, tc, :], pO[:, 0:16]), wants=[mcs])
        mA = B.s(lambda s, tc=tc: s.activation(Acol[:, tc, :], Ltm[:, tc, :], AF.Exp),
                 wants=[mL])
        meN = B.s(lambda s, tc=tc: s.activation(small[:, 48:64], Ltm[:, tc, :],
                  AF.Exp, scale=-1.0), wants=[mL])
        mbA = B.v(lambda v, tc=tc: v.tensor_mul(bAc[:, tc, :], beta_b[:, tc, :],
                  small[:, 48:64]), wants=[meN, msg])
        mLe = B.mm(pS[:, 0:16], sel127[:, :], Ltm[:, tc, :], True, True, wants=[mL])
        mae = B.s(lambda s, tc=tc: s.activation(aend[:, tc, :], pS[:, 0:16], AF.Exp),
                  wants=[mLe])
        mvs = B.v(lambda v, tc=tc: v.tensor_mul(vsc[:, tc, :], aend[:, tc, :],
                  bAc[:, tc, :]), wants=[mae, mbA])
        mprev_beta = mvs
    mDs = B.v(lambda v: v.tensor_copy(small[:, 0:16], aend[:, 0, :]),
              wants=[mprev_beta] + R.marks())
    for tc in range(1, NCH):
        mDs = B.v(lambda v, tc=tc: v.tensor_mul(small[:, 0:16], small[:, 0:16],
                  aend[:, tc, :]), wants=[mDs])
    B.dma(D_out, small[0:1, 0:16], wants=[mDs])
    B.dma(beta_out.rearrange("(a p) h -> p a h", a=NCH), beta_b[:, :, :],
          wants=[mprev_beta])
    m_decay = R.marks()
    if int(os.environ.get("KPH", "99")) <= 2:
        es_attn.close(); R.emit(blk); es.close(); return nc

    mSsb0 = B.v(lambda v: v.memset(Ssb[:, :, :], 0.0))
    mSsb0 = B.v(lambda v: v.memset(Ssbb[:, :, :], 0.0), wants=[mSsb0])
    m_last_scan = [mSsb0]

    for hf in range(2):
        t0 = 512 * hf
        fs = t0
        for (dst, Wd) in ((qf, Wq_d), (kf, Wk_d)):
            for ec in range(16):
                mw = B.g(lambda g, Wd=Wd, ec=ec: g.dma_start(
                    out=wA[:, :, :],
                    in_=Wd[:, ec * 128:(ec + 1) * 128].rearrange("(a p) e -> p a e", p=128)),
                    wants=R.marks() if ec == 0 else [m_ev1, m_ev2], dma=True)
                mm1 = mm2 = None
                for dc in range(16):
                    mm1 = B.mm(pB1[:, :], wA[:, dc, :], xn[:, dc, fs:fs + 512],
                               dc == 0, dc == 15, wants=[mw])
                    mm2 = B.mm(pB2[:, 0:32], wA[:, dc, :], xn[:, dc, fs + 512:fs + 544],
                               dc == 0, dc == 15, wants=[mw])
                m_ev1 = B.v(lambda v, dst=dst, ec=ec: v.tensor_copy(
                    dst[:, ec, 0:512], pB1[:, :]), wants=[mm1])
                m_ev2 = B.v(lambda v, dst=dst, ec=ec: v.tensor_copy(
                    dst[:, ec, 512:544], pB2[:, 0:32]), wants=[mm2])
        m_proj_qk = R.marks()

        m_conv_last = m_proj_qk
        for cs, dst in ((0, qf), (1, kf)):
            for ec in range(16):
                mc = B.v(lambda v, cs=cs, dst=dst, ec=ec: v.tensor_scalar(
                    cscr[:, 3:TFQ], dst[:, ec, 0:TFQ - 3], qkc[:, cs, ec, 0:1], None,
                    ALU.mult), wants=[m_params] + m_conv_last)
                for j in (1, 2, 3):
                    mc = B.v(lambda v, cs=cs, dst=dst, ec=ec, j=j: v.scalar_tensor_tensor(
                        cscr[:, 3:TFQ], dst[:, ec, j:TFQ - 3 + j], qkc[:, cs, ec, j:j + 1],
                        cscr[:, 3:TFQ], ALU.mult, ALU.add), wants=[mc])
                msl = B.s(lambda s, cs=cs, dst=dst, ec=ec: s.activation(
                    dst[:, ec, HX - 3:TFQ], cscr[:, HX - 3:TFQ], AF.Silu,
                    bias=qkb[:, cs, ec:ec + 1], scale=1.0), wants=[mc])
                m_conv_last = [msl]
        m_conv = R.marks()

        for cs, dst in ((0, qf), (1, kf)):
            wcol = lnw[:, 2 * cs:2 * cs + 1]
            bcol = lnw[:, 2 * cs + 1:2 * cs + 2]
            mcs0 = R.marks()
            mq_prev = mcs0
            msum = msq = None
            for h in range(H):
                msq_s = B.s(lambda s, dst=dst, h=h: s.activation(
                    scr2[:, 0:512], dst[:, h, HX:TFQ], AF.Square),
                    wants=mq_prev + ([msq] if msq else []))
                msum = B.mm(pstS[:, :], csel[:, h, :], dst[:, h, HX:TFQ],
                            h == 0, h == 15, wants=mcs0)
                msq = B.mm(pstQ[:, :], csel[:, h, :], scr2[:, 0:512],
                           h == 0, h == 15, wants=[msq_s])
                mq_prev = []
            mm_ = B.v(lambda v: v.tensor_scalar(rowst[:, 0, :], pstS[:, :], 1.0 / DK,
                      None, ALU.mult), wants=[msum])
            me2 = B.v(lambda v: v.tensor_scalar(rowst[:, 1, :], pstQ[:, :], 1.0 / DK,
                      None, ALU.mult), wants=[msq])
            mm2_ = B.v(lambda v: v.tensor_mul(rowst[:, 2, :], rowst[:, 0, :],
                       rowst[:, 0, :]), wants=[mm_])
            mvr = B.v(lambda v: v.tensor_sub(rowst[:, 1, :], rowst[:, 1, :],
                      rowst[:, 2, :]), wants=[me2, mm2_])
            msd = B.s(lambda s: s.activation(rowst[:, 1, :], rowst[:, 1, :], AF.Sqrt,
                      bias=epsc[0:16, 0:1], scale=1.0), wants=[mvr])
            mr = B.v(lambda v: v.reciprocal(rowst[:, 2, :], rowst[:, 1, :]), wants=[msd])
            mrm = B.v(lambda v: v.tensor_mul(rowst[:, 3, :], rowst[:, 2, :],
                      rowst[:, 0, :]), wants=[mr])
            mrb = B.v(lambda v: v.tensor_copy(rstb[:, 0, :], rowst[:, 2, :]), wants=[mr])
            mrmb = B.v(lambda v: v.tensor_copy(rstb[:, 1, :], rowst[:, 3, :]), wants=[mrm])
            my1 = my2 = None
            for h in range(H):
                mB1 = B.mm(pB1[:, :], rsel[:, h, :], rstb[:, 0, :], True, True,
                           wants=[mrb] + ([my1] if my1 else []))
                mB2 = B.mm(pB2[:, :], rsel[:, h, :], rstb[:, 1, :], True, True,
                           wants=[mrmb] + ([my2] if my2 else []))
                my1 = B.v(lambda v, dst=dst, h=h: v.tensor_mul(
                    ysc[:, :], dst[:, h, HX:TFQ], pB1[:, :]), wants=[mB1])
                my2 = B.v(lambda v: v.tensor_sub(ysc[:, :], ysc[:, :], pB2[:, :]),
                          wants=[mB2, my1])
                my3 = B.v(lambda v, dst=dst, h=h, wcol=wcol, bcol=bcol: v.tensor_scalar(
                    dst[:, h, HX:TFQ], ysc[:, :], wcol, bcol, ALU.mult, ALU.add),
                    wants=[my2])
        m_ln = R.marks()

        B.dma(qfm_out[:, t0:t0 + 512].rearrange("(a p) t -> p a t", p=128),
              qf[:, :, HX:TFQ], wants=m_ln)
        mtr_prev = None
        for ec in range(16):
            for tc in range(4):
                mtr = B.tr(pT[:, 0:128],
                           kf[:, ec, HX + tc * 128:HX + (tc + 1) * 128], ident[:, :],
                           wants=m_ln + ([mtr_prev] if mtr_prev else []))
                mcp = B.v(lambda v, ec=ec, tc=tc: v.tensor_copy(
                    ktm[:, tc, ec * 128:(ec + 1) * 128], pT[:, 0:128]), wants=[mtr])
                mtr_prev = mcp
        m_ktm = R.marks()

        for pi, (Wd, tdst, act) in enumerate(((Wv_d, vtm, None), (Wg_d, gtm_b, AF.Silu))):
            for ec in range(16):
                mw = B.g(lambda g, Wd=Wd, ec=ec: g.dma_start(
                    out=wA[:, :, :],
                    in_=Wd[:, ec * 128:(ec + 1) * 128].rearrange("(a p) e -> p a e", p=128)),
                    wants=R.marks() if ec == 0 else [m_ev1], dma=True)
                mm1 = None
                for dc in range(16):
                    mm1 = B.mm(pB1[:, :], wA[:, dc, :], xn[:, dc, fs + HX:fs + HX + 512],
                               dc == 0, dc == 15, wants=[mw])
                if act is None:
                    m_ev1 = B.v(lambda v, ec=ec: v.tensor_copy(
                        ogf[:, ec, :], pB1[:, :]), wants=[mm1])
                else:
                    m_ev1 = B.s(lambda s, ec=ec: s.activation(
                        ogf[:, ec, :], pB1[:, :], AF.Silu), wants=[mm1])
            m_vg = R.marks()
            mtr_prev = None
            for ec in range(16):
                for tc in range(4):
                    mtr = B.tr(pT[:, 0:128],
                               ogf[:, ec, tc * 128:(tc + 1) * 128], ident[:, :],
                               wants=m_vg + ([mtr_prev] if mtr_prev else []))
                    mcp = B.v(lambda v, tdst=tdst, ec=ec, tc=tc: v.tensor_copy(
                        tdst[:, tc, ec * 128:(ec + 1) * 128], pT[:, 0:128]),
                        wants=[mtr])
                    mtr_prev = mcp
        m_vgt = R.marks()
        B.dma(gtm_out[t0:t0 + 512, :].rearrange("(a p) d -> p a d", a=4),
              gtm_b[:, :, :], wants=m_vgt)

        Sdep = m_last_scan if len(m_last_scan) == H else [m_last_scan[0]] * H
        for cc in range(4):
            c = 4 * hf + cc
            mc0 = R.marks()
            mvst = B.v(lambda v, cc=cc, c=c: v.tensor_tensor(
                scr2[:, :].rearrange("p (h e) -> p h e", h=H),
                vtm[:, cc, :].rearrange("p (h e) -> p h e", h=H),
                bass.AP(vsc[:].tensor, vsc[:, c, :].offset, [vsc[:].ap[0], [1, H], [0, 128]]),
                ALU.mult), wants=mc0)
            mPh_prev = None
            mog_prev = None
            mtrp = [None, None]
            mSu_prev = None
            for h in range(H):
                fq = HX + cc * 128
                mP = B.mm(pP[:, 0:128], kf[:, h, fq:fq + 128], qf[:, h, fq:fq + 128],
                          True, True, wants=mc0 + ([mPh_prev] if mPh_prev else []))
                mPh = B.v(lambda v, c=c, h=h: v.scalar_tensor_tensor(
                    ph_t[:, :], pP[:, 0:128], bAc[:, c, h:h + 1], tri[:, :],
                    ALU.mult, ALU.mult), wants=[mP])
                mo1 = B.mm(pO[:, 0:128], qf[:, h, fq:fq + 128], Ssbb[:, h, :],
                           True, False, wants=[Sdep[h]] + ([mog_prev] if mog_prev else []))
                mo2 = B.mm(pO[:, 0:128], ph_t[:, :], vtm[:, cc, h * 128:(h + 1) * 128],
                           False, True, wants=[mPh])
                mog = B.v(lambda v, c=c, cc=cc, h=h: v.scalar_tensor_tensor(
                    ogst[:, :], pO[:, 0:128], Acol[:, c, h:h + 1],
                    gtm_b[:, cc, h * 128:(h + 1) * 128], ALU.mult, ALU.mult),
                    wants=[mo2])
                mtr = B.tr(pT[:, 0:128], ogst[:, :], ident[:, :],
                           wants=[mog] + ([mtrp[0]] if mtrp[0] else []))
                mcpog = B.v(lambda v, h=h, cc=cc: v.tensor_copy(
                    ogf[:, h, cc * 128:(cc + 1) * 128], pT[:, 0:128]), wants=[mtr])
                mtrp[0] = mcpog
                mdS = B.mm(pS[:, 0:128], ktm[:, cc, h * 128:(h + 1) * 128],
                           scr2[:, h * 128:(h + 1) * 128], True, True,
                           wants=[mvst] + ([mSu_prev] if mSu_prev else []))
                mSu = B.v(lambda v, c=c, h=h: v.scalar_tensor_tensor(
                    Ssb[:, h, :], Ssb[:, h, :], aend[:, c, h:h + 1], pS[:, 0:128],
                    ALU.mult, ALU.add), wants=[mdS, mo1])
                mSu = B.v(lambda v, h=h: v.tensor_copy(Ssbb[:, h, :], Ssb[:, h, :]),
                          wants=[mSu])
                Sdep[h] = mSu
                mSu_prev = mSu
                mog_prev = mog
                mPh_prev = mo2
        m_last_scan = Sdep

        m_scanned = R.marks()
        mwo_prev = None
        for dgq in range(8):
            mwo = B.g(lambda g, dgq=dgq: g.dma_start(
                out=woR[:, :, :],
                in_=Wo_d[:, dgq * 256:(dgq + 1) * 256].rearrange("(a p) e -> p a e", p=128)),
                wants=m_scanned + ([mwo_prev] if mwo_prev else []), dma=True)
            mev_prev = None
            mst_prev = [None, None]
            for tt2 in range(4):
                mo = None
                for vc in range(16):
                    mo = B.mm(pB1[:, 0:256], ogf[:, vc, tt2 * 128:(tt2 + 1) * 128],
                              woR[:, vc, :], vc == 0, vc == 15,
                              wants=[mwo] + ([mev_prev] if mev_prev else []))
                slot = tt2 % 2
                mev = B.v(lambda v, slot=slot: v.tensor_copy(
                    ost[:, slot, :], pB1[:, 0:256]),
                    wants=[mo] + ([mst_prev[slot]] if mst_prev[slot] else []))
                mst = B.dma(attn_dr[t0 + tt2 * 128:t0 + (tt2 + 1) * 128,
                                    dgq * 256:(dgq + 1) * 256],
                            ost[:, slot, :], wants=[mev])
                mst_prev[slot] = mst
                mev_prev = mev
            mwo_prev = mo
    B.dma(S_out, Ssb[:, :, :].rearrange("p h e -> p (h e)"), wants=R.marks())
    m_attn_done = R.marks()
    if int(os.environ.get("KPH", "99")) <= 3:
        es_attn.close(); R.emit(blk); es.close(); return nc

    es_attn.close()
    barrier()

    es_ffn = ExitStack()
    def fsb(name, shape, dt):
        return es_ffn.enter_context(nc.sbuf_tensor(name, shape, dt))
    def fps(name, shape, dt):
        return es_ffn.enter_context(nc.psum_tensor(name, shape, dt))

    h2f = fsb("h2f", [128, 8, 1024], BF16)
    h2t = fsb("h2t", [128, 8, 1024], BF16)
    qrf = fsb("qrf", [128, 8, 1024], BF16)
    keyb = fsb("keyb", [128, 16, 128], BF16)
    encb = fsb("encb", [128, 16, 128], F32)
    a64 = fsb("a64", [128, 512], F32)
    mx8 = fsb("mx8", [128, 16, 8], F32)
    r8 = fsb("r8", [128, 128], F32)
    kid = fsb("kid", [128, 16, 8], F32)
    sc8 = fsb("sc8", [128, 8, 8], F32)
    scR = fsb("scR", [128, 64], F32)
    dec2 = fsb("dec2", [128, 4, 64], F32)
    ixsel = fsb("ixsel", [128, 64], F32)
    iysel = fsb("iysel", [128, 64], F32)
    eidf = fsb("eidf", [128, 64], F32)
    eidi = fsb("eidi", [128, 64], I32)
    wsm = fsb("wsm", [128, 64], F32)
    smax = fsb("smax", [128, 8], F32)
    ssum = fsb("ssum", [128, 8], F32)
    zbuf = fsb("zbuf", [128, 128], F32)
    zt = fsb("zt", [128, 64], F32)
    deg = fsb("deg", [128, 2, 1024], BF16)
    ueg = fsb("ueg", [128, 2, 2048], BF16)
    acc4 = fsb("acc4", [128, 1, 2048], BF16)
    accF = fsb("accF", [128, 8, 2048], BF16)
    fout = fsb("fout", [128, 2048], F32)
    KM1 = fsb("KM1", [128, 128], F32)
    KM2 = fsb("KM2", [128, 512], F32)
    IOTA8 = fsb("IOTA8", [128, 512], BF16)

    pH1 = fps("pH1", [128, 512], F32)
    pH2 = fps("pH2", [128, 2048], F32)
    pQ = fps("pQ", [128, 512], F32)
    pSim = fps("pSim", [128, 512], F32)

    mi1 = B.g(lambda g: g.iota(KM1[:, :], pattern=[[1, 128]], base=0, channel_multiplier=0, allow_small_or_imprecise_dtypes=True))
    B.v(lambda v: v.tensor_scalar(KM1[:, :], KM1[:, :], -0.0078125, 63.5 * 0.0078125,
        ALU.mult, ALU.add), wants=[mi1])
    mi2 = B.g(lambda g: g.iota(KM2[:, :].rearrange("p (a m) -> p a m", a=8),
                               pattern=[[0, 8], [1, 64]], base=0, channel_multiplier=0, allow_small_or_imprecise_dtypes=True))
    B.v(lambda v: v.tensor_scalar(KM2[:, :], KM2[:, :], -0.015625, 31.5 * 0.015625,
        ALU.mult, ALU.add), wants=[mi2])
    B.g(lambda g: g.iota(IOTA8[:, :].rearrange("p (a j) -> p a j", a=64),
                         pattern=[[0, 64], [1, 8]], base=0, channel_multiplier=0, allow_small_or_imprecise_dtypes=True))
    mkey = B.g(lambda g: g.dma_start(out=keyb[0:64, :, :],
               in_=keysT_d.rearrange("a n k -> n a k")), dma=True)
    mkey2 = B.g(lambda g: g.dma_start(out=keyb[64:128, :, :],
                in_=keysT_d.rearrange("a n k -> n a k")), dma=True)
    mkey = mkey2
    m_ffn_const = R.marks()

    conv_marks = []
    def convert(src_d, dst_d, nrows, width):
        nonlocal_marks = []
        prev = [None, None]
        for i in range(nrows // 128):
            for w0 in range(0, width, 2048):
                wlen = min(2048, width - w0)
                slot = (i * ((width + 2047) // 2048) + w0 // 2048) % 2
                buf = ueg[:, slot, :].rearrange("p x -> p x")
                mg_ = B.g(lambda g, i=i, w0=w0, wlen=wlen, slot=slot, src_d=src_d:
                          g.dma_start(out=ueg[:, slot, 0:wlen],
                                      in_=src_d[i * 128:(i + 1) * 128, w0:w0 + wlen]),
                          wants=[prev[slot]] if prev[slot] else [], dma=True)
                ms_ = B.dma(dst_d[i * 128:(i + 1) * 128, w0:w0 + wlen],
                            ueg[:, slot, 0:wlen], wants=[mg_])
                prev[slot] = ms_
        return prev
    cm = convert(WupT_d, WupT_bf, D, DC)
    conv_marks += [m for m in cm if m]
    cm = convert(WdnT_d, WdnT_bf, DC, DP)
    conv_marks += [m for m in cm if m]
    m_conv_done = R.marks()

    bupc = small
    bdnc = cols[:, 15, :]
    mbu = B.dma(bupc[:, 0:64], bup_d.rearrange("o (a p) -> p (o a)", p=128),
                wants=m_ffn_const, slow=True)
    mbd = B.dma(bdnc[0:128, 0:8], bdn_d.rearrange("o (a p) -> p (o a)", p=128),
                wants=m_ffn_const, slow=True)
    m_bias = R.marks()
    h1g = fsb("h1g", [128, 8, 256], BF16)
    wB2 = fsb("wB2", [128, 1024], BF16)
    wA2 = fsb("wA2", [128, 16, 128], BF16)
    mh2ev_all = []
    mh2rd_bybuf = {0: None, 1: None}
    mm1_bybuf = {0: None, 1: None}
    for ts4 in range(4):
        f0 = HX + ts4 * 256
        mh1ev = None
        macc = [None] * 8
        mac_prev = None
        for cg in range(8):
            mh1evs = []
            mh2rd = None
            for ci in range(8):
                c = cg * 8 + ci
                wAbuf = wA if ci % 2 == 0 else wA2
                ab = ci % 2
                mwup = B.dma(wAbuf[:, :, :],
                             WupT_bf[:, c * 128:(c + 1) * 128].rearrange("(a p) e -> p a e", p=128),
                             wants=m_conv_done + ([mm1_bybuf[ab]] if mm1_bybuf[ab] else []))
                mm1 = None
                for dc in range(16):
                    mm1 = B.mm(pH1[:, 0:256], wAbuf[:, dc, :], xn[:, dc, f0:f0 + 256],
                               dc == 0, dc == 15,
                               wants=[mwup] + ([mh1ev] if mh1ev else []))
                mm1_bybuf[ab] = mm1
                mh1ev = B.s(lambda s, c=c, ci=ci: s.activation(
                    h1g[:, ci, :], pH1[:, 0:256], AF.Silu, bias=bupc[:, c:c + 1], scale=1.0),
                    wants=[mm1] + m_bias + R.marks()[0:0])
                mh1evs.append(mh1ev)
            for pc in range(8):
                wBuf = wB if pc % 2 == 0 else wB2
                pb = pc % 2
                mh2rd = None
                for ci in range(8):
                    c = cg * 8 + ci
                    mwd = B.dma(wBuf[:, ci * 128:(ci + 1) * 128],
                                WdnT_bf[c * 128:(c + 1) * 128, pc * 128:(pc + 1) * 128],
                                wants=m_conv_done + ([mh2rd_bybuf[pb]] if mh2rd_bybuf[pb] else []))
                    mh2rd = B.mm(pH2[:, 0:256], wBuf[:, ci * 128:(ci + 1) * 128],
                                 h1g[:, ci, :], ci == 0, ci == 7,
                                 wants=[mh1evs[ci], mwd] + ([mac_prev] if mac_prev and ci == 0 else []))
                mh2rd_bybuf[pb] = mh2rd
                if cg == 0:
                    mac = B.v(lambda v, pc=pc: v.tensor_copy(
                        fout[:, pc * 256:(pc + 1) * 256], pH2[:, 0:256]), wants=[mh2rd])
                else:
                    mac = B.v(lambda v, pc=pc: v.tensor_add(
                        fout[:, pc * 256:(pc + 1) * 256], fout[:, pc * 256:(pc + 1) * 256],
                        pH2[:, 0:256]), wants=[mh2rd, macc[pc]])
                macc[pc] = mac
                mac_prev = mac
        mtrp = [None, None]
        for pc in range(8):
            mev = B.v(lambda v, pc=pc, ts4=ts4: v.tensor_scalar(
                h2f[:, pc, ts4 * 256:(ts4 + 1) * 256], fout[:, pc * 256:(pc + 1) * 256],
                bdnc[0:128, pc:pc + 1], None, ALU.add), wants=[macc[pc]] + m_bias)
            for tp in range(2):
                tt8 = ts4 * 2 + tp
                mtr = B.tr(pT[:, 0:128],
                           h2f[:, pc, ts4 * 256 + tp * 128:ts4 * 256 + (tp + 1) * 128],
                           ident[:, :],
                           wants=[mev] + ([mtrp[0]] if mtrp[0] else []))
                mcp = B.v(lambda v, pc=pc, tt8=tt8: v.tensor_copy(
                    h2t[:, tt8, pc * 128:(pc + 1) * 128], pT[:, 0:128]), wants=[mtr])
                mtrp[0] = mcp
            mh2ev_all.append(mcp)
    m_front = R.marks()
    if int(os.environ.get("KPH", "99")) <= 4:
        for dbg in range(8):
            B.dma(out_A[dbg * 128:(dbg + 1) * 128, 0:1024], h2f[:, dbg, :],
                  wants=m_front, eng="gpsimd")
            B.dma(out_A[dbg * 128:(dbg + 1) * 128, 1024:2048], h2t[:, dbg, :],
                  wants=m_front, eng="gpsimd")
        es_ffn.close(); R.emit(blk); es.close(); return nc

    mqr_prev = None
    for rc in range(8):
        mwq = B.g(lambda g, rc=rc: g.dma_start(
            out=wA[:, 0:8, :],
            in_=WqrT_d[:, rc * 128:(rc + 1) * 128].rearrange("(a p) e -> p a e", p=128)),
            wants=m_front + ([mqr_prev] if mqr_prev else []), dma=True)
        for th in range(2):
            mmq = None
            for pc in range(8):
                mmq = B.mm(pQ[:, :], wA[:, pc, :], h2f[:, pc, th * 512:(th + 1) * 512],
                           pc == 0, pc == 7, wants=[mwq] + ([mqr_prev] if mqr_prev else []))
            mqr_prev = B.v(lambda v, rc=rc, th=th: v.tensor_copy(
                qrf[:, rc, th * 512:(th + 1) * 512], pQ[:, :]), wants=[mmq])
    m_qr = R.marks()

    IND = bass.IndirectOffsetOnAxis
    m_tail_prev = m_qr
    for tt8 in range(8):
        me2_all = []
        mPs_prev = None
        for xh in range(16):
            x_, h_ = xh // 8, xh % 8
            rc = x_ * 4 + h_ // 2
            off = (h_ % 2) * 64
            mPs = B.mm(pSim[:, 0:128], qrf[off:off + 64, rc, tt8 * 128:(tt8 + 1) * 128],
                       keyb[off:off + 64, xh, :], True, True,
                       wants=m_tail_prev + [mkey] + ([mPs_prev] if mPs_prev else []))
            me1 = B.v(lambda v, xh=xh: v.tensor_scalar(
                encb[:, xh, :], pSim[:, 0:128], 1024.0, MAGIC, ALU.mult, ALU.add),
                wants=[mPs])
            me2 = B.v(lambda v, xh=xh: v.scalar_tensor_tensor(
                encb[:, xh, :], encb[:, xh, :], -MAGIC, KM1[:, :], ALU.add, ALU.add),
                wants=[me1])
            mPs_prev = me1
            me2_all.append(me2)
        mmx = None
        for xh in range(16):
            mmx = B.v(lambda v, xh=xh: v.max(mx8[:, xh, :], encb[:, xh, :]),
                      wants=[me2_all[xh]])
        mR1 = B.v(lambda v: v.tensor_scalar(
            r8[:, :], mx8[:, :, :].rearrange("p a b -> p (a b)"), MAGIC, MAGIC,
            ALU.add, ALU.subtract), wants=[mmx])
        md8 = B.v(lambda v: v.tensor_sub(
            kid[:, :, :].rearrange("p a b -> p (a b)"),
            mx8[:, :, :].rearrange("p a b -> p (a b)"), r8[:, :]), wants=[mR1])
        mk8 = B.v(lambda v: v.tensor_scalar(
            kid[:, :, :].rearrange("p a b -> p (a b)"),
            kid[:, :, :].rearrange("p a b -> p (a b)"), -128.0, 63.5,
            ALU.mult, ALU.add), wants=[md8])
        ma64 = B.v(lambda v: v.tensor_tensor(
            a64[:, :],
            bass.AP(r8[:].tensor, r8[:].offset, [r8[:].ap[0], [8, 8], [1, 8], [0, 8]]),
            bass.AP(r8[:].tensor, r8[:, 64:65].offset, [r8[:].ap[0], [8, 8], [0, 8], [1, 8]]),
            ALU.add), wants=[mR1])
        me2b = B.v(lambda v: v.scalar_tensor_tensor(
            a64[:, :], a64[:, :], 0.0, KM2[:, :], ALU.add, ALU.add), wants=[ma64])
        msc = None
        for h_ in range(8):
            msc = B.v(lambda v, h_=h_: v.max(sc8[:, h_, :], a64[:, h_ * 64:(h_ + 1) * 64]),
                      wants=[me2b] + ([msc] if msc else []))
        mR2 = B.v(lambda v: v.tensor_scalar(
            scR[:, :], sc8[:, :, :].rearrange("p a b -> p (a b)"), MAGIC, MAGIC,
            ALU.add, ALU.subtract), wants=[msc])
        md2 = B.v(lambda v: v.tensor_sub(
            dec2[:, 0, :], sc8[:, :, :].rearrange("p a b -> p (a b)"), scR[:, :]),
            wants=[mR2])
        mm2_ = B.v(lambda v: v.tensor_scalar(
            dec2[:, 1, :], dec2[:, 0, :], -64.0, 31.5, ALU.mult, ALU.add), wants=[md2])
        mi2a = B.v(lambda v: v.tensor_scalar(
            dec2[:, 2, :], dec2[:, 1, :], 0.125, -0.4375, ALU.mult, ALU.add),
            wants=[mm2_])
        mi2 = B.v(lambda v: v.tensor_scalar(
            dec2[:, 2, :], dec2[:, 2, :], MAGIC, MAGIC, ALU.add, ALU.subtract),
            wants=[mi2a])
        mj2 = B.v(lambda v: v.scalar_tensor_tensor(
            dec2[:, 3, :], dec2[:, 2, :], -8.0, dec2[:, 1, :], ALU.mult, ALU.add),
            wants=[mi2])
        sel_marks = []
        for (sel_out, idx_m, base_off) in ((ixsel, 2, 0), (iysel, 3, 64)):
            mc_ = B.v(lambda v, idx_m=idx_m: v.tensor_tensor(
                scr2[:, 1536:2048],
                bass.AP(dec2[:].tensor, dec2[:, idx_m, :].offset,
                        [dec2[:].ap[0], [1, 64], [0, 8]]),
                IOTA8[:, :], ALU.is_equal), wants=[mi2, mj2])
            mu_ = B.v(lambda v, base_off=base_off: v.tensor_tensor(
                scr2[:, 1024:1536], scr2[:, 1536:2048],
                bass.AP(kid[:].tensor, kid[:].offset + base_off,
                        [kid[:].ap[0], [8, 8], [0, 8], [1, 8]]),
                ALU.mult), wants=[mc_, mk8])
            ms_ = B.v(lambda v, sel_out=sel_out: v.tensor_reduce(
                sel_out[:, :], scr2[:, 1024:1536].rearrange("p (a b) -> p a b", a=64),
                mybir.AxisListType.X, ALU.add), wants=[mu_])
            sel_marks.append(ms_)
        mef = B.v(lambda v: v.scalar_tensor_tensor(
            eidf[:, :], ixsel[:, :], 128.0, iysel[:, :], ALU.mult, ALU.add),
            wants=sel_marks)
        mei = B.s(lambda s: s.copy(eidi[:, :], eidf[:, :]), wants=[mef])
        mmh = B.v(lambda v: v.tensor_reduce(
            smax[:, :], scR[:, :].rearrange("p (a b) -> p a b", a=8),
            mybir.AxisListType.X, ALU.max), wants=[mR2])
        msb = B.v(lambda v: v.tensor_tensor(
            wsm[:, :], scR[:, :],
            bass.AP(smax[:].tensor, smax[:].offset, [smax[:].ap[0], [1, 8], [0, 8]]),
            ALU.subtract), wants=[mmh])
        msc_ = B.v(lambda v: v.tensor_scalar(
            wsm[:, :], wsm[:, :], 0.0009765625, None, ALU.mult), wants=[msb])
        mex = B.s(lambda s: s.activation(wsm[:, :], wsm[:, :], AF.Exp),
                  wants=[msc_])
        mse = B.v(lambda v: v.tensor_reduce(
            ssum[:, :], wsm[:, :].rearrange("p (a b) -> p a b", a=8),
            mybir.AxisListType.X, ALU.add), wants=[mex])
        mrs_l = []
        for ri in range(8):
            mrs_l.append(B.v(lambda v, ri=ri: v.reciprocal(
                smax[:, ri:ri + 1], ssum[:, ri:ri + 1]), wants=[mse]))
        mw_ = B.v(lambda v: v.tensor_tensor(
            wsm[:, :], wsm[:, :],
            bass.AP(smax[:].tensor, smax[:].offset, [smax[:].ap[0], [1, 8], [0, 8]]),
            ALU.mult), wants=mrs_l)
        mz_prev = [None, None]
        mzl = None
        for m in range(64):
            slot = m % 2
            mg_ = B.g(lambda g, m=m, slot=slot: g.indirect_dma_start(
                out=deg[:, slot, :], out_offset=None, in_=de_d,
                in_offset=IND(ap=eidi[:, m:m + 1], axis=0)),
                wants=[mei] + ([mz_prev[slot]] if mz_prev[slot] else []),
                dma=True)
            mzl = B.v(lambda v, m=m, slot=slot, tt8=tt8: v.scalar_tensor_tensor(
                fout[:, 0:1024], deg[:, slot, :], 1.0, h2t[:, tt8, :], ALU.mult,
                ALU.mult, accum_out=zbuf[:, 2 * m:2 * m + 1]), wants=[mg_])
            mz_prev[slot] = mzl
        mzw = B.v(lambda v: v.tensor_tensor(
            zt[:, :],
            bass.AP(zbuf[:].tensor, zbuf[:].offset, [zbuf[:].ap[0], [2, 64]]),
            wsm[:, :], ALU.mult), wants=[mzl, mw_])
        mzs = B.s(lambda s: s.activation(zt[:, :], zt[:, :], AF.Silu), wants=[mzw])
        mu_prev = [None, None]
        macc = [None, None]
        for m in range(64):
            slot = m % 2
            a_ = 0
            mgu = B.g(lambda g, m=m, slot=slot: g.indirect_dma_start(
                out=ueg[:, slot, :], out_offset=None, in_=ue_d,
                in_offset=IND(ap=eidi[:, m:m + 1], axis=0)),
                wants=[mei] + ([mu_prev[slot]] if mu_prev[slot] else []), dma=True)
            if m < 1:
                mac = B.v(lambda v, m=m, slot=slot, a_=a_: v.tensor_scalar(
                    acc4[:, a_, :], ueg[:, slot, :], zt[:, m:m + 1], None, ALU.mult),
                    wants=[mgu, mzs])
            else:
                mac = B.v(lambda v, m=m, slot=slot, a_=a_: v.scalar_tensor_tensor(
                    acc4[:, a_, :], ueg[:, slot, :], zt[:, m:m + 1], acc4[:, a_, :],
                    ALU.mult, ALU.add), wants=[mgu, macc[a_]])
            macc[a_] = mac
            mu_prev[slot] = mac
        mtf = B.v(lambda v, tt8=tt8: v.tensor_copy(accF[:, tt8, :],
                  acc4[:, 0, :]), wants=[macc[0]])
        m_tail_prev = [mtf]
    m_tail = R.marks()
    if int(os.environ.get("KPH", "99")) <= 6:
        B.dma(out_A[0:128, 0:64], eidf[:, :], wants=m_tail, eng="gpsimd")
        B.dma(out_A[0:128, 64:192],
              bass.AP(zbuf[:].tensor, zbuf[:].offset, [zbuf[:].ap[0], [1, 128]]),
              wants=m_tail, eng="gpsimd")
        B.dma(out_A[0:128, 192:256], wsm[:, :], wants=m_tail, eng="gpsimd")
        B.dma(out_A[0:128, 256:320], scR[:, :], wants=m_tail, eng="gpsimd")
        B.dma(out_A[0:128, 320:384], zt[:, :], wants=m_tail, eng="gpsimd")
        B.dma(out_A[128:256, 0:2048], accF[:, 7, :], wants=m_tail, eng="gpsimd")
        B.dma(out_A[256:384, 0:128], r8[:, :], wants=m_tail, eng="gpsimd")
        B.dma(out_A[256:384, 128:256],
              kid[:, :, :].rearrange("p a b -> p (a b)"), wants=m_tail, eng="gpsimd")
        B.dma(out_A[256:384, 256:384],
              mx8[:, :, :].rearrange("p a b -> p (a b)"), wants=m_tail, eng="gpsimd")
        B.dma(out_A[256:384, 384:896], a64[:, :], wants=m_tail, eng="gpsimd")
        B.dma(out_A[384:512, 0:2048],
              encb[:, :, :].rearrange("p a b -> p (a b)"), wants=m_tail, eng="gpsimd")
        es_ffn.close(); R.emit(blk); es.close(); return nc

    for tt8 in range(8):
        mx_ = B.dma(sq[:, :], x_in[HX + tt8 * 128:HX + (tt8 + 1) * 128, :],
                    wants=m_tail if tt8 == 0 else [mo2_])
        mat = B.dma(scr2[:, :], attn_dr[tt8 * 128:(tt8 + 1) * 128, :],
                    wants=m_tail if tt8 == 0 else [mo2_])
        mf1 = B.v(lambda v, tt8=tt8: v.tensor_add(fout[:, :], scr2[:, :],
                  accF[:, tt8, :]), wants=[mat])
        mf2 = B.v(lambda v: v.tensor_add(fout[:, :], fout[:, :], sq[:, :]),
                  wants=[mf1, mx_])
        mo2_ = B.dma(out_A[tt8 * 128:(tt8 + 1) * 128, :], fout[:, :], wants=[mf2])

    es_ffn.close()
    R.emit(blk)
    es.close()
    return nc


def build_B_marker():
    pass


def build_B():
    Rec.PARANOID = bool(int(os.environ.get('KPAR', '0')))
    """Launch B: S_start = Horner-combine of predecessors' states (host passes
    them per-core in descending order, zero/one padded), then additive
    correction out = out_A + (gate * A_t * (S q_t)) @ Wo."""
    nc = bass.Bass("TRN2", target_bir_lowering=False, detect_race_conditions=False)
    es = ExitStack()
    B = Builder(nc, es)
    R = B.R

    S_in = nc.dram_tensor("S_in", [7, DK, H * DK], F32, kind="ExternalInput").ap()
    D_in = nc.dram_tensor("D_in", [1, 7 * H], F32, kind="ExternalInput").ap()
    beta_in = nc.dram_tensor("beta_in", [T, H], F32, kind="ExternalInput").ap()
    qfm_in = nc.dram_tensor("qfm_in", [D, T], BF16, kind="ExternalInput").ap()
    gtm_in = nc.dram_tensor("gtm_in", [T, D], BF16, kind="ExternalInput").ap()
    outA_in = nc.dram_tensor("outA_in", [T, D], F32, kind="ExternalInput").ap()
    Wo_d = nc.dram_tensor("Wo", [D, D], F32, kind="ExternalInput").ap()
    out_B = nc.dram_tensor("out_B", [T, D], F32, kind="ExternalOutput").ap()

    qft = B.sb("qft", [128, 16, T], BF16)
    gt = B.sb("gt", [128, NCH, D], BF16)
    ogfa = B.sb("ogfa", [128, 16, NCH * 128], BF16)
    fcor = B.sb("fcor", [128, NCH, D], BF16)
    Sacc = B.sb("Sacc", [128, H, DK], F32)
    SB2 = B.sb("SB2", [128, H, DK], F32)
    Saccb = B.sb("Saccb", [128, H, DK], BF16)
    Dbc = B.sb("Dbc", [128, 7, H], F32)
    beta_b = B.sb("beta_b", [128, NCH, H], F32)
    Ltm = B.sb("Ltm", [128, NCH, H], F32)
    Acol = B.sb("Acol", [128, NCH, H], F32)
    aend = B.sb("aend", [128, NCH, H], F32)
    triF = B.sb("triF", [128, 128], F32)
    sel127 = B.sb("sel127", [128, 128], F32)
    ident = B.sb("ident", [128, 128], BF16)
    onesF = B.sb("onesF", [1, 128], F32)
    small = B.sb("small", [128, 128], F32)
    woR = B.sb("woR", [128, 16, 256], BF16)
    ogst = B.sb("ogst", [128, 128], BF16)
    fout = B.sb("fout", [128, D], F32)

    pO = B.ps("pO", [128, 128], F32)
    pS = B.ps("pS", [128, 128], F32)
    pB = B.ps("pB", [128, 256], F32)
    pT = B.ps("pT", [128, 1024], BF16)

    blk = es.enter_context(nc.Block())

    m0 = B.g(lambda g: g.memset(triF[:, :], 1.0))
    B.g(lambda g: g.affine_select(out=triF[:, :], in_=triF[:, :], compare_op=ALU.is_ge,
        fill=0.0, base=0, pattern=[[1, 128]], channel_multiplier=-1), wants=[m0])
    m0 = B.g(lambda g: g.memset(sel127[:, :], 1.0))
    B.g(lambda g: g.affine_select(out=sel127[:, :], in_=sel127[:, :],
        compare_op=ALU.is_equal, fill=0.0, base=-127, pattern=[[0, 128]],
        channel_multiplier=1), wants=[m0])
    m0 = B.g(lambda g: g.memset(ident[:, :], 0.0))
    B.g(lambda g: g.affine_select(out=ident[:, :], in_=ident[:, :],
        compare_op=ALU.not_equal, fill=1.0, base=0, pattern=[[-1, 128]],
        channel_multiplier=1), wants=[m0])
    B.v(lambda v: v.memset(onesF[:, :], 1.0))
    m_consts = R.marks()

    mq = B.dma(qft[:, :, :], qfm_in.rearrange("(a p) t -> p a t", p=128))
    mg_ = B.dma(gt[:, :, :], gtm_in.rearrange("(a p) d -> p a d", a=NCH))
    mb = B.dma(beta_b[:, :, :], beta_in.rearrange("(a p) h -> p a h", a=NCH))
    mD = B.dma(small[0:1, 0:7 * H], D_in, wants=m_consts)
    mDb = B.mm(pB[:, 0:112], onesF[0:1, :], small[0:1, 0:112], True, True,
               wants=[mD] + m_consts)
    mDbc = B.v(lambda v: v.tensor_copy(Dbc[:, :, :].rearrange("p a h -> p (a h)"),
               pB[:, 0:112]), wants=[mDb])

    mprev = m_consts
    for tc in range(NCH):
        md = B.v(lambda v, tc=tc: v.tensor_scalar(small[:, 0:16], beta_b[:, tc, :],
                 -1.0, 1.0, ALU.mult, ALU.add), wants=[mb] + mprev)
        mln = B.s(lambda s: s.activation(small[:, 16:32], small[:, 0:16], AF.Ln),
                  wants=[md])
        mcs = B.mm(pO[:, 0:16], triF[:, :], small[:, 16:32], True, True, wants=[mln])
        mL = B.v(lambda v, tc=tc: v.tensor_copy(Ltm[:, tc, :], pO[:, 0:16]), wants=[mcs])
        mA = B.s(lambda s, tc=tc: s.activation(Acol[:, tc, :], Ltm[:, tc, :], AF.Exp),
                 wants=[mL])
        mLe = B.mm(pS[:, 0:16], sel127[:, :], Ltm[:, tc, :], True, True, wants=[mL])
        mae = B.s(lambda s, tc=tc: s.activation(aend[:, tc, :], pS[:, 0:16], AF.Exp),
                  wants=[mLe])
        mprev = [mA, mae, mcs, mLe]

    mS = B.dma(Sacc[:, :, :].rearrange("p h e -> p (h e)"), S_in[6])
    mldS = None
    for j in range(5, -1, -1):
        mldS = B.dma(SB2[:, :, :].rearrange("p h e -> p (h e)"), S_in[j],
                     wants=[mldS] if mldS else [])
        for h in range(H):
            mS = B.v(lambda v, j=j, h=h: v.scalar_tensor_tensor(
                Sacc[:, h, :], Sacc[:, h, :], Dbc[:, j, h:h + 1], SB2[:, h, :],
                ALU.mult, ALU.add), wants=[mldS, mS, mDbc])
    mSb0 = B.v(lambda v: v.tensor_copy(
        Saccb[:, :, :].rearrange("p h e -> p (h e)"),
        Sacc[:, :, :].rearrange("p h e -> p (h e)")), wants=[mS])
    m_state = R.marks()

    mog_prev = None
    mSu_prev = [None] * H
    mtrp = [None, None]
    for tc in range(NCH):
        for h in range(H):
            mo = B.mm(pO[:, 0:128], qft[:, h, tc * 128:(tc + 1) * 128], Saccb[:, h, :],
                      True, True,
                      wants=m_state + [mq] + ([mog_prev] if mog_prev else [])
                      + ([mSu_prev[h]] if mSu_prev[h] else []))
            mog = B.v(lambda v, tc=tc, h=h: v.scalar_tensor_tensor(
                ogst[:, :], pO[:, 0:128], Acol[:, tc, h:h + 1], gt[:, tc, h * 128:(h + 1) * 128],
                ALU.mult, ALU.mult), wants=[mo, mg_])
            mtr = B.tr(pT[:, 0:128], ogst[:, :], ident[:, :],
                       wants=[mog] + ([mtrp[0]] if mtrp[0] else []))
            mcp = B.v(lambda v, tc=tc, h=h: v.tensor_copy(
                ogfa[:, h, tc * 128:(tc + 1) * 128], pT[:, 0:128]), wants=[mtr])
            mtrp[0] = mcp
            mog_prev = mog
            mSu = B.v(lambda v, tc=tc, h=h: v.tensor_scalar(
                Sacc[:, h, :], Sacc[:, h, :], aend[:, tc, h:h + 1], None, ALU.mult),
                wants=[mo])
            mSu = B.v(lambda v, h=h: v.tensor_copy(Saccb[:, h, :], Sacc[:, h, :]),
                      wants=[mSu])
            mSu_prev[h] = mSu
    m_corr = R.marks()

    mwo_prev = None
    for dgq in range(8):
        mwo = B.g(lambda g, dgq=dgq: g.dma_start(
            out=woR[:, :, :],
            in_=Wo_d[:, dgq * 256:(dgq + 1) * 256].rearrange("(a p) e -> p a e", p=128)),
            wants=m_corr + ([mwo_prev] if mwo_prev else []), dma=True)
        mev_prev = None
        for tc in range(NCH):
            mo = None
            for vc in range(16):
                mo = B.mm(pB[:, 0:256], ogfa[:, vc, tc * 128:(tc + 1) * 128],
                          woR[:, vc, :], vc == 0, vc == 15,
                          wants=[mwo] + ([mev_prev] if mev_prev else []))
            mev_prev = B.v(lambda v, tc=tc, dgq=dgq: v.tensor_copy(
                fcor[:, tc, dgq * 256:(dgq + 1) * 256], pB[:, 0:256]), wants=[mo])
        mwo_prev = mo
    m_wo = R.marks()

    mo2_ = None
    for tc in range(NCH):
        mx_ = B.dma(fout[:, :], outA_in[tc * 128:(tc + 1) * 128, :],
                    wants=m_wo if tc == 0 else [mo2_])
        mf1 = B.v(lambda v, tc=tc: v.tensor_add(fout[:, :], fout[:, :], fcor[:, tc, :]),
                  wants=[mx_] + m_wo)
        mo2_ = B.dma(out_B[tc * 128:(tc + 1) * 128, :], fout[:, :], wants=[mf1])

    R.emit(blk)
    es.close()
    return nc


_NC_CACHE = {}


def _get_progs():
    if "A" not in _NC_CACHE:
        _NC_CACHE["A"] = build_A()
        _NC_CACHE["B"] = build_B()
    return _NC_CACHE["A"], _NC_CACHE["B"]


def kernel(**inp):
    f32 = np.float32
    x = np.asarray(inp["L0_attn_hidden_states"], f32).reshape(8192, D)
    Wq = np.ascontiguousarray(np.asarray(inp["L0_attn_Wq"], f32))
    Wk = np.ascontiguousarray(np.asarray(inp["L0_attn_Wk"], f32))
    Wv = np.ascontiguousarray(np.asarray(inp["L0_attn_Wv"], f32))
    Wb = np.ascontiguousarray(np.asarray(inp["L0_attn_Wb"], f32))
    bb = np.asarray(inp["L0_attn_bb"], f32).reshape(1, H)
    Wg = np.ascontiguousarray(np.asarray(inp["L0_attn_Wg"], f32))
    Wo = np.ascontiguousarray(np.asarray(inp["L0_attn_Wo"], f32))
    qcw = np.ascontiguousarray(np.asarray(inp["L0_attn_qconv_w"], f32))
    qcb = np.asarray(inp["L0_attn_qconv_b"], f32).reshape(1, D)
    kcw = np.ascontiguousarray(np.asarray(inp["L0_attn_kconv_w"], f32))
    kcb = np.asarray(inp["L0_attn_kconv_b"], f32).reshape(1, D)
    qnw = np.asarray(inp["L0_attn_qn_w"], f32).reshape(1, DK)
    qnb = np.asarray(inp["L0_attn_qn_b"], f32).reshape(1, DK)
    knw = np.asarray(inp["L0_attn_kn_w"], f32).reshape(1, DK)
    knb = np.asarray(inp["L0_attn_kn_b"], f32).reshape(1, DK)
    WupT = np.ascontiguousarray(np.asarray(inp["L0_ffn_W_up"], f32).T)
    bup = np.asarray(inp["L0_ffn_b_up"], f32).reshape(1, DC)
    WdnT = np.ascontiguousarray(np.asarray(inp["L0_ffn_W_down"], f32).T)
    bdn = np.asarray(inp["L0_ffn_b_down"], f32).reshape(1, DP)
    WqrT = np.ascontiguousarray(np.asarray(inp["L0_ffn_W_q"], f32).T)
    keys = np.asarray(inp["L0_ffn_keys"], f32)
    keysT = np.ascontiguousarray(np.transpose(keys, (2, 0, 3, 1)).reshape(16, 64, 128))
    de = np.ascontiguousarray(np.asarray(inp["L0_ffn_down_embed"], f32))
    ue = np.ascontiguousarray(np.asarray(inp["L0_ffn_up_embed"], f32))

    xp = np.zeros((HX + 8192, D), f32)
    xp[HX:] = x

    ncA, ncB = _get_progs()
    shared = dict(Wq=Wq, Wk=Wk, Wv=Wv, Wb=Wb, bb=bb, Wg=Wg, Wo=Wo,
                  qconv_w=qcw, qconv_b=qcb, kconv_w=kcw, kconv_b=kcb,
                  qn_w=qnw, qn_b=qnb, kn_w=knw, kn_b=knb,
                  WupT=WupT, b_up=bup, WdnT=WdnT, b_dn=bdn, WqrT=WqrT,
                  keysT=keysT, de_f32=de, ue_f32=ue)
    in_maps = []
    for i in range(NCORE):
        m = dict(shared)
        m["x_slice"] = np.ascontiguousarray(xp[T * i:T * i + HX + T])
        in_maps.append(m)
    resA = run_bass_kernel_spmd(ncA, in_maps, core_ids=list(range(NCORE)))
    rA = resA.results

    in_maps_b = []
    for i in range(NCORE):
        Ss = np.zeros((7, DK, H * DK), f32)
        Dd = np.ones((7, H), f32)
        for jj, j in enumerate(range(i - 1, -1, -1)):
            Ss[jj] = rA[j]["S_out"]
            Dd[jj] = rA[j]["D_out"].reshape(H)
        in_maps_b.append(dict(
            S_in=Ss, D_in=np.ascontiguousarray(Dd.reshape(1, 7 * H)),
            beta_in=rA[i]["beta_out"], qfm_in=rA[i]["qfm_out"],
            gtm_in=rA[i]["gtm_out"], outA_in=rA[i]["out_A"], Wo=Wo))
    resB = run_bass_kernel_spmd(ncB, in_maps_b, core_ids=list(range(NCORE)))
    out = np.concatenate([np.asarray(resB.results[i]["out_B"], f32)
                          for i in range(NCORE)], axis=0)
    return out.reshape(1, 8192, D).astype(np.float32)
